# version 01
# baseline (speedup 1.0000x reference)
import math
import numpy as np
import ml_dtypes
import concourse.bass as bass
import concourse.mybir as mybir
from concourse.bass_utils import run_bass_kernel_spmd
from concourse.bass_core_rust import IndirectOffsetOnAxis

F32 = mybir.dt.float32
BF16 = mybir.dt.bfloat16
U32 = mybir.dt.uint32
AF = mybir.ActivationFunctionType
ALU = mybir.AluOpType
AX = mybir.AxisListType
BF = ml_dtypes.bfloat16

T = 4096
DM = 1024
NT = T // 128
NC = DM // 128
NH = 16
HD = 64
WIN = 512
E0 = 8
E1 = 8
HFF = 5464
CAP = 1024
NHB = (HFF + 127) // 128
LASTHB = HFF - 128 * (NHB - 1)



def make_consts():
    c = {}
    c["c_id128b"] = np.eye(128, dtype=BF)
    c["c_id128f"] = np.eye(128, dtype=np.float32)
    c["c_ones_col"] = np.ones((128, 1), dtype=BF)
    c["c_ones_row"] = np.ones((1, 128), dtype=BF)
    c["c_ones_rowf"] = np.ones((1, 128), dtype=np.float32)
    iu = np.arange(128)
    c["c_u128"] = (iu[:, None] < iu[None, :]).astype(BF)
    cc, pp = np.meshgrid(iu, iu, indexing="ij")
    c["c_u8s"] = (((cc % 8) == (pp % 8)) & (cc < pp)).astype(np.float32)
    c["c_a8s"] = ((cc % 8) == (pp % 8)).astype(np.float32)
    c["c_iota8"] = np.tile(np.arange(8, dtype=np.float32), (128, 1))
    c["c_iota8t"] = np.tile(np.arange(8, dtype=np.float32), (128, 32))
    c["c_i8revt"] = np.tile(7.0 - np.arange(8, dtype=np.float32), (128, 32))
    c["c_u128b"] = (iu[:, None] < iu[None, :]).astype(BF)
    c["c_u8sb"] = (((cc % 8) == (pp % 8)) & (cc < pp)).astype(BF)
    c["c_a8sb"] = ((cc % 8) == (pp % 8)).astype(BF)
    p = np.arange(128)[:, None]
    f = np.arange(256)[None, :]
    m0 = (f < p)
    m1 = (f < p + 128)
    m4 = (p <= f)
    m5 = (p + 128 <= f)
    c["c_amask"] = np.concatenate(
        [m.astype(BF) for m in (m0, m1, m4, m5)], axis=1)
    half = HD // 2
    theta = (1.0 / 10000.0 ** (np.arange(half, dtype=np.float64) / half))
    pos = np.arange(T, dtype=np.float64)
    fr = pos[:, None] * theta[None, :]
    fr2 = np.concatenate([fr, fr], axis=1)
    c["c_rcos"] = np.cos(fr2).astype(BF)
    c["c_rsin"] = np.sin(fr2).astype(BF)
    iu2 = np.arange(128)
    c["c_bc2f"] = np.stack([(iu2 < 64), (iu2 >= 64)]).astype(np.float32)
    c["c_fvals"] = np.tile(np.array([[1e-5, 1e-12, 0.0, 0.0]], dtype=np.float32), (128, 1))
    return c


CONSTS = make_consts()



class Ctx:

    ENGS = ("tensor", "vector", "scalar", "gpsimd", "sync")

    def __init__(self, nc):
        self.nc = nc
        self.q = {e: [] for e in self.ENGS}
        self.sems = {}
        self.cnt = {}
        self._sem_cms = []

    def make_sems(self, stack):
        for name in ("te", "ve", "se", "ge", "dsy", "dge", "dse", "dve"):
            self.sems[name] = stack.enter_context(self.nc.semaphore(name))
            self.cnt[name] = 0

    def add(self, eng, body, waits=(), inc=None, n=1, fence=False):
        waits = tuple(waits)
        tok = None
        if inc is not None:
            self.cnt[inc] += n
            tok = (inc, self.cnt[inc])
        sems = self.sems
        def closure(E, waits=waits, inc=inc, n=n, body=body, fence=fence):
            for (chn, v) in waits:
                E.wait_ge(sems[chn], v)
            last = body(E)
            if inc is not None:
                if fence:
                    E.drain()
                    last = E.nop()
                if last is None:
                    last = E.nop()
                last.then_inc(sems[inc], n)
        self.q[eng].append(closure)
        return tok

    def dma(self, eng, out, in_, waits=(), **kw):
        ch = {"sync": "dsy", "gpsimd": "dge", "scalar": "dse", "vector": "dve"}[eng]
        return self.add(eng, lambda E: E.dma_start(out=out, in_=in_, **kw),
                        waits=waits, inc=ch, n=16)

    def barrier_tok(self):
        return [(chn, v) for chn, v in self.cnt.items() if v > 0]


def ts(i, n):
    return slice(i * n, (i + 1) * n)



DEBUG_A = False

def emit_passA(ctx, d, sb, ps):
    nc = ctx.nc
    X, WQKV, WOUT = d["x"], d["l0_wqkv"], d["l0_wout"]
    X1D, X1TD = d["x1d"], d["x1td"]

    wq = sb["A_wq"]
    wo = sb["A_wo"]
    idb = sb["c_idb"]
    onescol = sb["c_onescol"]
    onesrow = sb["c_onesrow"]
    onesrowf = sb["c_onesrowf"]
    amask = sb["A_amask"]
    rcos = sb["A_rcos"]
    rsin = sb["A_rsin"]
    xf = sb["A_xf"]
    xbf = sb["A_xbf"]
    xT = sb["A_xT"]
    qkb = sb["A_qkb"]
    ropet = sb["A_ropet"]
    qT = sb["A_qT"]
    kT = sb["A_kT"]
    vr = sb["A_vr"]
    pt = sb["A_pt"]
    densb = sb["A_densb"]
    recb = sb["A_recb"]
    bcsb = sb["A_bcsb"]
    attnT = sb["A_attnT"]
    rln = sb["A_rln"]
    bnst = sb["A_bnst"]
    mv = sb["A_mv"]
    lnt = sb["A_lnt"]
    x1b = sb["A_x1b"]
    x1t = sb["A_x1t"]

    psA, psB = ps["psA"], ps["psB"]
    psTa, psTb = ps["psTa"], ps["psTb"]
    psS1 = ps["psS1"]
    psS2 = ps["psS2"]
    psAV = ps["psAV"]

    t_w = ctx.dma("gpsimd",
                  wq[:].rearrange("p (c o) -> p c o", o=3072),
                  WQKV.rearrange("(c p) o -> p c o", p=128))
    t_wo = ctx.dma("gpsimd",
                   wo[:].rearrange("p (c o) -> p c o", o=1024),
                   WOUT.rearrange("(c p) o -> p c o", p=128))
    t_ct = []
    t_ct.append(ctx.dma("sync", amask[:], d["c_amask"]))
    t_ct.append(ctx.dma("sync",
                        rcos[:].rearrange("p (t dd) -> p t dd", dd=64),
                        d["c_rcos"].rearrange("(t p) dd -> p t dd", p=128)))
    t_ct.append(ctx.dma("sync",
                        rsin[:].rearrange("p (t dd) -> p t dd", dd=64),
                        d["c_rsin"].rearrange("(t p) dd -> p t dd", p=128)))

    last_qkv_psum_read = {0: None, 1: None}
    last_xT_use = None
    last_qT_use = None
    last_kslot_use = {0: None, 1: None}
    last_vslot_use = {0: None, 1: None}
    last_xf_use = {0: None, 1: None}
    last_psT_evac = {0: None, 1: None}
    last_attnT_use = None
    prev_ln_done = None
    t_exp = None
    t_avn = None
    t_x1t_dma = None
    t_x1b_dma = None
    t_dav_prev = None
    t_de_prev = None
    t_rc_prev = None
    t_bc_prev = None
    t_bce_prev = None

    XR = X.rearrange("(t p) dd -> p t dd", p=128)

    for s in range(8):
        sl = s % 2
        w = [t for t in (last_xf_use[sl],) if t]
        t_x = ctx.dma("sync", xf[:].rearrange("p (a t dd) -> p a t dd", a=2, dd=1024)[:, sl],
                      XR[:, s * 4:(s + 1) * 4, :], waits=w)

        w = [t_x]
        if last_xT_use:
            w.append(last_xT_use)
        t_cast = ctx.add("vector", lambda E, s=s, sl=sl: E.tensor_copy(
            xbf[:], xf[:, sl * 4096:(sl + 1) * 4096]), waits=w, inc="ve")

        t_ev = None
        for ti in range(4):
            for c in range(NC):
                pp = (ti * NC + c) % 2
                pst = psTa if pp == 0 else psTb
                w = [t_cast]
                if last_psT_evac[pp]:
                    w.append(last_psT_evac[pp])
                t_tr = ctx.add("tensor", lambda E, ti=ti, c=c, pst=pst: E.transpose(
                    pst[:], xbf[:, ti * 1024 + c * 128: ti * 1024 + (c + 1) * 128], idb[:]),
                    waits=w, inc="te")
                t_ev = ctx.add("vector", lambda E, ti=ti, c=c, pst=pst: E.tensor_copy(
                    xT[:, c * 512 + ti * 128: c * 512 + (ti + 1) * 128], pst[:]),
                    waits=[t_tr], inc="ve")
                last_psT_evac[pp] = t_ev
        t_xT = t_ev

        t_rope_done = None
        for oc in range(6):
            for ti in range(4):
                pp = (oc * 4 + ti) % 2
                pq = psA if pp == 0 else psB
                w = [t_xT, t_w]
                if last_qkv_psum_read[pp]:
                    w.append(last_qkv_psum_read[pp])
                def qkvmm(E, oc=oc, ti=ti, pq=pq):
                    last = None
                    for c in range(NC):
                        last = E.matmul(
                            pq[:],
                            xT[:, c * 512 + ti * 128: c * 512 + (ti + 1) * 128],
                            wq[:, c * 3072 + oc * 512: c * 3072 + (oc + 1) * 512],
                            start=(c == 0), stop=(c == NC - 1))
                    return last
                t_mm = ctx.add("tensor", qkvmm, waits=w, inc="te")

                if oc < 4:
                    gt = s * 4 + ti
                    qk = qkb[:, (pp) * 512:(pp + 1) * 512]
                    def rope(E, pq=pq, gt=gt, qk=qk):
                        pA = pq[:].rearrange("p (h dd) -> p h dd", dd=64)[:, :, 0:32]
                        pB = pq[:].rearrange("p (h dd) -> p h dd", dd=64)[:, :, 32:64]
                        cosA = rcos[:, gt * 64: gt * 64 + 32].unsqueeze(1).broadcast_to([128, 8, 32])
                        cosB = rcos[:, gt * 64 + 32: gt * 64 + 64].unsqueeze(1).broadcast_to([128, 8, 32])
                        sinA = rsin[:, gt * 64: gt * 64 + 32].unsqueeze(1).broadcast_to([128, 8, 32])
                        sinB = rsin[:, gt * 64 + 32: gt * 64 + 64].unsqueeze(1).broadcast_to([128, 8, 32])
                        r3 = lambda t: t.rearrange("p (h dd) -> p h dd", dd=32)
                        E.tensor_tensor(r3(ropet[:, 0:256]), pA, cosA, ALU.mult)
                        E.tensor_tensor(r3(ropet[:, 512:768]), pB, sinA, ALU.mult)
                        E.tensor_tensor(r3(ropet[:, 256:512]), pB, cosB, ALU.mult)
                        E.tensor_tensor(r3(ropet[:, 768:1024]), pA, sinB, ALU.mult)
                        qk3 = qk.rearrange("p (h dd) -> p h dd", dd=64)
                        E.tensor_tensor(qk3[:, :, 0:32], r3(ropet[:, 0:256]),
                                        r3(ropet[:, 512:768]), ALU.subtract)
                        return E.tensor_tensor(qk3[:, :, 32:64], r3(ropet[:, 256:512]),
                                               r3(ropet[:, 768:1024]), ALU.add)
                    t_rope = ctx.add("vector", rope, waits=[t_mm], inc="ve")
                    last_qkv_psum_read[pp] = t_rope
                    for j in range(4):
                        tp = j % 2
                        pst = psTa if tp == 0 else psTb
                        w2 = [t_rope]
                        if last_psT_evac[tp]:
                            w2.append(last_psT_evac[tp])
                        t_tr = ctx.add("tensor", lambda E, qk=qk, j=j, pst=pst: E.transpose(
                            pst[:], qk[:, j * 128:(j + 1) * 128], idb[:]), waits=w2, inc="te")
                        cq = (oc % 2) * 4 + j
                        if oc < 2:
                            dst = qT[:, cq * 512 + ti * 128: cq * 512 + (ti + 1) * 128]
                        else:
                            dst = kT[:, (sl * 8 + cq) * 512 + ti * 128: (sl * 8 + cq) * 512 + (ti + 1) * 128]
                        t_ev = ctx.add("vector", lambda E, dst=dst, pst=pst: E.tensor_copy(
                            dst, pst[:]), waits=[t_tr], inc="ve")
                        last_psT_evac[tp] = t_ev
                        t_rope_done = t_ev
                else:
                    dst = vr[:, (sl * 4 + ti) * 1024 + (oc - 4) * 512:
                             (sl * 4 + ti) * 1024 + (oc - 3) * 512]
                    t_ev = ctx.add("vector", lambda E, dst=dst, pq=pq: E.tensor_copy(
                        dst, pq[:]), waits=[t_mm], inc="ve")
                    last_qkv_psum_read[pp] = t_ev
                    t_rope_done = t_ev
        t_qkv = t_rope_done
        if DEBUG_A and s == 1:
            ctx.dma("sync", d["D_qT"], qT[:], waits=[t_qkv])
            ctx.dma("sync", d["D_kT"], kT[:, 8 * 512:16 * 512], waits=[t_qkv])
            ctx.dma("sync", d["D_vr"], vr[:, 4 * 1024:8 * 1024], waits=[t_qkv])

        for qi in range(2):
            qt = s * 2 + qi
            valid = [m for m in range(6) if qt * 2 - 4 + m >= 0]
            for h in range(NH):
                hh, ck = (h % 2) * 64, h // 2
                w = [t_qkv]
                if t_exp:
                    w.append(t_exp)
                if t_de_prev:
                    w.append(t_de_prev)
                if t_bce_prev:
                    w.append(t_bce_prev)
                def scoresmm(E, qi=qi, qt=qt, h=h, hh=hh, ck=ck, valid=tuple(valid)):
                    last = None
                    for m in valid:
                        jb = qt * 2 - 4 + m
                        slj = (jb // 4) % 2
                        off = (jb % 4) * 128
                        lhsT = kT[hh:hh + 64,
                                  (slj * 8 + ck) * 512 + off: (slj * 8 + ck) * 512 + off + 128]
                        rhs = qT[hh:hh + 64, ck * 512 + qi * 256: ck * 512 + (qi + 1) * 256]
                        if m < 4:
                            out = psS1[:, m * 256:(m + 1) * 256]
                        else:
                            out = psS2[:, (m - 4) * 256:(m - 3) * 256]
                        last = E.matmul(out, lhsT, rhs, start=True, stop=True)
                    return last
                t_sc = ctx.add("tensor", scoresmm, waits=w, inc="te")

                w = [t_sc]
                if t_dav_prev:
                    w.append(t_dav_prev)
                def expf(E, valid=tuple(valid)):
                    v1 = [m for m in valid if m < 4]
                    last = None
                    if v1:
                        a, b = min(v1), max(v1) + 1
                        last = E.activation(pt[:, a * 256:b * 256], psS1[:, a * 256:b * 256],
                                            AF.Exp, scale=0.125)
                    last = E.activation(pt[:, 1024:1536], psS2[:], AF.Exp, scale=0.125)
                    return last
                t_exp = ctx.add("scalar", expf, waits=w, inc="se")

                mask_of = {0: 0, 1: 1, 4: 2, 5: 3}
                def maskf(E, valid=tuple(valid)):
                    last = None
                    for m in valid:
                        if m in mask_of:
                            last = E.tensor_tensor(
                                pt[:, m * 256:(m + 1) * 256], pt[:, m * 256:(m + 1) * 256],
                                amask[:, mask_of[m] * 256:(mask_of[m] + 1) * 256], ALU.mult)
                    return last
                t_msk = ctx.add("vector", maskf, waits=[t_exp], inc="ve")

                w = [t_msk]
                if t_avn:
                    w.append(t_avn)
                def denav(E, s=s, qt=qt, h=h, valid=tuple(valid)):
                    for i, m in enumerate(valid):
                        E.matmul(psS2[0:1, 256:512], onescol[:], pt[:, m * 256:(m + 1) * 256],
                                 start=(i == 0), stop=(i == len(valid) - 1))
                    last = None
                    for i, m in enumerate(valid):
                        jb = qt * 2 - 4 + m
                        slj = (jb // 4) % 2
                        lhsT = vr[:, (slj * 4 + jb % 4) * 1024 + h * 64:
                                  (slj * 4 + jb % 4) * 1024 + (h + 1) * 64]
                        last = E.matmul(psAV[0:64, :],
                                        lhsT, pt[:, m * 256:(m + 1) * 256],
                                        start=(i == 0), stop=(i == len(valid) - 1))
                    return last
                t_dav = ctx.add("tensor", denav, waits=w, inc="te")
                t_dav_prev = t_dav
                if DEBUG_A and qt == 2 and h == 0:
                    ctx.dma("sync", d["D_pt"], pt[:], waits=[t_msk])

                w = [t_dav]
                if t_rc_prev:
                    w.append(t_rc_prev)
                t_de = ctx.add("scalar", lambda E: E.copy(densb[:], psS2[0:1, 256:512]),
                               waits=w, inc="se")
                t_de_prev = t_de
                if DEBUG_A and qt == 2 and h == 0:
                    ctx.dma("sync", d["D_den"], densb[:], waits=[t_de])
                w = [t_de]
                if t_bc_prev:
                    w.append(t_bc_prev)
                t_rc = ctx.add("vector", lambda E: E.reciprocal(recb[:], densb[:]),
                               waits=w, inc="ve")
                t_rc_prev = t_rc
                t_bc = ctx.add("tensor", lambda E: E.matmul(
                    psS1[0:64, 0:256], onesrowf[0:1, 0:64], recb[:],
                    start=True, stop=True), waits=[t_rc], inc="te")
                t_bc_prev = t_bc
                t_bce = ctx.add("vector", lambda E: E.tensor_copy(bcsb[:], psS1[0:64, 0:256]),
                                waits=[t_bc], inc="ve")
                t_bce_prev = t_bce
                w = [t_bce]
                if last_attnT_use:
                    w.append(last_attnT_use)
                t_avn = ctx.add("vector", lambda E, h=h, ck=ck, hh=hh: E.tensor_tensor(
                    attnT[hh:hh + 64, ck * 256:(ck + 1) * 256], psAV[0:64, :], bcsb[:], ALU.mult),
                    waits=w, inc="ve")

            if DEBUG_A and qt == 2:
                ctx.dma("sync", d["D_attnT"], attnT[:], waits=[t_avn])
            last_attnT_use = None
            for qsub in range(2):
                gt = qt * 2 + qsub
                for och in range(2):
                    pp = och % 2
                    pq = psA if pp == 0 else psB
                    w = [t_avn, t_wo]
                    if last_qkv_psum_read[pp]:
                        w.append(last_qkv_psum_read[pp])
                    def opmm(E, qsub=qsub, och=och, pq=pq):
                        last = None
                        for c in range(NC):
                            last = E.matmul(
                                pq[:], attnT[:, c * 256 + qsub * 128: c * 256 + qsub * 128 + 128],
                                wo[:, c * 1024 + och * 512: c * 1024 + (och + 1) * 512],
                                start=(c == 0), stop=(c == NC - 1))
                        return last
                    t_op = ctx.add("tensor", opmm, waits=w, inc="te")
                    w2 = [t_op, t_x]
                    if prev_ln_done:
                        w2.append(prev_ln_done)
                    t_res = ctx.add("vector", lambda E, sl=sl, qi=qi, qsub=qsub, och=och, pq=pq:
                                    E.tensor_tensor(
                                        rln[:, och * 512:(och + 1) * 512], pq[:],
                                        xf[:, sl * 4096 + (qi * 2 + qsub) * 1024 + och * 512:
                                           sl * 4096 + (qi * 2 + qsub) * 1024 + (och + 1) * 512],
                                        ALU.add), waits=w2, inc="ve")
                    last_qkv_psum_read[pp] = t_res
                    if och == 1:
                        t_res_done = t_res
                last_attnT_use = t_res_done
                last_xf_use[sl] = t_res_done
                if DEBUG_A and gt == 4:
                    ctx.dma("sync", d["D_rln"], rln[:], waits=[t_res_done])
                def lnf(E):
                    E.bn_stats(bnst[:, 0:6], rln[:, 0:512])
                    E.bn_stats(bnst[:, 6:12], rln[:, 512:1024])
                    E.drain()
                    E.bn_aggr(mv[:], bnst[:].rearrange("p (g k) -> p g k", k=6))
                    E.drain()
                    return E.nop()
                wbn = [t_res_done]
                if prev_ln_done:
                    wbn.append(prev_ln_done)
                t_bn = ctx.add("vector", lnf, waits=wbn, inc="ve")
                def rstdf(E):
                    E.activation(lnt[:, 0:1], mv[:, 1:2], AF.Ln, bias=1e-5)
                    return E.activation(lnt[:, 1:2], lnt[:, 0:1], AF.Exp, scale=-0.5)
                t_rs = ctx.add("scalar", rstdf, waits=[t_bn], inc="se")
                w3 = [t_rs]
                if t_x1b_dma:
                    w3.append(t_x1b_dma)
                t_ap = ctx.add("vector", lambda E: E.tensor_scalar(
                    x1b[:], rln[:], mv[:, 0:1], lnt[:, 1:2], ALU.subtract, ALU.mult),
                    waits=w3, inc="ve")
                prev_ln_done = t_ap
                if DEBUG_A and gt == 4:
                    ctx.dma("sync", d["D_mv"], mv[:], waits=[t_ap])
                    ctx.dma("sync", d["D_lnt"], lnt[:], waits=[t_ap])
                    ctx.dma("sync", d["D_x1b"], x1b[:], waits=[t_ap])
                t_x1b_dma = ctx.dma("sync", X1D.rearrange("(t p) dd -> p t dd", p=128)[:, gt, :],
                                    x1b[:], waits=[t_ap])
                t_evl = None
                for c in range(NC):
                    tp = c % 2
                    pst = psTa if tp == 0 else psTb
                    w4 = [t_ap]
                    if last_psT_evac[tp]:
                        w4.append(last_psT_evac[tp])
                    t_tr = ctx.add("tensor", lambda E, c=c, pst=pst: E.transpose(
                        pst[:], x1b[:, c * 128:(c + 1) * 128], idb[:]), waits=w4, inc="te")
                    w5 = [t_tr]
                    if t_x1t_dma:
                        w5.append(t_x1t_dma)
                    t_evl = ctx.add("vector", lambda E, c=c, pst=pst: E.tensor_copy(
                        x1t[:, c * 128:(c + 1) * 128], pst[:]), waits=w5, inc="ve")
                    last_psT_evac[tp] = t_evl
                t_x1t_dma = ctx.dma("sync",
                                    X1TD.rearrange("(c p) tt -> p c tt", p=128)[:, :, gt * 128:(gt + 1) * 128],
                                    x1t[:].rearrange("p (c tt) -> p c tt", tt=128),
                                    waits=[t_evl])
        last_xT_use = t_qkv
        last_qT_use = t_qkv
        last_kslot_use[sl] = None
        last_vslot_use[sl] = None

    return ctx.barrier_tok()



def alloc_passA(nc, stack, sb, ps):
    S = lambda name, shape, dt: sb.__setitem__(name, stack.enter_context(
        nc.sbuf_tensor("sb_" + name, shape, dt)))
    S("A_wq", [128, 8 * 3072], BF16)
    S("A_wo", [128, 8 * 1024], BF16)
    S("A_amask", [128, 1024], BF16)
    S("A_rcos", [128, 32 * 64], BF16)
    S("A_rsin", [128, 32 * 64], BF16)
    S("A_xf", [128, 2 * 4096], F32)
    S("A_xbf", [128, 4096], BF16)
    S("A_xT", [128, 8 * 512], BF16)
    S("A_qkb", [128, 2 * 512], BF16)
    S("A_ropet", [128, 4 * 256], F32)
    S("A_qT", [128, 8 * 512], BF16)
    S("A_kT", [128, 2 * 8 * 512], BF16)
    S("A_vr", [128, 2 * 4 * 1024], BF16)
    S("A_pt", [128, 1536], BF16)
    S("A_densb", [1, 256], F32)
    S("A_recb", [1, 256], F32)
    S("A_bcsb", [64, 256], BF16)
    S("A_attnT", [128, 8 * 256], BF16)
    S("A_rln", [128, 1024], F32)
    S("A_bnst", [128, 12], F32)
    S("A_mv", [128, 2], F32)
    S("A_lnt", [128, 2], F32)
    S("A_x1b", [128, 1024], BF16)
    S("A_x1t", [128, 8 * 128], BF16)


def alloc_common(nc, stack, sb):
    S = lambda name, shape, dt: sb.__setitem__(name, stack.enter_context(
        nc.sbuf_tensor("sb_" + name, shape, dt)))
    S("c_idb", [128, 128], BF16)
    S("c_fvals", [128, 4], F32)
    nc.const_aps.register(F32, 1e-5, sb["c_fvals"][:, 0:1])
    nc.const_aps.register(F32, 1e-12, sb["c_fvals"][:, 1:2])
    S("c_onescol", [128, 1], BF16)
    S("c_onesrow", [1, 128], BF16)
    S("c_onesrowf", [1, 128], F32)


def load_common(ctx, d, sb):
    toks = []
    toks.append(ctx.dma("sync", sb["c_idb"][:], d["c_id128b"]))
    toks.append(ctx.dma("sync", sb["c_fvals"][:], d["c_fvals"]))
    toks.append(ctx.dma("sync", sb["c_onescol"][:], d["c_ones_col"]))
    toks.append(ctx.dma("sync", sb["c_onesrow"][:], d["c_ones_row"]))
    toks.append(ctx.dma("sync", sb["c_onesrowf"][:], d["c_ones_rowf"]))
    return toks


def alloc_psum_A(nc, stack, ps):
    P = lambda name, shape, dt: ps.__setitem__(name, stack.enter_context(
        nc.psum_tensor(name, shape, dt)))
    P("psA", [128, 512], F32)
    P("psB", [128, 512], F32)
    P("psTa", [128, 128], BF16)
    P("psTb", [128, 128], BF16)
    P("psS1", [128, 1024], F32)
    P("psS2", [128, 512], F32)
    P("psAV", [128, 256], F32)




def build_passA_program():
    import contextlib
    nc = bass.Bass("TRN2", target_bir_lowering=False)
    d = {}
    d["x"] = nc.dram_tensor("x", [T, DM], F32, kind="ExternalInput").ap()
    d["l0_wqkv"] = nc.dram_tensor("l0_wqkv", [DM, 3 * DM], F32, kind="ExternalInput").ap()
    d["l0_wout"] = nc.dram_tensor("l0_wout", [DM, DM], F32, kind="ExternalInput").ap()
    for k, v in CONSTS.items():
        dt = BF16 if v.dtype == BF else (F32 if v.dtype == np.float32 else U32)
        d[k] = nc.dram_tensor(k, list(v.shape), dt, kind="ExternalInput").ap()
    d["x1d"] = nc.dram_tensor("x1d", [T, DM], BF16, kind="ExternalOutput").ap()
    d["x1td"] = nc.dram_tensor("x1td", [DM, T], BF16, kind="ExternalOutput").ap()
    if DEBUG_A:
        d["D_qT"] = nc.dram_tensor("D_qT", [128, 8 * 512], BF16, kind="ExternalOutput").ap()
        d["D_kT"] = nc.dram_tensor("D_kT", [128, 8 * 512], BF16, kind="ExternalOutput").ap()
        d["D_vr"] = nc.dram_tensor("D_vr", [128, 4 * 1024], BF16, kind="ExternalOutput").ap()
        d["D_pt"] = nc.dram_tensor("D_pt", [128, 1536], BF16, kind="ExternalOutput").ap()
        d["D_den"] = nc.dram_tensor("D_den", [1, 256], F32, kind="ExternalOutput").ap()
        d["D_attnT"] = nc.dram_tensor("D_attnT", [128, 8 * 256], BF16, kind="ExternalOutput").ap()
        d["D_rln"] = nc.dram_tensor("D_rln", [128, 1024], F32, kind="ExternalOutput").ap()
        d["D_mv"] = nc.dram_tensor("D_mv", [128, 2], F32, kind="ExternalOutput").ap()
        d["D_lnt"] = nc.dram_tensor("D_lnt", [128, 2], F32, kind="ExternalOutput").ap()
        d["D_x1b"] = nc.dram_tensor("D_x1b", [128, 1024], BF16, kind="ExternalOutput").ap()

    ctx = Ctx(nc)
    with contextlib.ExitStack() as stack:
        ctx.make_sems(stack)
        sb, ps = {}, {}
        alloc_common(nc, stack, sb)
        alloc_passA(nc, stack, sb, ps)
        alloc_psum_A(nc, stack, ps)
        blk = stack.enter_context(nc.Block())
        load_common(ctx, d, sb)
        emit_passA(ctx, d, sb, ps)
        fin = ctx.barrier_tok()
        ctx.add("sync", lambda E: None, waits=fin)
        for eng in Ctx.ENGS:
            opsl = ctx.q[eng]
            getattr(blk, eng)(lambda E, opsl=opsl: [f(E) for f in opsl])
    return nc


def consts_inmap():
    return {k: np.asarray(v) for k, v in CONSTS.items()}




_PROGS = {}


def _get_prog(name):
    if name not in _PROGS:
        _PROGS[name] = {"A": build_passA_program, "B": build_passB_program,
                        "C": build_passC_program, "D": build_passD_program}[name]()
    return _PROGS[name]


def kernel(**inputs):
    consts = consts_inmap()
    f32 = lambda k: np.ascontiguousarray(np.asarray(inputs[k], np.float32))
    X = f32("L0_attn_x")
    ncA = _get_prog("A")
    mapsA = [dict(x=np.ascontiguousarray(X[b]), l0_wqkv=f32("L0_attn_Wqkv"),
                  l0_wout=f32("L0_attn_Wout"), **consts) for b in range(8)]
    resA = run_bass_kernel_spmd(ncA, mapsA, core_ids=list(range(8))).results

    ncB = _get_prog("B")
    mapsB = [dict(x1d_in=resA[b]["x1d"], x1td_in=resA[b]["x1td"],
                  l0_wg=f32("L0_ffn_Wg"), l0_bg=f32("L0_ffn_bg"),
                  l0_we=f32("L0_ffn_We"), l0_be=f32("L0_ffn_be"), **consts)
             for b in range(8)]
    resB = run_bass_kernel_spmd(ncB, mapsB, core_ids=list(range(8))).results

    ncC = _get_prog("C")
    mapsC = [dict(x2d_in=resB[b]["x2d"], x2td_in=resB[b]["x2td"],
                  l1_wqkv=f32("L1_attn_w_qkv"), l1_wout=f32("L1_attn_w_out"),
                  l1_bout=f32("L1_attn_b_out"), l1_ng=f32("L1_attn_norm_g"),
                  l1_nb=f32("L1_attn_norm_b"), l1_rw=f32("L1_ffn_router_w"), **consts)
             for b in range(8)]
    resC = run_bass_kernel_spmd(ncC, mapsC, core_ids=list(range(8))).results

    ncD = _get_prog("D")
    mapsD = [dict(x3d_in=resC[b]["x3d"], logitsd_in=resC[b]["logitsd"],
                  l1_w0=f32("L1_ffn_w0_kernel"), l1_w1=f32("L1_ffn_w1_kernel"),
                  l1_wo=f32("L1_ffn_wo_kernel"), **consts) for b in range(8)]
    resD = run_bass_kernel_spmd(ncD, mapsD, core_ids=list(range(8))).results

    return np.stack([np.asarray(resD[b]["out"], np.float32) for b in range(8)], axis=0)


if __name__ == "__main__":
    nc = build_passA_program()
    print("built, instructions:", nc.n_instructions() if hasattr(nc, "n_instructions") else "?")



def alloc_passB(nc, stack, sb, ps):
    S = lambda name, shape, dt: sb.__setitem__(name, stack.enter_context(
        nc.sbuf_tensor("sb_" + name, shape, dt)))
    S("B_wg", [128, 8 * 1024], BF16)
    S("B_bg", [1, 1024], BF16)
    S("B_be", [1, 8 * 1024], BF16)
    S("B_we", [128, 2 * 8 * 1024], BF16)
    S("B_x1t", [128, 4 * 1024], BF16)
    S("B_maskp", [128, 32 * 8], F32)
    S("B_resmax", [128, 8], F32)
    S("B_m", [128, 4], F32)
    S("B_junk", [128, 1024], BF16)
    S("B_tt", [128, 4 * 512], BF16)
    S("B_acc", [128, 32 * 1024], BF16)
    S("B_x1b", [128, 2 * 1024], BF16)
    S("B_rln", [128, 1024], F32)
    S("B_bnst", [128, 12], F32)
    S("B_mv", [128, 2], F32)
    S("B_lnt", [128, 2], F32)
    S("B_x2b", [128, 1024], BF16)
    S("B_x2t", [128, 8 * 128], BF16)


def alloc_psum_B(nc, stack, ps):
    P = lambda name, shape, dt: ps.__setitem__(name, stack.enter_context(
        nc.psum_tensor(name, shape, dt)))
    P("psL", [128, 1024], F32)
    P("psE0", [128, 512], F32)
    P("psE1", [128, 512], F32)
    P("psE2", [128, 512], F32)
    P("psE3", [128, 512], F32)
    P("psTa", [128, 128], BF16)
    P("psTb", [128, 128], BF16)


def emit_passB(ctx, d, sb, ps):
    nc = ctx.nc
    X1D, X1TD = d["x1d_in"], d["x1td_in"]
    WG, BG, WE, BE = d["l0_wg"], d["l0_bg"], d["l0_we"], d["l0_be"]
    X2D, X2TD = d["x2d"], d["x2td"]

    idb = sb["c_idb"]
    onesrow = sb["c_onesrow"]
    wg = sb["B_wg"]; bg = sb["B_bg"]; be8 = sb["B_be"]; we = sb["B_we"]
    x1t = sb["B_x1t"]; maskp = sb["B_maskp"]; resmax = sb["B_resmax"]
    mcol = sb["B_m"]; junk = sb["B_junk"]; tt = sb["B_tt"]; acc = sb["B_acc"]
    x1b = sb["B_x1b"]; rln = sb["B_rln"]; bnst = sb["B_bnst"]
    mv = sb["B_mv"]; lnt = sb["B_lnt"]; x2b = sb["B_x2b"]; x2t = sb["B_x2t"]
    psL = ps["psL"]
    psE = [ps["psE0"], ps["psE1"], ps["psE2"], ps["psE3"]]
    psTa, psTb = ps["psTa"], ps["psTb"]

    X1T_R = X1TD.rearrange("(c p) t -> p c t", p=128)
    X1D_R = X1D.rearrange("(t p) dd -> p t dd", p=128)

    t_wg = ctx.dma("gpsimd", wg[:].rearrange("p (c o) -> p c o", o=1024),
                   WG.rearrange("(c p) o -> p c o", p=128))
    t_bg = ctx.dma("gpsimd", bg[:], BG.unsqueeze(0))
    t_be = ctx.dma("gpsimd", be8[:], BE.rearrange("e o -> (e o)").unsqueeze(0))

    t_x1t_use = [None] * 4
    t_exp_prev = None
    t_red_prev = None
    t_gate_done = [None] * 32
    t_x1t_tok = [None] * 32
    for t in range(NT):
        sl = t % 4
        w = [t_x1t_use[sl]] if t_x1t_use[sl] else []
        t_ld = ctx.dma("sync", x1t[:].rearrange("p (s c tt) -> p s c tt", s=4, tt=128)[:, sl],
                       X1T_R[:, :, t * 128:(t + 1) * 128], waits=w)
        t_x1t_tok[t] = t_ld
        w = [t_ld, t_wg, t_bg]
        if t_exp_prev:
            w.append(t_exp_prev)
        if t_red_prev:
            w.append(t_red_prev)
        def gatemm(E, sl=sl):
            last = None
            for fc in range(2):
                for c in range(NC):
                    E.matmul(psL[:, fc * 512:(fc + 1) * 512],
                             x1t[:, (sl * 8 + c) * 128:(sl * 8 + c + 1) * 128],
                             wg[:, c * 1024 + fc * 512: c * 1024 + (fc + 1) * 512],
                             start=(c == 0), stop=False)
                last = E.matmul(psL[:, fc * 512:(fc + 1) * 512], onesrow[0:1, :],
                                bg[0:1, fc * 512:(fc + 1) * 512], start=False, stop=True)
            return last
        t_mm = ctx.add("tensor", gatemm, waits=w, inc="te")
        t_x1t_use[sl] = t_mm
        def redf(E, t=t):
            lv = psL[:].rearrange("p (g e) -> p e g", e=8)
            E.tensor_reduce(resmax[:], lv, AX.X, ALU.max)
            E.drain()
            E.tensor_reduce(mcol[:, 0:1], resmax[:], AX.X, ALU.max)
            E.drain()
            return E.tensor_scalar(mcol[:, 1:2], mcol[:, 0:1], -1.0, None, ALU.mult)
        t_red = ctx.add("vector", redf, waits=[t_mm], inc="ve")
        t_red_prev = t_red
        t_exp = ctx.add("scalar", lambda E: E.activation(
            junk[:], psL[:], AF.Exp, bias=mcol[:, 1:2], accum_out=mcol[:, 2:3]),
            waits=[t_red], inc="se")
        t_exp_prev = t_exp
        def pf(E, t=t):
            E.reciprocal(mcol[:, 3:4], mcol[:, 2:3])
            E.drain()
            return E.tensor_scalar(maskp[:, t * 8:(t + 1) * 8], resmax[:],
                                   mcol[:, 0:1], mcol[:, 3:4], ALU.is_equal, ALU.mult)
        t_gate_done[t] = ctx.add("vector", pf, waits=[t_exp], inc="ve")

    WEV = we[:].rearrange("p (s c o) -> p s c o", s=2, o=1024)
    X1TV = x1t[:].rearrange("p (s c tt) -> p s c tt", s=4, tt=128)

    def we_load(e, waits):
        return ctx.dma("gpsimd", WEV[:, e % 2],
                       WE[e].rearrange("(c p) o -> p c o", p=128), waits=waits)

    t_we = [we_load(0, []), None]
    t_we[1] = we_load(1, [])
    t_we_lastmm = [None, None]
    t_relu = {}
    t_add = {}
    t_tt_use = [None] * 4
    tt_i = 0
    for e in range(E0):
        sl = e % 2
        for t in range(NT):
            tsl = t % 4
            w = [t_x1t_use[tsl]] if t_x1t_use[tsl] else []
            t_ld = ctx.dma("sync", X1TV[:, tsl],
                           X1T_R[:, :, t * 128:(t + 1) * 128], waits=w)
            for fc in range(2):
                pe = psE[(t * 2 + fc) % 4]
                w = [t_ld, t_we[sl], t_be]
                prev = t_relu.get(((t * 2 + fc) % 4, "psum"))
                if prev:
                    w.append(prev)
                def emm(E, tsl=tsl, fc=fc, e=e, pe=pe, sl=sl):
                    for c in range(NC):
                        E.matmul(pe[:],
                                 x1t[:, (tsl * 8 + c) * 128:(tsl * 8 + c + 1) * 128],
                                 we[:, (sl * 8 + c) * 1024 + fc * 512:
                                    (sl * 8 + c) * 1024 + (fc + 1) * 512],
                                 start=(c == 0), stop=False)
                    return E.matmul(pe[:], onesrow[0:1, :],
                                    be8[0:1, e * 1024 + fc * 512: e * 1024 + (fc + 1) * 512],
                                    start=False, stop=True)
                t_mm = ctx.add("tensor", emm, waits=w, inc="te")
                t_x1t_use[tsl] = t_mm
                t_we_lastmm[sl] = t_mm
                if e == 0:
                    dst = acc[:, t * 1024 + fc * 512: t * 1024 + (fc + 1) * 512]
                    w2 = [t_mm, t_gate_done[t]]
                    t_rl = ctx.add("scalar", lambda E, dst=dst, pe=pe, t=t, e=e: E.activation(
                        dst, pe[:], AF.Relu, scale=maskp[:, t * 8 + e: t * 8 + e + 1]),
                        waits=w2, inc="se")
                    t_relu[((t * 2 + fc) % 4, "psum")] = t_rl
                    t_add[(t, fc)] = t_rl
                else:
                    ti = tt_i % 4
                    tt_i += 1
                    dst = tt[:, ti * 512:(ti + 1) * 512]
                    w2 = [t_mm, t_gate_done[t]]
                    if t_tt_use[ti]:
                        w2.append(t_tt_use[ti])
                    t_rl = ctx.add("scalar", lambda E, dst=dst, pe=pe, t=t, e=e: E.activation(
                        dst, pe[:], AF.Relu, scale=maskp[:, t * 8 + e: t * 8 + e + 1]),
                        waits=w2, inc="se")
                    t_relu[((t * 2 + fc) % 4, "psum")] = t_rl
                    aslice = acc[:, t * 1024 + fc * 512: t * 1024 + (fc + 1) * 512]
                    t_ad = ctx.add("vector", lambda E, aslice=aslice, dst=dst: E.tensor_tensor(
                        aslice, aslice, dst, ALU.add), waits=[t_rl, t_add[(t, fc)]], inc="ve")
                    t_tt_use[ti] = t_ad
                    t_add[(t, fc)] = t_ad
        if e + 2 < E0:
            we_next_tok = we_load(e + 2, [t_we_lastmm[sl]])
            t_we[sl] = we_next_tok

    t_x1b_use = [None, None]
    prev_ln_done = None
    t_x2b_dma = None
    t_x2t_dma = None
    last_psT_evac = {0: None, 1: None}
    t_rln_prev = None
    t_x2trans_prev = None
    for t in range(NT):
        sl = t % 2
        w = [t_x1b_use[sl]] if t_x1b_use[sl] else []
        t_ld = ctx.dma("sync", x1b[:, sl * 1024:(sl + 1) * 1024],
                       X1D_R[:, t, :], waits=w)
        w = [t_ld, t_add[(t, 0)], t_add[(t, 1)]]
        if t_rln_prev:
            w.append(t_rln_prev)
        t_res = ctx.add("vector", lambda E, t=t, sl=sl: E.tensor_tensor(
            rln[:], acc[:, t * 1024:(t + 1) * 1024],
            x1b[:, sl * 1024:(sl + 1) * 1024], ALU.add), waits=w, inc="ve")
        t_x1b_use[sl] = t_res
        def lnf(E):
            E.bn_stats(bnst[:, 0:6], rln[:, 0:512])
            E.bn_stats(bnst[:, 6:12], rln[:, 512:1024])
            E.drain()
            E.bn_aggr(mv[:], bnst[:].rearrange("p (g k) -> p g k", k=6))
            E.drain()
            return E.nop()
        wbn = [t_res]
        if prev_ln_done:
            wbn.append(prev_ln_done)
        t_bn = ctx.add("vector", lnf, waits=wbn, inc="ve")
        def rstdf(E):
            E.activation(lnt[:, 0:1], mv[:, 1:2], AF.Ln, bias=1e-5)
            return E.activation(lnt[:, 1:2], lnt[:, 0:1], AF.Exp, scale=-0.5)
        t_rs = ctx.add("scalar", rstdf, waits=[t_bn], inc="se")
        w3 = [t_rs]
        if t_x2b_dma:
            w3.append(t_x2b_dma)
        if t_x2trans_prev:
            w3.append(t_x2trans_prev)
        t_ap = ctx.add("vector", lambda E: E.tensor_scalar(
            x2b[:], rln[:], mv[:, 0:1], lnt[:, 1:2], ALU.subtract, ALU.mult),
            waits=w3, inc="ve")
        prev_ln_done = t_ap
        t_rln_prev = t_ap
        t_x2b_dma = ctx.dma("sync", X2D.rearrange("(t p) dd -> p t dd", p=128)[:, t, :],
                            x2b[:], waits=[t_ap])
        t_evl = None
        for c in range(NC):
            tp = c % 2
            pst = psTa if tp == 0 else psTb
            w4 = [t_ap]
            if last_psT_evac[tp]:
                w4.append(last_psT_evac[tp])
            t_tr = ctx.add("tensor", lambda E, c=c, pst=pst: E.transpose(
                pst[:], x2b[:, c * 128:(c + 1) * 128], idb[:]), waits=w4, inc="te")
            w5 = [t_tr]
            if t_x2t_dma:
                w5.append(t_x2t_dma)
            t_evl = ctx.add("vector", lambda E, c=c, pst=pst: E.tensor_copy(
                x2t[:, c * 128:(c + 1) * 128], pst[:]), waits=w5, inc="ve")
            last_psT_evac[tp] = t_evl
            t_x2trans_prev = t_tr
        t_x2t_dma = ctx.dma("sync",
                            X2TD.rearrange("(c p) tt -> p c tt", p=128)[:, :, t * 128:(t + 1) * 128],
                            x2t[:].rearrange("p (c tt) -> p c tt", tt=128),
                            waits=[t_evl])
    return ctx.barrier_tok()


def build_passB_program():
    import contextlib
    nc = bass.Bass("TRN2", target_bir_lowering=False)
    d = {}
    d["x1d_in"] = nc.dram_tensor("x1d_in", [T, DM], BF16, kind="ExternalInput").ap()
    d["x1td_in"] = nc.dram_tensor("x1td_in", [DM, T], BF16, kind="ExternalInput").ap()
    d["l0_wg"] = nc.dram_tensor("l0_wg", [DM, DM], F32, kind="ExternalInput").ap()
    d["l0_bg"] = nc.dram_tensor("l0_bg", [DM], F32, kind="ExternalInput").ap()
    d["l0_we"] = nc.dram_tensor("l0_we", [E0, DM, DM], F32, kind="ExternalInput").ap()
    d["l0_be"] = nc.dram_tensor("l0_be", [E0, DM], F32, kind="ExternalInput").ap()
    for k, v in CONSTS.items():
        dt = BF16 if v.dtype == BF else (F32 if v.dtype == np.float32 else U32)
        d[k] = nc.dram_tensor(k, list(v.shape), dt, kind="ExternalInput").ap()
    d["x2d"] = nc.dram_tensor("x2d", [T, DM], BF16, kind="ExternalOutput").ap()
    d["x2td"] = nc.dram_tensor("x2td", [DM, T], BF16, kind="ExternalOutput").ap()

    ctx = Ctx(nc)
    with contextlib.ExitStack() as stack:
        ctx.make_sems(stack)
        sb, ps = {}, {}
        alloc_common(nc, stack, sb)
        alloc_passB(nc, stack, sb, ps)
        alloc_psum_B(nc, stack, ps)
        blk = stack.enter_context(nc.Block())
        load_common(ctx, d, sb)
        emit_passB(ctx, d, sb, ps)
        fin = ctx.barrier_tok()
        ctx.add("sync", lambda E: None, waits=fin)
        for eng in Ctx.ENGS:
            opsl = ctx.q[eng]
            getattr(blk, eng)(lambda E, opsl=opsl: [f(E) for f in opsl])
    return nc



def alloc_passC(nc, stack, sb):
    S = lambda name, shape, dt: sb.__setitem__(name, stack.enter_context(
        nc.sbuf_tensor("sb_" + name, shape, dt)))
    S("C_wtmp", [128, 1024], BF16)
    S("C_wq", [128, 8 * 1024], BF16)
    S("C_wkv", [128, 8 * 2048], BF16)
    S("C_wo", [128, 8 * 1024], BF16)
    S("C_ng", [1, 16], F32)
    S("C_nb2rhs", [2, 8], F32)
    S("C_bout", [1, 1024], F32)
    S("C_rw", [128, 8 * 8], F32)
    S("C_x2t", [128, 4 * 1024], BF16)
    S("C_kt", [128, 1024], BF16)
    S("C_vt", [128, 1024], BF16)
    S("C_M", [128, 2 * 512], BF16)
    S("C_N", [128, 2 * 512], BF16)
    S("C_Macc", [128, 4 * 512], F32)
    S("C_Sacc", [128, 8], F32)
    S("C_S", [128, 8], F32)
    S("C_nb2", [128, 8], F32)
    S("C_bS", [128, 8], BF16)
    S("C_cvec", [1, 1024], F32)
    S("C_cvh", [1, 1024], BF16)
    S("C_x2s", [128, 8 * 512], BF16)
    S("C_qT", [128, 8 * 512], BF16)
    S("C_nsq", [64, 512], BF16)
    S("C_snorm", [1, 512], F32)
    S("C_arow", [1, 512], F32)
    S("C_bcsb", [64, 512], F32)
    S("C_attnT", [128, 8 * 512], BF16)
    S("C_x2b", [128, 2 * 1024], BF16)
    S("C_rln", [128, 1024], F32)
    S("C_bnst", [128, 12], F32)
    S("C_mv", [128, 2], F32)
    S("C_lnt", [128, 2], F32)
    S("C_x3f", [128, 1024], F32)
    S("C_x3b", [128, 1024], BF16)
    S("C_x3tf", [128, 128], F32)
    S("C_logits", [128, 32 * 8], F32)


def emit_passC(ctx, d, sb, stack):
    nc = ctx.nc
    X2D, X2TD = d["x2d_in"], d["x2td_in"]
    WQKV1, WOUT1, BOUT1 = d["l1_wqkv"], d["l1_wout"], d["l1_bout"]
    NG, NB, RW = d["l1_ng"], d["l1_nb"], d["l1_rw"]
    X3D, LOGD = d["x3d"], d["logitsd"]

    idb = sb["c_idb"]; idf = sb["c_idf"]
    onescol = sb["c_onescol"]; onesrowf = sb["c_onesrowf"]
    wtmp = sb["C_wtmp"]; wq = sb["C_wq"]; wkv = sb["C_wkv"]; wo = sb["C_wo"]
    ngr = sb["C_ng"]; nb2rhs = sb["C_nb2rhs"]; boutr = sb["C_bout"]; rw = sb["C_rw"]
    x2t = sb["C_x2t"]; kt = sb["C_kt"]; vt = sb["C_vt"]
    Msb = sb["C_M"]; Nsb = sb["C_N"]; Ssb = sb["C_S"]
    Macc = sb["C_Macc"]; Sacc = sb["C_Sacc"]
    nb2 = sb["C_nb2"]; bS = sb["C_bS"]; cvec = sb["C_cvec"]; cvh = sb["C_cvh"]
    x2s = sb["C_x2s"]; qTt = sb["C_qT"]; nsq = sb["C_nsq"]
    snorm = sb["C_snorm"]; arow = sb["C_arow"]; bcsb = sb["C_bcsb"]
    attnT = sb["C_attnT"]
    x2b = sb["C_x2b"]; rln = sb["C_rln"]; bnst = sb["C_bnst"]
    mv = sb["C_mv"]; lnt = sb["C_lnt"]; x3f = sb["C_x3f"]; x3b = sb["C_x3b"]
    x3tf = sb["C_x3tf"]; logits = sb["C_logits"]

    X2T_R = X2TD.rearrange("(c p) t -> p c t", p=128)
    X2D_R = X2D.rearrange("(t p) dd -> p t dd", p=128)

    t_ng = ctx.dma("sync", ngr[:], NG.rearrange("a h bb e -> a (h bb e)"))
    t_bout = ctx.dma("sync", boutr[:], BOUT1.unsqueeze(0))
    t_rw = ctx.dma("sync", rw[:].rearrange("p (c e) -> p c e", e=8),
                   RW.rearrange("(c p) e -> p c e", p=128))
    t_nb = ctx.dma("sync", nb2rhs[:], NB.rearrange("a (c i) bb e -> i (a c bb e)", i=2),
                   allow_slow_non_contiguous=True)

    with (
        nc.psum_tensor("psCTa", [128, 128], BF16) as psTa,
        nc.psum_tensor("psCTb", [128, 128], BF16) as psTb,
        nc.psum_tensor("psC_nb", [128, 8], F32) as psNB2,
    ):
        last_ev = {0: None, 1: None}
        t_wl_prev = None

        def wload_transpose(SRC, dst, oblocks):
            nonlocal t_wl_prev
            last = None
            for i, ob in enumerate(oblocks):
                w = [t_wl_prev] if t_wl_prev else []
                t_ld = ctx.dma("gpsimd", wtmp[:],
                               SRC[ob * 128:(ob + 1) * 128, :], waits=w)
                for c in range(NC):
                    tp = c % 2
                    w2 = [t_ld]
                    if last_ev[tp]:
                        w2.append(last_ev[tp])
                    pst = psTa if tp == 0 else psTb
                    t_tr = ctx.add("tensor", lambda E, c=c, pst=pst: E.transpose(
                        pst[:], wtmp[:, c * 128:(c + 1) * 128], idb[:]), waits=w2, inc="te")
                    t_ev = ctx.add("vector", lambda E, c=c, i=i, pst=pst, dst=dst: E.tensor_copy(
                        dst[:].rearrange("p (c o) -> p c o", c=NC)[:, c, i * 128:(i + 1) * 128],
                        pst[:]), waits=[t_tr], inc="ve")
                    last_ev[tp] = t_ev
                    last = t_ev
                t_wl_prev = last
            return last

        t_wq = wload_transpose(WQKV1, wq, list(range(8)))
        t_wkv = wload_transpose(WQKV1, wkv, list(range(8, 24)))
        t_wo = wload_transpose(WOUT1, wo, list(range(8)))
        t_nb2 = ctx.add("tensor", lambda E: E.matmul(
            psNB2[:], sb["c_bc2f"][:], nb2rhs[:], start=True, stop=True),
            waits=[t_nb, d["tok_bc2f"]], inc="te")
        t_nb2e = ctx.add("vector", lambda E: E.tensor_copy(nb2[:], psNB2[:]),
                         waits=[t_nb2], inc="ve")
    t_c0 = ctx.barrier_tok()

    with (
        nc.psum_tensor("psKV", [128, 512], F32) as psKV,
        nc.psum_tensor("psKV2", [128, 512], F32) as psKV2,
        nc.psum_tensor("psMA", [128, 512], F32) as psMA,
        nc.psum_tensor("psMB", [128, 512], F32) as psMB,
        nc.psum_tensor("psNA", [128, 512], F32) as psNA,
        nc.psum_tensor("psNBv", [128, 512], F32) as psNAB,
        nc.psum_tensor("psS", [128, 8], F32) as psS,
        nc.psum_tensor("psCV", [1, 512], F32) as psCV,
    ):
        t_x2t_use = [None] * 4
        t_kv_ev = {0: None, 1: None}
        t_stats_done = None
        for t in range(NT):
            tsl = t % 4
            w = [t_x2t_use[tsl]] if t_x2t_use[tsl] else []
            t_ld = ctx.dma("sync",
                           x2t[:].rearrange("p (s c tt) -> p s c tt", s=4, tt=128)[:, tsl],
                           X2T_R[:, :, t * 128:(t + 1) * 128], waits=w)
            t_mm_last = None
            for g in range(4):
                pp = g % 2
                pkv = psKV if pp == 0 else psKV2
                w2 = [t_ld, t_wkv]
                if t == 0 and g < 2:
                    w2 += t_c0
                if t_kv_ev[pp]:
                    w2.append(t_kv_ev[pp])
                def kvmm(E, tsl=tsl, g=g, pkv=pkv):
                    last = None
                    for c in range(NC):
                        last = E.matmul(
                            pkv[:],
                            x2t[:, (tsl * 8 + c) * 128:(tsl * 8 + c + 1) * 128],
                            wkv[:, c * 2048 + g * 512: c * 2048 + (g + 1) * 512],
                            start=(c == 0), stop=(c == NC - 1))
                    return last
                t_mm = ctx.add("tensor", kvmm, waits=w2, inc="te")
                dst = (kt if g < 2 else vt)[:, (g % 2) * 512:((g % 2) + 1) * 512]
                t_ev = ctx.add("vector", lambda E, dst=dst, pkv=pkv: E.tensor_copy(
                    dst, pkv[:]), waits=[t_mm], inc="ve")
                t_kv_ev[pp] = t_ev
                t_mm_last = t_ev
            def statmm(E, t=t):
                last = None
                for hp in range(8):
                    lk = kt[:, hp * 128:(hp + 1) * 128]
                    E.matmul(psMA[:, hp * 64:(hp + 1) * 64], lk,
                             kt[:, hp * 128: hp * 128 + 64], start=True, stop=True)
                    E.matmul(psMB[:, hp * 64:(hp + 1) * 64], lk,
                             kt[:, hp * 128 + 64: hp * 128 + 128], start=True, stop=True)
                    E.matmul(psNA[:, hp * 64:(hp + 1) * 64], lk,
                             vt[:, hp * 128: hp * 128 + 64], start=True, stop=True)
                    last = E.matmul(psNAB[:, hp * 64:(hp + 1) * 64], lk,
                                    vt[:, hp * 128 + 64: hp * 128 + 128], start=True, stop=True)
                for c in range(NC):
                    last = E.matmul(psS[:, c:c + 1], vt[:, c * 128:(c + 1) * 128],
                                    onescol[:], start=True, stop=True)
                return last
            w4 = [t_mm_last]
            if t_stats_done:
                w4.append(t_stats_done)
            t_st = ctx.add("tensor", statmm, waits=w4, inc="te")
            t_x2t_use[tsl] = t_st
            def stacc(E, t=t):
                if t == 0:
                    E.tensor_copy(Macc[:, 0:512], psMA[:])
                    E.tensor_copy(Macc[:, 512:1024], psMB[:])
                    E.tensor_copy(Macc[:, 1024:1536], psNA[:])
                    E.tensor_copy(Macc[:, 1536:2048], psNAB[:])
                    return E.tensor_copy(Sacc[:], psS[:])
                E.tensor_tensor(Macc[:, 0:512], Macc[:, 0:512], psMA[:], ALU.add)
                E.tensor_tensor(Macc[:, 512:1024], Macc[:, 512:1024], psMB[:], ALU.add)
                E.tensor_tensor(Macc[:, 1024:1536], Macc[:, 1024:1536], psNA[:], ALU.add)
                E.tensor_tensor(Macc[:, 1536:2048], Macc[:, 1536:2048], psNAB[:], ALU.add)
                return E.tensor_tensor(Sacc[:], Sacc[:], psS[:], ALU.add)
            t_stats_done = ctx.add("vector", stacc, waits=[t_st], inc="ve")

        def evst(E):
            E.tensor_copy(Msb[:, 0:512], Macc[:, 0:512])
            E.tensor_copy(Msb[:, 512:1024], Macc[:, 512:1024])
            E.tensor_copy(Nsb[:, 0:512], Macc[:, 1024:1536])
            E.tensor_copy(Nsb[:, 512:1024], Macc[:, 1536:2048])
            E.tensor_copy(Ssb[:], Sacc[:])
            E.drain()
            return E.tensor_tensor(bS[:], Ssb[:], nb2[:], ALU.mult)
        t_evst = ctx.add("vector", evst, waits=[t_stats_done, t_nb2e], inc="ve")

        t_cv_prev = None
        for oc in range(2):
            def cvh_mm(E, oc=oc):
                last = None
                for c in range(NC):
                    last = E.matmul(psCV[0:1, :], bS[:, c:c + 1],
                                    wo[:, c * 1024 + oc * 512: c * 1024 + (oc + 1) * 512],
                                    start=(c == 0), stop=(c == NC - 1))
                return last
            w4 = [t_evst, t_wo]
            if t_cv_prev:
                w4.append(t_cv_prev)
            t_m = ctx.add("tensor", cvh_mm, waits=w4, inc="te")
            t_cv_prev = ctx.add("vector", lambda E, oc=oc: E.tensor_tensor(
                cvec[0:1, oc * 512:(oc + 1) * 512], psCV[0:1, :],
                boutr[0:1, oc * 512:(oc + 1) * 512], ALU.add),
                waits=[t_m, t_bout], inc="ve")
        t_cvh = ctx.add("vector", lambda E: E.tensor_copy(cvh[:], cvec[:]),
                        waits=[t_cv_prev], inc="ve")
    t_c1 = ctx.barrier_tok()

    with (
        nc.psum_tensor("psQa", [128, 512], F32) as psQa,
        nc.psum_tensor("psQb", [128, 512], F32) as psQb,
        nc.psum_tensor("psT1", [128, 512], F32) as psT1,
        nc.psum_tensor("psO", [128, 512], F32) as psO,
        nc.psum_tensor("psM2", [128, 512], F32) as psM2,
        nc.psum_tensor("psBC", [64, 512], F32) as psBC,
        nc.psum_tensor("psY", [128, 512], F32) as psY,
        nc.psum_tensor("psRT", [128, 128], F32) as psRT,
    ):
        t_x2s_use = None
        t_q_ev = {0: None, 1: None}
        t_nsq_use = None
        t_sn = None
        t_ar_use = None
        t_bce = None
        t_avn = None
        t_attnT_use = None
        t_y_ev = None
        t_x2b_use = [None, None]
        prev_ln_done = None
        t_x3b_dma = None
        t_rln_prev = None
        t_x3trans_prev = None
        t_x3f_use = None
        t_rt_ev = None
        t_rl_ev_prev = None
        t_nsq_mm_prev = None
        t_norm_read_prev = None
        first = True
        for s in range(8):
            w = [t_x2s_use] if t_x2s_use else []
            t_lds = ctx.dma("sync", x2s[:].rearrange("p (c t) -> p c t", t=512),
                            X2T_R[:, :, s * 512:(s + 1) * 512], waits=w)
            t_qT = None
            for oc in range(NC):
                pp = oc % 2
                pq = psQa if pp == 0 else psQb
                w2 = [t_lds, t_wq]
                if first:
                    w2 += t_c1
                if t_q_ev[pp]:
                    w2.append(t_q_ev[pp])
                if t_rt_ev:
                    w2.append(t_rt_ev)
                if t_rl_ev_prev:
                    w2.append(t_rl_ev_prev)
                first = False
                def qmm(E, oc=oc, pq=pq):
                    last = None
                    for c in range(NC):
                        last = E.matmul(
                            pq[:], wq[:, c * 1024 + oc * 128: c * 1024 + (oc + 1) * 128],
                            x2s[:, c * 512:(c + 1) * 512],
                            start=(c == 0), stop=(c == NC - 1))
                    return last
                t_mm = ctx.add("tensor", qmm, waits=w2, inc="te")
                t_ev = ctx.add("vector", lambda E, oc=oc, pq=pq: E.tensor_copy(
                    qTt[:, oc * 512:(oc + 1) * 512], pq[:]), waits=[t_mm], inc="ve")
                t_q_ev[pp] = t_ev
                t_qT = t_ev
            for h in range(NH):
                hp, r0 = h // 2, (h % 2) * 64
                Mh = Msb[r0:r0 + 64, (h % 2) * 512 + hp * 64: (h % 2) * 512 + (hp + 1) * 64]
                Nh = Nsb[r0:r0 + 64, (h % 2) * 512 + hp * 64: (h % 2) * 512 + (hp + 1) * 64]
                qh = qTt[r0:r0 + 64, hp * 512:(hp + 1) * 512]
                w3 = [t_qT]
                if t_nsq_use:
                    w3.append(t_nsq_use)
                t_t1 = ctx.add("tensor", lambda E, Mh=Mh, qh=qh: E.matmul(
                    psT1[0:64, :], Mh, qh, start=True, stop=True), waits=w3, inc="te")
                w3b = [t_qT]
                if t_avn:
                    w3b.append(t_avn)
                t_ou = ctx.add("tensor", lambda E, Nh=Nh, qh=qh: E.matmul(
                    psO[0:64, :], Nh, qh, start=True, stop=True), waits=w3b, inc="te")
                w4 = [t_t1]
                if t_nsq_mm_prev:
                    w4.append(t_nsq_mm_prev)
                t_nsq_use = ctx.add("vector", lambda E, qh=qh: E.tensor_tensor(
                    nsq[:], psT1[0:64, :], qh, ALU.mult), waits=w4, inc="ve")
                w5 = [t_nsq_use]
                if t_norm_read_prev:
                    w5.append(t_norm_read_prev)
                t_n2 = ctx.add("tensor", lambda E: E.matmul(
                    psM2[0:1, 0:512], onescol[0:64, :], nsq[:], start=True, stop=True),
                    waits=w5, inc="te")
                t_nsq_mm_prev = t_n2
                w6 = [t_n2]
                if t_ar_use:
                    w6.append(t_ar_use)
                t_sn = ctx.add("scalar", lambda E: E.activation(
                    snorm[:], psM2[0:1, 0:512], AF.Sqrt), waits=w6, inc="se")
                t_norm_read_prev = t_sn
                def archain(E, h=h):
                    E.tensor_scalar(arow[:], snorm[:], 1e-12, None, ALU.max)
                    E.drain()
                    E.reciprocal(arow[:], arow[:])
                    E.drain()
                    return E.tensor_scalar(arow[:], arow[:], ngr[0:1, h:h + 1], None, ALU.mult)
                t_ar = ctx.add("vector", archain, waits=[t_sn, t_ng], inc="ve")
                w7 = [t_ar]
                if t_bce:
                    w7.append(t_bce)
                t_bc = ctx.add("tensor", lambda E: E.matmul(
                    psBC[:], onesrowf[0:1, 0:64], arow[:], start=True, stop=True),
                    waits=w7, inc="te")
                t_ar_use = t_bc
                t_bce = ctx.add("vector", lambda E: E.tensor_copy(bcsb[:], psBC[:]),
                                waits=[t_bc], inc="ve")
                w8 = [t_bce, t_ou]
                if t_attnT_use:
                    w8.append(t_attnT_use)
                t_avn = ctx.add("vector", lambda E, hp=hp, r0=r0: E.tensor_tensor(
                    attnT[r0:r0 + 64, hp * 512:(hp + 1) * 512], psO[0:64, :], bcsb[:],
                    ALU.mult), waits=w8, inc="ve")
            t_attnT_use = None
            for tsub in range(4):
                gt = s * 4 + tsub
                sl = gt % 2
                w = [t_x2b_use[sl]] if t_x2b_use[sl] else []
                t_ldr = ctx.dma("sync", x2b[:, sl * 1024:(sl + 1) * 1024],
                                X2D_R[:, gt, :], waits=w)
                t_res_done = None
                for och in range(2):
                    w2 = [t_avn, t_wo, t_cvh]
                    if t_y_ev:
                        w2.append(t_y_ev)
                    def ymm(E, tsub=tsub, och=och):
                        for c in range(NC):
                            E.matmul(psY[:],
                                     attnT[:, c * 512 + tsub * 128: c * 512 + tsub * 128 + 128],
                                     wo[:, c * 1024 + och * 512: c * 1024 + (och + 1) * 512],
                                     start=(c == 0), stop=False)
                        return E.matmul(psY[:], sb["c_onesrow"][0:1, :],
                                        cvh[0:1, och * 512:(och + 1) * 512], start=False, stop=True)
                    t_y = ctx.add("tensor", ymm, waits=w2, inc="te")
                    w3 = [t_y, t_ldr]
                    if t_rln_prev:
                        w3.append(t_rln_prev)
                    t_res = ctx.add("vector", lambda E, och=och, sl=sl: E.tensor_tensor(
                        rln[:, och * 512:(och + 1) * 512], psY[:],
                        x2b[:, sl * 1024 + och * 512: sl * 1024 + (och + 1) * 512],
                        ALU.add), waits=w3, inc="ve")
                    t_y_ev = t_res
                    t_res_done = t_res
                t_x2b_use[sl] = t_res_done
                t_attnT_use = t_res_done
                def lnf(E):
                    E.bn_stats(bnst[:, 0:6], rln[:, 0:512])
                    E.bn_stats(bnst[:, 6:12], rln[:, 512:1024])
                    E.drain()
                    E.bn_aggr(mv[:], bnst[:].rearrange("p (g k) -> p g k", k=6))
                    E.drain()
                    return E.nop()
                wbn = [t_res_done]
                if prev_ln_done:
                    wbn.append(prev_ln_done)
                t_bn = ctx.add("vector", lnf, waits=wbn, inc="ve")
                def rstdf(E):
                    E.activation(lnt[:, 0:1], mv[:, 1:2], AF.Ln, bias=1e-5)
                    return E.activation(lnt[:, 1:2], lnt[:, 0:1], AF.Exp, scale=-0.5)
                t_rs = ctx.add("scalar", rstdf, waits=[t_bn], inc="se")
                w4 = [t_rs]
                if t_x3f_use:
                    w4.append(t_x3f_use)
                t_apf = ctx.add("vector", lambda E: E.tensor_scalar(
                    x3f[:], rln[:], mv[:, 0:1], lnt[:, 1:2], ALU.subtract, ALU.mult),
                    waits=w4, inc="ve")
                prev_ln_done = t_apf
                t_rln_prev = t_apf
                w5 = [t_apf]
                if t_x3b_dma:
                    w5.append(t_x3b_dma)
                t_apb = ctx.add("vector", lambda E: E.tensor_copy(x3b[:], x3f[:]),
                                waits=w5, inc="ve")
                t_x3b_dma = ctx.dma("sync", X3D.rearrange("(t p) dd -> p t dd", p=128)[:, gt, :],
                                    x3b[:], waits=[t_apb])
                t_rt_prev_local = None
                for c in range(NC):
                    w6 = [t_apf, t_rw]
                    if t_rt_ev:
                        w6.append(t_rt_ev)
                    t_rt = ctx.add("tensor", lambda E, c=c: E.transpose(
                        psRT[:], x3f[:, c * 128:(c + 1) * 128], idf[:]), waits=w6, inc="te")
                    w7 = [t_rt]
                    if t_x3f_use:
                        w7.append(t_x3f_use)
                    t_rt_ev = ctx.add("vector", lambda E: E.tensor_copy(x3tf[:], psRT[:]),
                                      waits=[t_rt], inc="ve")
                    t_rl = ctx.add("tensor", lambda E, c=c, gt=gt: E.matmul(
                        psM2[:, 8:16], x3tf[:], rw[:, c * 8:(c + 1) * 8],
                        start=(c == 0), stop=(c == NC - 1)), waits=[t_rt_ev], inc="te")
                    t_rt_prev_local = t_rl
                t_x3f_use = t_rt_ev
                t_rl_ev_prev = ctx.add("vector", lambda E, gt=gt: E.tensor_copy(
                    logits[:, gt * 8:(gt + 1) * 8], psM2[:, 8:16]),
                    waits=[t_rt_prev_local], inc="ve")
    t_lg = ctx.dma("sync", LOGD.rearrange("(t p) e -> p t e", p=128),
                   logits[:].rearrange("p (t e) -> p t e", e=8), waits=[t_rl_ev_prev])
    return ctx.barrier_tok()


def build_passC_program():
    import contextlib
    nc = bass.Bass("TRN2", target_bir_lowering=False)
    d = {}
    d["x2d_in"] = nc.dram_tensor("x2d_in", [T, DM], BF16, kind="ExternalInput").ap()
    d["x2td_in"] = nc.dram_tensor("x2td_in", [DM, T], BF16, kind="ExternalInput").ap()
    d["l1_wqkv"] = nc.dram_tensor("l1_wqkv", [3 * DM, DM], F32, kind="ExternalInput").ap()
    d["l1_wout"] = nc.dram_tensor("l1_wout", [DM, DM], F32, kind="ExternalInput").ap()
    d["l1_bout"] = nc.dram_tensor("l1_bout", [DM], F32, kind="ExternalInput").ap()
    d["l1_ng"] = nc.dram_tensor("l1_ng", [1, 16, 1, 1], F32, kind="ExternalInput").ap()
    d["l1_nb"] = nc.dram_tensor("l1_nb", [1, 16, 1, 1], F32, kind="ExternalInput").ap()
    d["l1_rw"] = nc.dram_tensor("l1_rw", [DM, 8], F32, kind="ExternalInput").ap()
    for k, v in CONSTS.items():
        dt = BF16 if v.dtype == BF else (F32 if v.dtype == np.float32 else U32)
        d[k] = nc.dram_tensor(k, list(v.shape), dt, kind="ExternalInput").ap()
    d["x3d"] = nc.dram_tensor("x3d", [T, DM], BF16, kind="ExternalOutput").ap()
    d["logitsd"] = nc.dram_tensor("logitsd", [T, 8], F32, kind="ExternalOutput").ap()

    ctx = Ctx(nc)
    import contextlib as _cl
    with _cl.ExitStack() as stack:
        ctx.make_sems(stack)
        sb, ps = {}, {}
        alloc_common(nc, stack, sb)
        sb["c_idf"] = stack.enter_context(nc.sbuf_tensor("sb_c_idf", [128, 128], F32))
        sb["c_bc2f"] = stack.enter_context(nc.sbuf_tensor("sb_c_bc2f", [2, 128], F32))
        alloc_passC(nc, stack, sb)
        blk = stack.enter_context(nc.Block())
        load_common(ctx, d, sb)
        ctx.dma("sync", sb["c_idf"][:], d["c_id128f"])
        d["tok_bc2f"] = ctx.dma("sync", sb["c_bc2f"][:], d["c_bc2f"])
        emit_passC(ctx, d, sb, stack)
        fin = ctx.barrier_tok()
        ctx.add("sync", lambda E: None, waits=fin)
        for eng in Ctx.ENGS:
            opsl = ctx.q[eng]
            getattr(blk, eng)(lambda E, opsl=opsl: [f(E) for f in opsl])
    return nc



DEBUG_D = False
GATING_ONLY = False

def alloc_passD(nc, stack, sb):
    S = lambda name, shape, dt: sb.__setitem__(name, stack.enter_context(
        nc.sbuf_tensor("sb_" + name, shape, dt)))
    S("D_lg", [128, 32 * 8], F32)
    S("D_t0", [128, 32 * 8], F32)
    S("D_t1", [128, 32 * 8], F32)
    S("D_raw", [128, 32 * 8], F32)
    S("D_m1", [128, 32 * 8], F32)
    S("D_m2", [128, 32 * 8], F32)
    S("D_m1b", [128, 32 * 8], BF16)
    S("D_mx", [128, 32], F32)
    S("D_g1", [128, 32], F32)
    S("D_g2", [128, 32], F32)
    S("D_i1", [128, 32], F32)
    S("D_i2", [128, 32], F32)
    S("D_k1", [128, 32], F32)
    S("D_p1", [128, 32], F32)
    S("D_p2", [128, 32], F32)
    S("D_cnt", [1, 256], F32)
    S("D_cntT", [128, 2], BF16)
    S("D_cntT1c", [128, 2], BF16)
    S("D_offc", [128, 2], F32)
    S("D_offr", [1, 256], F32)
    S("D_offb", [128, 32 * 8], F32)
    S("D_px", [128, 32 * 8], F32)
    S("D_sl1", [128, 32], U32)
    S("D_sl2", [128, 32], U32)
    S("D_x3", [128, 2 * 1024], BF16)
    S("D_xe", [128, 1024], BF16)
    S("D_xeT", [128, 8 * 1024], BF16)
    S("D_w0", [128, 2 * 8 * 512], BF16)
    S("D_w1", [128, 2 * 8 * 512], BF16)
    S("D_gel", [128, 512], BF16)
    S("D_hid", [128, 2 * 512], BF16)
    S("D_wo", [128, 43 * 1024], BF16)
    S("D_hc", [128, 2 * 43 * 128], BF16)
    S("D_yb", [128, 2 * 512], BF16)
    S("D_A", [128, 1024], BF16)
    S("D_B", [128, 1024], BF16)
    S("D_mo", [128, 1024], F32)
    S("D_rln", [128, 1024], F32)
    S("D_bnst", [128, 12], F32)
    S("D_mv", [128, 2], F32)
    S("D_lnt", [128, 2], F32)
    S("D_out", [128, 2 * 1024], F32)
    S("D_dbg", [128, 8], F32)
    S("D_dbgr", [1, 512], F32)


def emit_passD(ctx, d, sb, stack):
    global GATING_ONLY
    nc = ctx.nc
    X3D, LOGD = d["x3d_in"], d["logitsd_in"]
    W0, W1, WO = d["l1_w0"], d["l1_w1"], d["l1_wo"]
    XD, YD, HIDD = d["Xd"], d["Yd"], d["Hidd"]
    OUT = d["out"]

    idb = sb["c_idb"]
    onescol = sb["c_onescol"]; onesrowf = sb["c_onesrowf"]
    u128 = sb["c_u128b"]; u8s = sb["c_u8sb"]; a8s = sb["c_a8sb"]
    iota8 = sb["c_iota8t"]; i8rev = sb["c_i8revt"]; idf = sb["c_idf"]
    lg = sb["D_lg"]; t0 = sb["D_t0"]; t1_ = sb["D_t1"]; raw = sb["D_raw"]
    m1 = sb["D_m1"]; m2 = sb["D_m2"]; m1b = sb["D_m1b"]
    mx = sb["D_mx"]; g1 = sb["D_g1"]; g2 = sb["D_g2"]
    i1 = sb["D_i1"]; i2 = sb["D_i2"]; k1 = sb["D_k1"]
    p1 = sb["D_p1"]; p2 = sb["D_p2"]
    cnt = sb["D_cnt"]; cntT = sb["D_cntT"]; offc = sb["D_offc"]; offr = sb["D_offr"]
    offb = sb["D_offb"]; px = sb["D_px"]
    sl1 = sb["D_sl1"]; sl2 = sb["D_sl2"]
    x3t = sb["D_x3"]; xe = sb["D_xe"]; xeT = sb["D_xeT"]
    w0b = sb["D_w0"]; w1b = sb["D_w1"]; gel = sb["D_gel"]; hid = sb["D_hid"]
    wor = sb["D_wo"]; hc = sb["D_hc"]; yb = sb["D_yb"]
    Ab = sb["D_A"]; Bb = sb["D_B"]; mo = sb["D_mo"]; rln = sb["D_rln"]
    bnst = sb["D_bnst"]; mv = sb["D_mv"]; lnt = sb["D_lnt"]; outb = sb["D_out"]

    X3R = X3D.rearrange("(t p) dd -> p t dd", p=128)
    V8 = lambda ap: ap.rearrange("p (t e) -> p t e", e=8)
    B32 = lambda col: col.unsqueeze(2).broadcast_to([128, 32, 8])

    t_lg = ctx.dma("sync", lg[:].rearrange("p (t e) -> p t e", e=8),
                   LOGD.rearrange("(t p) e -> p t e", p=128))
    BCREG = []
    ctx.add("gpsimd", lambda E: (BCREG.append(E.to_reg(8191)), None)[1], inc="ge")

    with (
        nc.psum_tensor("psG", [128, 256], F32) as psG,
        nc.psum_tensor("psGB", [128, 256], F32) as psGB,
        nc.psum_tensor("psCnt", [1, 256], F32) as psCnt,
        nc.psum_tensor("psOffs", [128, 2], F32) as psOffs,
        nc.psum_tensor("psTR", [128, 128], F32) as psTR,
    ):
        def soft(E):
            E.tensor_reduce(mx[:], V8(lg[:]), AX.X, ALU.max)
            E.drain()
            E.tensor_tensor(V8(t0[:]), V8(lg[:]), B32(mx[:]), ALU.subtract)
            return E.nop()
        t_s1 = ctx.add("vector", soft, waits=[t_lg], inc="ve")
        t_s2 = ctx.add("scalar", lambda E: E.activation(t1_[:], t0[:], AF.Exp),
                       waits=[t_s1], inc="se")
        def soft2(E):
            E.tensor_reduce(mx[:], V8(t1_[:]), AX.X, ALU.add)
            E.drain()
            E.reciprocal(mx[:], mx[:])
            E.drain()
            E.tensor_tensor(V8(raw[:]), V8(t1_[:]), B32(mx[:]), ALU.mult)
            E.drain()
            E.tensor_reduce(g1[:], V8(raw[:]), AX.X, ALU.max)
            E.drain()
            E.tensor_tensor(V8(m1[:]), V8(raw[:]), B32(g1[:]), ALU.is_equal)
            E.tensor_tensor(V8(t0[:]), V8(m1[:]), V8(i8rev[:]), ALU.mult)
            E.drain()
            E.tensor_reduce(i1[:], V8(t0[:]), AX.X, ALU.max)
            E.drain()
            E.tensor_scalar(i1[:], i1[:], -1.0, 7.0, ALU.mult, ALU.add)
            E.drain()
            E.tensor_tensor(V8(m1[:]), V8(iota8[:]), B32(i1[:]), ALU.is_equal)
            E.tensor_scalar(V8(t0[:]), V8(m1[:]), -1.0, 1.0, ALU.mult, ALU.add)
            E.tensor_tensor(V8(t1_[:]), V8(raw[:]), V8(t0[:]), ALU.mult)
            E.drain()
            E.tensor_reduce(g2[:], V8(t1_[:]), AX.X, ALU.max)
            E.drain()
            E.tensor_tensor(V8(m2[:]), V8(t1_[:]), B32(g2[:]), ALU.is_equal)
            E.tensor_tensor(V8(t0[:]), V8(m2[:]), V8(i8rev[:]), ALU.mult)
            E.drain()
            E.tensor_reduce(i2[:], V8(t0[:]), AX.X, ALU.max)
            E.drain()
            E.tensor_scalar(i2[:], i2[:], -1.0, 7.0, ALU.mult, ALU.add)
            E.drain()
            E.tensor_tensor(V8(m2[:]), V8(iota8[:]), B32(i2[:]), ALU.is_equal)
            E.drain()
            return E.tensor_copy(m1b[:], m1[:])
        t_s3 = ctx.add("vector", soft2, waits=[t_s2, d["tok_gconsts"]], inc="ve", fence=True)

        def cums(which, msrc, mbsrc, extra_c1=None, extra_tok=None, after=()):
            t_a = ctx.add("tensor", lambda E: E.matmul(
                psG[:], u128[:], mbsrc[:], start=True, stop=True),
                waits=[t_s3] + list(after), inc="te")
            t_b = ctx.add("tensor", lambda E: E.matmul(
                psCnt[:], onescol[:], mbsrc[:], start=True, stop=True),
                waits=[t_s3] + list(after), inc="te")
            t_c = ctx.add("vector", lambda E: E.tensor_copy(cnt[:], psCnt[:]),
                          waits=[t_b], inc="ve", fence=True)
            toks = []
            for j in range(2):
                t_t = ctx.add("tensor", lambda E, j=j: E.transpose(
                    psTR[:, 0:1], cnt[0:1, j * 128:(j + 1) * 128], idf[0:1, 0:1]),
                    waits=[t_c] + toks, inc="te")
                t_e = ctx.add("vector", lambda E, j=j: E.tensor_copy(
                    cntT[:, j:j + 1], psTR[:, 0:1]), waits=[t_t], inc="ve", fence=True)
                toks = [t_e]
            t_cT = toks[0]
            def offmm(E):
                E.matmul(psOffs[:], u8s[:], cntT[:], start=True, stop=True)
                E.matmul(psTR[:, 0:1], a8s[:], cntT[:, 0:1], start=True, stop=True)
                if extra_c1 is not None:
                    E.matmul(psTR[:, 1:2], a8s[:], extra_c1[:, 0:1], start=True, stop=True)
                    return E.matmul(psTR[:, 2:3], a8s[:], extra_c1[:, 1:2], start=True, stop=True)
                return None
            w = [t_cT]
            if extra_tok is not None:
                w.append(extra_tok)
            t_o = ctx.add("tensor", offmm, waits=w, inc="te")
            if extra_c1 is None and "D_dbg_cnt" in d and GATING_ONLY:
                ctx.add("vector", lambda E: E.tensor_copy(sb["D_dbg"][:, 4:6], psOffs[:]),
                        waits=[t_o], inc="ve")
                ctx.add("vector", lambda E: E.tensor_copy(sb["D_dbg"][:, 6:8], cntT[:]),
                        waits=[t_o], inc="ve")
            def offcomb(E):
                E.tensor_copy(offc[:], psOffs[:])
                E.drain()
                E.tensor_tensor(offc[:, 1:2], offc[:, 1:2], psTR[:, 0:1], ALU.add)
                E.drain()
                if extra_c1 is not None:
                    E.tensor_tensor(offc[:, 0:1], offc[:, 0:1], psTR[:, 1:2], ALU.add)
                    E.tensor_tensor(offc[:, 1:2], offc[:, 1:2], psTR[:, 1:2], ALU.add)
                    E.drain()
                    E.tensor_tensor(offc[:, 0:1], offc[:, 0:1], psTR[:, 2:3], ALU.add)
                    E.tensor_tensor(offc[:, 1:2], offc[:, 1:2], psTR[:, 2:3], ALU.add)
                    E.drain()
                return E.nop()
            t_oe = ctx.add("vector", offcomb, waits=[t_o], inc="ve", fence=True)
            if extra_c1 is None and "D_dbg_cnt" in d:
                ctx.add("vector", lambda E: E.tensor_copy(sb["D_dbg"][:, 0:2], cntT[:]),
                        waits=[t_cT], inc="ve")
                ctx.add("vector", lambda E: E.tensor_copy(sb["D_dbgr"][0:1, 0:256], cnt[:]),
                        waits=[t_cT], inc="ve")
                ctx.add("vector", lambda E: E.tensor_copy(sb["D_dbg"][:, 2:4], offc[:]),
                        waits=[t_oe], inc="ve")
            toks = []
            for j in range(2):
                t_t = ctx.add("tensor", lambda E, j=j: E.transpose(
                    psTR[0:1, :], offc[:, j:j + 1], idf[:]), waits=[t_oe] + toks, inc="te")
                t_e = ctx.add("vector", lambda E, j=j: E.tensor_copy(
                    offr[0:1, j * 128:(j + 1) * 128], psTR[0:1, :]), waits=[t_t], inc="ve", fence=True)
                toks = [t_e]
            if extra_c1 is None and "D_dbg_cnt" in d:
                ctx.add("vector", lambda E: E.tensor_copy(sb["D_dbgr"][0:1, 256:512], offr[:]),
                        waits=toks, inc="ve")
            t_bb = ctx.add("tensor", lambda E: E.matmul(
                psGB[:], onesrowf[0:1, :], offr[:], start=True, stop=True),
                waits=toks, inc="te")
            t_be = ctx.add("vector", lambda E: E.tensor_copy(offb[:, 0:256], psGB[:]),
                           waits=[t_bb], inc="ve", fence=True)
            t_px = ctx.add("vector", lambda E: E.tensor_tensor(
                px[:, 0:256], psG[:], offb[:, 0:256], ALU.add), waits=[t_a, t_be], inc="ve")
            return t_px

        t_px1 = cums("m1", m1, m1b)
        def cap1(E):
            E.tensor_scalar(t0[:, 0:256], px[:, 0:256], 1024.0, None, ALU.is_lt)
            E.drain()
            E.tensor_tensor(m1[:, 0:256], m1[:, 0:256], t0[:, 0:256], ALU.mult)
            E.drain()
            E.tensor_tensor(t0[:, 0:256], px[:, 0:256], m1[:, 0:256], ALU.mult)
            E.drain()
            E.tensor_reduce(p1[:], V8(t0[:]), AX.X, ALU.add)
            E.tensor_reduce(k1[:], V8(m1[:]), AX.X, ALU.add)
            E.drain()
            E.tensor_tensor(g1[:], g1[:], k1[:], ALU.mult)
            E.drain()
            return E.tensor_copy(m1b[:], m1[:])
        t_c1 = ctx.add("vector", cap1, waits=[t_px1], inc="ve", fence=True)
        t_cb = ctx.add("tensor", lambda E: E.matmul(
            psCnt[:], onescol[:], m1b[:], start=True, stop=True), waits=[t_c1], inc="te")
        t_cc = ctx.add("vector", lambda E: E.tensor_copy(cnt[:], psCnt[:]),
                       waits=[t_cb], inc="ve", fence=True)
        toks = []
        for j in range(2):
            t_t = ctx.add("tensor", lambda E, j=j: E.transpose(
                psTR[:, 0:1], cnt[0:1, j * 128:(j + 1) * 128], idf[0:1, 0:1]),
                waits=[t_cc] + toks, inc="te")
            t_e = ctx.add("vector", lambda E, j=j: E.tensor_copy(
                sb["D_cntT1c"][:, j:j + 1], psTR[:, 0:1]), waits=[t_t], inc="ve", fence=True)
            toks = [t_e]
        extra_c1_tok = toks
        def slot1f(E):
            E.tensor_scalar(t0[:, 0:32], i1[:], 1024.0, None, ALU.mult)
            E.drain()
            E.tensor_tensor(t0[:, 0:32], t0[:, 0:32], p1[:], ALU.add)
            E.tensor_scalar(t0[:, 32:64], k1[:], -1048576.0, 1048576.0, ALU.mult, ALU.add)
            E.drain()
            E.tensor_tensor(t0[:, 0:32], t0[:, 0:32], t0[:, 32:64], ALU.add)
            E.drain()
            return E.tensor_copy(sl1[:], t0[:, 0:32])
        t_sl1 = ctx.add("vector", slot1f, waits=[t_c1], inc="ve", fence=True)

        t_m2b = ctx.add("vector", lambda E: E.tensor_copy(m1b[:], m2[:]),
                        waits=[t_cb, t_sl1], inc="ve", fence=True)
        t_px2 = cums("m2", m2, m1b, extra_c1=sb["D_cntT1c"],
                     extra_tok=extra_c1_tok[0], after=[t_sl1, extra_c1_tok[0], t_m2b])
        def cap2(E):
            E.tensor_scalar(t0[:, 0:256], px[:, 0:256], 1024.0, None, ALU.is_lt)
            E.drain()
            E.tensor_tensor(m2[:, 0:256], m2[:, 0:256], t0[:, 0:256], ALU.mult)
            E.drain()
            E.tensor_tensor(t0[:, 0:256], px[:, 0:256], m2[:, 0:256], ALU.mult)
            E.drain()
            E.tensor_reduce(p2[:], V8(t0[:]), AX.X, ALU.add)
            E.tensor_reduce(k1[:], V8(m2[:]), AX.X, ALU.add)
            E.drain()
            E.tensor_tensor(g2[:], g2[:], k1[:], ALU.mult)
            E.drain()
            E.tensor_scalar(t0[:, 0:32], i2[:], 1024.0, None, ALU.mult)
            E.drain()
            E.tensor_tensor(t0[:, 0:32], t0[:, 0:32], p2[:], ALU.add)
            E.tensor_scalar(t0[:, 32:64], k1[:], -1048576.0, 1048576.0, ALU.mult, ALU.add)
            E.drain()
            E.tensor_tensor(t0[:, 0:32], t0[:, 0:32], t0[:, 32:64], ALU.add)
            E.drain()
            return E.tensor_copy(sl2[:], t0[:, 0:32])
        t_sl2 = ctx.add("vector", cap2, waits=[t_px2], inc="ve", fence=True)
        if GATING_ONLY:
            ctx.dma("sync", d["D_dbg_cnt"], sb["D_dbg"][:], waits=[t_sl2])
            ctx.dma("sync", d["D_dbg_rows"], sb["D_dbgr"][:], waits=[t_sl2])
            ctx.dma("sync", d["D_dbg_px"], px[:], waits=[t_px1])
            ctx.dma("sync", d["D_dbg_sl1"], sl1[:], waits=[t_sl1])
            ctx.dma("sync", d["D_dbg_sl2"], sl2[:], waits=[t_sl2])
            ctx.dma("sync", d["D_dbg_g1"], g1[:], waits=[t_sl2])
            ctx.dma("sync", d["D_dbg_g2"], g2[:], waits=[t_sl2])
            ctx.dma("sync", d["D_dbg_i1"], i1[:], waits=[t_sl2])
            ctx.dma("sync", d["D_dbg_p1"], p1[:], waits=[t_sl2])
            return ctx.barrier_tok()
        if "D_dbg_sl1" in d:
            ctx.dma("sync", d["D_dbg_cnt"], sb["D_dbg"][:], waits=[t_sl2])
            ctx.dma("sync", d["D_dbg_rows"], sb["D_dbgr"][:], waits=[t_sl2])
            ctx.dma("sync", d["D_dbg_px"], px[:], waits=[t_px1])
            ctx.dma("sync", d["D_dbg_sl1"], sl1[:], waits=[t_sl1])
            ctx.dma("sync", d["D_dbg_sl2"], sl2[:], waits=[t_sl2])
            ctx.dma("sync", d["D_dbg_g1"], g1[:], waits=[t_sl2])
            ctx.dma("sync", d["D_dbg_g2"], g2[:], waits=[t_sl2])
            ctx.dma("sync", d["D_dbg_i1"], i1[:], waits=[t_sl2])
            ctx.dma("sync", d["D_dbg_p1"], p1[:], waits=[t_sl2])

        t_x3_use = [None, None]
        t_scat = None
        for t in range(NT):
            sl = t % 2
            w = [t_x3_use[sl]] if t_x3_use[sl] else []
            t_ld = ctx.dma("sync", x3t[:, sl * 1024:(sl + 1) * 1024], X3R[:, t, :], waits=w)
            ctx.add("gpsimd", lambda E, t=t, sl=sl: E.indirect_dma_start(
                XD, IndirectOffsetOnAxis(sl1[:, t:t + 1], 0),
                x3t[:, sl * 1024:(sl + 1) * 1024], None,
                bounds_check=BCREG[0], oob_is_err=False),
                waits=[t_ld, t_sl1, t_sl2, d["tok_xzero"]], inc="dge", n=16)
            t_scat = ctx.add("gpsimd", lambda E, t=t, sl=sl: E.indirect_dma_start(
                XD, IndirectOffsetOnAxis(sl2[:, t:t + 1], 0),
                x3t[:, sl * 1024:(sl + 1) * 1024], None,
                bounds_check=BCREG[0], oob_is_err=False),
                waits=[], inc="dge", n=16)
            t_x3_use[sl] = t_scat
    t_gate_end = ctx.barrier_tok()

    NHC = 11
    hcw = [512] * 10 + [344]
    hboff = [0, 4, 8, 12, 16, 20, 24, 28, 32, 36, 40]
    hbn = [4] * 10 + [3]
    hbw_last = {10: [128, 128, 88]}
    with (
        nc.psum_tensor("psZ0a", [128, 512], F32) as psZ0a,
        nc.psum_tensor("psZ1a", [128, 512], F32) as psZ1a,
        nc.psum_tensor("psZ0b", [128, 512], F32) as psZ0b,
        nc.psum_tensor("psZ1b", [128, 512], F32) as psZ1b,
        nc.psum_tensor("psDa", [128, 512], F32) as psDa,
        nc.psum_tensor("psDb", [128, 512], F32) as psDb,
        nc.psum_tensor("psTx", [128, 128], BF16) as psTx,
        nc.psum_tensor("psTy", [128, 128], BF16) as psTy,
    ):
        t_wo_use = None
        t_hc_use = [None, None]
        t_xeT_use = None
        t_hid_use = [None, None]
        t_psT_ev = {0: None, 1: None}
        t_z_ev = {0: None, 1: None}
        t_hid_dma = None
        t_gel_use = None
        t_d_ev = {0: None, 1: None}
        t_yb_use = [None, None]
        t_w01c = {}
        t_w01c_use = {}
        t_wo_tok = None
        t_comb_gather = None

        for e in range(E1):
            t_xe_tr = None
            for ti in range(8):
                w = []
                if e == 0 and ti == 0:
                    w = list(t_gate_end)
                if t_xeT_use and ti == 0:
                    w.append(t_xeT_use)
                if t_xe_tr:
                    w.append(t_xe_tr)
                t_ldx = ctx.dma("sync", xe[:],
                                XD[e * 1024 + ti * 128: e * 1024 + (ti + 1) * 128, :],
                                waits=w + ([t_scat] if t_scat else []))
                for c in range(NC):
                    tp = c % 2
                    pst = psTx if tp == 0 else psTy
                    w2 = [t_ldx]
                    if t_psT_ev[tp]:
                        w2.append(t_psT_ev[tp])
                    t_tr = ctx.add("tensor", lambda E, c=c, pst=pst: E.transpose(
                        pst[:], xe[:, c * 128:(c + 1) * 128], idb[:]), waits=w2, inc="te")
                    t_ev = ctx.add("vector", lambda E, c=c, ti=ti, pst=pst: E.tensor_copy(
                        xeT[:, c * 1024 + ti * 128: c * 1024 + (ti + 1) * 128], pst[:]),
                        waits=[t_tr], inc="ve")
                    t_psT_ev[tp] = t_ev
                    t_xe_tr = t_ev
            w = [t_wo_use] if t_wo_use else []
            ctx.dma("gpsimd",
                    wor[:].rearrange("p (hb m) -> p hb m", m=1024)[:, 0:42, :],
                    WO[e][0:5376, :].rearrange("(hb p) m -> p hb m", p=128),
                    waits=w)
            t_wo_tok = ctx.dma("gpsimd", wor[0:88, 42 * 1024:43 * 1024],
                               WO[e][5376:5464, :], waits=[])
            for hci in range(NHC):
                sl = hci % 2
                wv = [t_w01c_use.get((sl,))] if t_w01c_use.get((sl,)) else []
                t_l0 = ctx.dma("gpsimd",
                               w0b[:].rearrange("p (s c h) -> p s c h", s=2, h=512)[:, sl, :, 0:hcw[hci]],
                               W0[e].rearrange("(c p) h -> p c h", p=128)[:, :, hci * 512: hci * 512 + hcw[hci]],
                               waits=wv)
                t_l1 = ctx.dma("gpsimd",
                               w1b[:].rearrange("p (s c h) -> p s c h", s=2, h=512)[:, sl, :, 0:hcw[hci]],
                               W1[e].rearrange("(c p) h -> p c h", p=128)[:, :, hci * 512: hci * 512 + hcw[hci]],
                               waits=wv)
                for hb in range(hbn[hci]):
                    hbw = 128 if (hci < 10 or hb < 2) else 88
                    for cc in range(2):
                        zp = (hb * 2 + cc) % 2
                        pz0 = psZ0a if zp == 0 else psZ0b
                        pz1 = psZ1a if zp == 0 else psZ1b
                        w3 = [t_l0, t_l1, t_xe_tr]
                        if t_z_ev[zp]:
                            w3.append(t_z_ev[zp])
                        def zmm(E, sl=sl, hb=hb, cc=cc, hbw=hbw, pz0=pz0, pz1=pz1):
                            for c in range(NC):
                                E.matmul(pz0[0:hbw, 0:512],
                                         w0b[:, (sl * 8 + c) * 512 + hb * 128:
                                             (sl * 8 + c) * 512 + hb * 128 + hbw],
                                         xeT[:, c * 1024 + cc * 512: c * 1024 + (cc + 1) * 512],
                                         start=(c == 0), stop=(c == NC - 1))
                            last = None
                            for c in range(NC):
                                last = E.matmul(pz1[0:hbw, 0:512],
                                                w1b[:, (sl * 8 + c) * 512 + hb * 128:
                                                    (sl * 8 + c) * 512 + hb * 128 + hbw],
                                                xeT[:, c * 1024 + cc * 512: c * 1024 + (cc + 1) * 512],
                                                start=(c == 0), stop=(c == NC - 1))
                            return last
                        t_z = ctx.add("tensor", zmm, waits=w3, inc="te")
                        w4 = [t_z]
                        if t_gel_use:
                            w4.append(t_gel_use)
                        t_g = ctx.add("scalar", lambda E, pz0=pz0, hbw=hbw: E.activation(
                            gel[0:hbw, :], pz0[0:hbw, 0:512], AF.Gelu_apprx_tanh),
                            waits=w4, inc="se")
                        hsl = (hb * 2 + cc) % 2
                        w5 = [t_g]
                        if t_hid_use[hsl]:
                            w5.append(t_hid_use[hsl])
                        t_h = ctx.add("vector", lambda E, pz1=pz1, hbw=hbw, hsl=hsl: E.tensor_tensor(
                            hid[0:hbw, hsl * 512: hsl * 512 + 512], gel[0:hbw, :],
                            pz1[0:hbw, 0:512], ALU.mult), waits=w5, inc="ve")
                        t_gel_use = t_h
                        t_z_ev[zp] = t_h
                        hbg = hboff[hci] + hb
                        w6 = [t_h]
                        if t_hid_dma:
                            w6.append(t_hid_dma)
                        t_hid_dma = ctx.dma("sync",
                                            HIDD[hbg * 128: hbg * 128 + hbw, cc * 512:(cc + 1) * 512],
                                            hid[0:hbw, hsl * 512: hsl * 512 + 512],
                                            waits=w6)
                        t_hid_use[hsl] = t_hid_dma
                t_w01c_use[(sl,)] = t_z
            for cb in range(8):
                csl = cb % 2
                w = [t_hc_use[csl]] if t_hc_use[csl] else []
                t_lhc = ctx.dma("sync",
                                hc[:].rearrange("p (s hb cc) -> p s hb cc", s=2, cc=128)[:, csl],
                                HIDD.rearrange("(hb p) c -> p hb c", p=128)[:, :, cb * 128:(cb + 1) * 128],
                                waits=w + [t_hid_dma])
                for mc in range(2):
                    dp = (cb * 2 + mc) % 2
                    pd = psDa if dp == 0 else psDb
                    w2 = [t_lhc, t_wo_tok]
                    if t_d_ev[dp]:
                        w2.append(t_d_ev[dp])
                    def dmm(E, csl=csl, mc=mc, pd=pd):
                        last = None
                        for hb in range(43):
                            hbw = 128 if hb < 42 else 88
                            last = E.matmul(pd[:],
                                            hc[0:hbw, (csl * 43 + hb) * 128:(csl * 43 + hb) * 128 + 128],
                                            wor[0:hbw, hb * 1024 + mc * 512: hb * 1024 + (mc + 1) * 512],
                                            start=(hb == 0), stop=(hb == 42))
                        return last
                    t_d = ctx.add("tensor", dmm, waits=w2, inc="te")
                    t_hc_use[csl] = t_d
                    ysl = (cb * 2 + mc) % 2
                    w3 = [t_d]
                    if t_yb_use[ysl]:
                        w3.append(t_yb_use[ysl])
                    t_ye = ctx.add("vector", lambda E, ysl=ysl, pd=pd: E.tensor_copy(
                        yb[:, ysl * 512:(ysl + 1) * 512], pd[:]), waits=w3, inc="ve")
                    t_d_ev[dp] = t_ye
                    t_yd = ctx.dma("sync",
                                   YD[e * 1024 + cb * 128: e * 1024 + (cb + 1) * 128,
                                      mc * 512:(mc + 1) * 512],
                                   yb[:, ysl * 512:(ysl + 1) * 512], waits=[t_ye])
                    t_yb_use[ysl] = t_yd
            t_xeT_use = t_z
            t_wo_use = t_d
        t_experts_done = ctx.barrier_tok()

    with nc.psum_tensor("psCb", [128, 128], F32) as _unused:
        t_AB_use = None
        prev_ln = None
        t_out_dma = [None, None]
        t_rln_prev = None
        t_x3_use2 = [None, None]
        t_memA = ctx.add("gpsimd", lambda E: E.memset(Ab[:], 0.0), inc="ge")
        t_memB = ctx.add("gpsimd", lambda E: E.memset(Bb[:], 0.0), inc="ge")
        for t in range(NT):
            sl = t % 2
            w = [t_x3_use2[sl]] if t_x3_use2[sl] else []
            t_ldr = ctx.dma("sync", x3t[:, sl * 1024:(sl + 1) * 1024], X3R[:, t, :], waits=w)
            w2 = list(t_experts_done) if t == 0 else []
            w2 += [t_memA, t_memB]
            if t_AB_use:
                w2.append(t_AB_use)
            ctx.add("gpsimd", lambda E, t=t: E.indirect_dma_start(
                Ab[:], None, YD, IndirectOffsetOnAxis(sl1[:, t:t + 1], 0),
                bounds_check=BCREG[0], oob_is_err=False), waits=w2, inc="dge", n=16)
            t_ga = ctx.add("gpsimd", lambda E, t=t: E.indirect_dma_start(
                Bb[:], None, YD, IndirectOffsetOnAxis(sl2[:, t:t + 1], 0),
                bounds_check=BCREG[0], oob_is_err=False), waits=[], inc="dge", n=16)
            def comb(E, t=t, sl=sl):
                E.tensor_scalar(mo[:], Ab[:], g1[:, t:t + 1], None, ALU.mult)
                E.drain()
                E.scalar_tensor_tensor(mo[:], Bb[:], g2[:, t:t + 1], mo[:], ALU.mult, ALU.add)
                E.drain()
                return E.tensor_tensor(rln[:], mo[:], x3t[:, sl * 1024:(sl + 1) * 1024], ALU.add)
            w3 = [t_ga, t_ldr]
            if t_rln_prev:
                w3.append(t_rln_prev)
            t_res = ctx.add("vector", comb, waits=w3, inc="ve")
            t_AB_use = t_res
            t_x3_use2[sl] = t_res
            def lnf(E):
                E.bn_stats(bnst[:, 0:6], rln[:, 0:512])
                E.bn_stats(bnst[:, 6:12], rln[:, 512:1024])
                E.drain()
                E.bn_aggr(mv[:], bnst[:].rearrange("p (g k) -> p g k", k=6))
                E.drain()
                return E.nop()
            wbn = [t_res]
            if prev_ln:
                wbn.append(prev_ln)
            t_bn = ctx.add("vector", lnf, waits=wbn, inc="ve")
            def rstdf(E):
                E.activation(lnt[:, 0:1], mv[:, 1:2], AF.Ln, bias=1e-5)
                return E.activation(lnt[:, 1:2], lnt[:, 0:1], AF.Exp, scale=-0.5)
            t_rs = ctx.add("scalar", rstdf, waits=[t_bn], inc="se")
            osl = t % 2
            w4 = [t_rs]
            if t_out_dma[osl]:
                w4.append(t_out_dma[osl])
            t_ap = ctx.add("vector", lambda E, osl=osl: E.tensor_scalar(
                outb[:, osl * 1024:(osl + 1) * 1024], rln[:], mv[:, 0:1], lnt[:, 1:2],
                ALU.subtract, ALU.mult), waits=w4, inc="ve")
            prev_ln = t_ap
            t_rln_prev = t_ap
            t_out_dma[osl] = ctx.dma("sync", OUT.rearrange("(t p) dd -> p t dd", p=128)[:, t, :],
                                     outb[:, osl * 1024:(osl + 1) * 1024], waits=[t_ap])
    return ctx.barrier_tok()


def build_passD_program():
    import contextlib as _cl
    nc = bass.Bass("TRN2", target_bir_lowering=False)
    d = {}
    d["x3d_in"] = nc.dram_tensor("x3d_in", [T, DM], BF16, kind="ExternalInput").ap()
    d["logitsd_in"] = nc.dram_tensor("logitsd_in", [T, 8], F32, kind="ExternalInput").ap()
    d["l1_w0"] = nc.dram_tensor("l1_w0", [E1, DM, HFF], F32, kind="ExternalInput").ap()
    d["l1_w1"] = nc.dram_tensor("l1_w1", [E1, DM, HFF], F32, kind="ExternalInput").ap()
    d["l1_wo"] = nc.dram_tensor("l1_wo", [E1, HFF, DM], F32, kind="ExternalInput").ap()
    for k, v in CONSTS.items():
        dt = BF16 if v.dtype == BF else (F32 if v.dtype == np.float32 else U32)
        d[k] = nc.dram_tensor(k, list(v.shape), dt, kind="ExternalInput").ap()
    d["Xd"] = nc.dram_tensor("Xd", [E1 * CAP, DM], BF16,
                             kind="ExternalOutput" if DEBUG_D else "Internal").ap()
    d["Yd"] = nc.dram_tensor("Yd", [E1 * CAP, DM], BF16,
                             kind="ExternalOutput" if DEBUG_D else "Internal").ap()
    d["Hidd"] = nc.dram_tensor("Hidd", [5504, DM], BF16, kind="Internal").ap()
    d["out"] = nc.dram_tensor("out", [T, DM], F32, kind="ExternalOutput").ap()
    if DEBUG_D:
        for nm, dt in (("D_dbg_sl1", U32), ("D_dbg_sl2", U32), ("D_dbg_g1", F32),
                       ("D_dbg_g2", F32), ("D_dbg_i1", F32), ("D_dbg_p1", F32)):
            d[nm] = nc.dram_tensor(nm, [128, 32], dt, kind="ExternalOutput").ap()
        d["D_dbg_cnt"] = nc.dram_tensor("D_dbg_cnt", [128, 8], F32, kind="ExternalOutput").ap()
        d["D_dbg_rows"] = nc.dram_tensor("D_dbg_rows", [1, 512], F32, kind="ExternalOutput").ap()
        d["D_dbg_px"] = nc.dram_tensor("D_dbg_px", [128, 256], F32, kind="ExternalOutput").ap()
        d["D_dbg_u8s"] = nc.dram_tensor("D_dbg_u8s", [128, 128], F32, kind="ExternalOutput").ap()

    ctx = Ctx(nc)
    with _cl.ExitStack() as stack:
        ctx.make_sems(stack)
        sb, ps = {}, {}
        alloc_common(nc, stack, sb)
        sb["c_idf"] = stack.enter_context(nc.sbuf_tensor("sb_c_idf", [128, 128], F32))
        for nm, shape, dt in (("c_u128b", [128, 128], BF16), ("c_u8sb", [128, 128], BF16),
                              ("c_a8sb", [128, 128], BF16), ("c_iota8t", [128, 256], F32),
                              ("c_i8revt", [128, 256], F32)):
            sb[nm] = stack.enter_context(nc.sbuf_tensor("sb_" + nm, shape, dt))
        alloc_passD(nc, stack, sb)
        blk = stack.enter_context(nc.Block())
        load_common(ctx, d, sb)
        ctx.dma("sync", sb["c_idf"][:], d["c_id128f"])
        toks = []
        for nm in ("c_u128b", "c_u8sb", "c_a8sb", "c_iota8t", "c_i8revt"):
            toks.append(ctx.dma("sync", sb[nm][:], d[nm]))
        d["tok_gconsts"] = toks[-1]
        t_z0 = ctx.add("gpsimd", lambda E: E.memset(sb["D_xe"][:], 0.0), inc="ge")
        t_xz = None
        for r in range(64):
            t_xz = ctx.dma("sync", d["Xd"][r * 128:(r + 1) * 128, :], sb["D_xe"][:],
                           waits=[t_z0])
        d["tok_xzero"] = t_xz
        emit_passD(ctx, d, sb, stack)
        fin = ctx.barrier_tok()
        ctx.add("sync", lambda E: None, waits=fin)
        for eng in Ctx.ENGS:
            opsl = ctx.q[eng]
            getattr(blk, eng)(lambda E, opsl=opsl: [f(E) for f in opsl])
    return nc


# version 02
# speedup vs baseline: 1.0074x; 1.0074x over previous
import math
import numpy as np
import ml_dtypes
import concourse.bass as bass
import concourse.mybir as mybir
from concourse.bass_utils import run_bass_kernel_spmd
from concourse.bass_core_rust import IndirectOffsetOnAxis

F32 = mybir.dt.float32
BF16 = mybir.dt.bfloat16
U32 = mybir.dt.uint32
AF = mybir.ActivationFunctionType
ALU = mybir.AluOpType
AX = mybir.AxisListType
BF = ml_dtypes.bfloat16

T = 4096
DM = 1024
NT = T // 128
NC = DM // 128
NH = 16
HD = 64
WIN = 512
E0 = 8
E1 = 8
HFF = 5464
CAP = 1024
NHB = (HFF + 127) // 128
LASTHB = HFF - 128 * (NHB - 1)



def make_consts():
    c = {}
    c["c_id128b"] = np.eye(128, dtype=BF)
    c["c_id128f"] = np.eye(128, dtype=np.float32)
    c["c_ones_col"] = np.ones((128, 1), dtype=BF)
    c["c_ones_row"] = np.ones((1, 128), dtype=BF)
    c["c_ones_rowf"] = np.ones((1, 128), dtype=np.float32)
    iu = np.arange(128)
    c["c_u128"] = (iu[:, None] < iu[None, :]).astype(BF)
    cc, pp = np.meshgrid(iu, iu, indexing="ij")
    c["c_u8s"] = (((cc % 8) == (pp % 8)) & (cc < pp)).astype(np.float32)
    c["c_a8s"] = ((cc % 8) == (pp % 8)).astype(np.float32)
    c["c_iota8"] = np.tile(np.arange(8, dtype=np.float32), (128, 1))
    c["c_iota8t"] = np.tile(np.arange(8, dtype=np.float32), (128, 32))
    c["c_i8revt"] = np.tile(7.0 - np.arange(8, dtype=np.float32), (128, 32))
    c["c_u128b"] = (iu[:, None] < iu[None, :]).astype(BF)
    c["c_u8sb"] = (((cc % 8) == (pp % 8)) & (cc < pp)).astype(BF)
    c["c_a8sb"] = ((cc % 8) == (pp % 8)).astype(BF)
    p = np.arange(128)[:, None]
    f = np.arange(256)[None, :]
    m0 = (f < p)
    m1 = (f < p + 128)
    m4 = (p <= f)
    m5 = (p + 128 <= f)
    c["c_amask"] = np.concatenate(
        [m.astype(BF) for m in (m0, m1, m4, m5)], axis=1)
    half = HD // 2
    theta = (1.0 / 10000.0 ** (np.arange(half, dtype=np.float64) / half))
    pos = np.arange(T, dtype=np.float64)
    fr = pos[:, None] * theta[None, :]
    fr2 = np.concatenate([fr, fr], axis=1)
    c["c_rcos"] = np.cos(fr2).astype(BF)
    c["c_rsin"] = np.sin(fr2).astype(BF)
    iu2 = np.arange(128)
    c["c_bc2f"] = np.stack([(iu2 < 64), (iu2 >= 64)]).astype(np.float32)
    c["c_fvals"] = np.tile(np.array([[1e-5, 1e-12, 0.0, 0.0]], dtype=np.float32), (128, 1))
    return c


CONSTS = make_consts()



class Ctx:

    ENGS = ("tensor", "vector", "scalar", "gpsimd", "sync")

    def __init__(self, nc):
        self.nc = nc
        self.q = {e: [] for e in self.ENGS}
        self.sems = {}
        self.cnt = {}
        self._sem_cms = []

    def make_sems(self, stack):
        for name in ("te", "ve", "se", "ge", "dsy", "dge", "dse", "dve"):
            self.sems[name] = stack.enter_context(self.nc.semaphore(name))
            self.cnt[name] = 0

    def add(self, eng, body, waits=(), inc=None, n=1, fence=False):
        waits = tuple(waits)
        tok = None
        if inc is not None:
            self.cnt[inc] += n
            tok = (inc, self.cnt[inc])
        sems = self.sems
        def closure(E, waits=waits, inc=inc, n=n, body=body, fence=fence):
            for (chn, v) in waits:
                E.wait_ge(sems[chn], v)
            last = body(E)
            if inc is not None:
                if fence:
                    E.drain()
                    last = E.nop()
                if last is None:
                    last = E.nop()
                last.then_inc(sems[inc], n)
        self.q[eng].append(closure)
        return tok

    def dma(self, eng, out, in_, waits=(), **kw):
        ch = {"sync": "dsy", "gpsimd": "dge", "scalar": "dse", "vector": "dve"}[eng]
        return self.add(eng, lambda E: E.dma_start(out=out, in_=in_, **kw),
                        waits=waits, inc=ch, n=16)

    def barrier_tok(self):
        return [(chn, v) for chn, v in self.cnt.items() if v > 0]


def ts(i, n):
    return slice(i * n, (i + 1) * n)



DEBUG_A = False

def emit_passA(ctx, d, sb, ps):
    nc = ctx.nc
    X, WQKV, WOUT = d["x"], d["l0_wqkv"], d["l0_wout"]
    X1D, X1TD = d["x1d"], d["x1td"]

    wq = sb["A_wq"]
    wo = sb["A_wo"]
    idb = sb["c_idb"]
    onescol = sb["c_onescol"]
    onesrow = sb["c_onesrow"]
    onesrowf = sb["c_onesrowf"]
    amask = sb["A_amask"]
    rcos = sb["A_rcos"]
    rsin = sb["A_rsin"]
    xf = sb["A_xf"]
    xbf = sb["A_xbf"]
    xT = sb["A_xT"]
    qkb = sb["A_qkb"]
    ropet = sb["A_ropet"]
    qT = sb["A_qT"]
    kT = sb["A_kT"]
    vr = sb["A_vr"]
    pt = sb["A_pt"]
    densb = sb["A_densb"]
    recb = sb["A_recb"]
    bcsb = sb["A_bcsb"]
    attnT = sb["A_attnT"]
    rln = sb["A_rln"]
    bnst = sb["A_bnst"]
    mv = sb["A_mv"]
    lnt = sb["A_lnt"]
    x1b = sb["A_x1b"]
    x1t = sb["A_x1t"]

    psA, psB = ps["psA"], ps["psB"]
    psTa, psTb = ps["psTa"], ps["psTb"]
    psS1 = ps["psS1"]
    psS2 = ps["psS2"]
    psAV = ps["psAV"]

    t_w = ctx.dma("gpsimd",
                  wq[:].rearrange("p (c o) -> p c o", o=3072),
                  WQKV.rearrange("(c p) o -> p c o", p=128))
    t_wo = ctx.dma("gpsimd",
                   wo[:].rearrange("p (c o) -> p c o", o=1024),
                   WOUT.rearrange("(c p) o -> p c o", p=128))
    t_ct = []
    t_ct.append(ctx.dma("sync", amask[:], d["c_amask"]))
    t_ct.append(ctx.dma("sync",
                        rcos[:].rearrange("p (t dd) -> p t dd", dd=64),
                        d["c_rcos"].rearrange("(t p) dd -> p t dd", p=128)))
    t_ct.append(ctx.dma("sync",
                        rsin[:].rearrange("p (t dd) -> p t dd", dd=64),
                        d["c_rsin"].rearrange("(t p) dd -> p t dd", p=128)))

    last_qkv_psum_read = {0: None, 1: None}
    last_xT_use = None
    last_qT_use = None
    last_kslot_use = {0: None, 1: None}
    last_vslot_use = {0: None, 1: None}
    last_xf_use = {0: None, 1: None}
    last_psT_evac = {0: None, 1: None}
    last_attnT_use = None
    prev_ln_done = None
    t_exp = None
    t_avn = None
    t_x1t_dma = None
    t_x1b_dma = None
    t_dav_prev = None
    t_de_prev = None
    t_rc_prev = None
    t_bc_prev = None
    t_bce_prev = None

    XR = X.rearrange("(t p) dd -> p t dd", p=128)

    for s in range(8):
        sl = s % 2
        w = [t for t in (last_xf_use[sl],) if t]
        t_x = ctx.dma("sync", xf[:].rearrange("p (a t dd) -> p a t dd", a=2, dd=1024)[:, sl],
                      XR[:, s * 4:(s + 1) * 4, :], waits=w)

        w = [t_x]
        if last_xT_use:
            w.append(last_xT_use)
        t_cast = ctx.add("vector", lambda E, s=s, sl=sl: E.tensor_copy(
            xbf[:], xf[:, sl * 4096:(sl + 1) * 4096]), waits=w, inc="ve")

        t_ev = None
        for ti in range(4):
            for c in range(NC):
                pp = (ti * NC + c) % 2
                pst = psTa if pp == 0 else psTb
                w = [t_cast]
                if last_psT_evac[pp]:
                    w.append(last_psT_evac[pp])
                t_tr = ctx.add("tensor", lambda E, ti=ti, c=c, pst=pst: E.transpose(
                    pst[:], xbf[:, ti * 1024 + c * 128: ti * 1024 + (c + 1) * 128], idb[:]),
                    waits=w, inc="te")
                t_ev = ctx.add("vector", lambda E, ti=ti, c=c, pst=pst: E.tensor_copy(
                    xT[:, c * 512 + ti * 128: c * 512 + (ti + 1) * 128], pst[:]),
                    waits=[t_tr], inc="ve")
                last_psT_evac[pp] = t_ev
        t_xT = t_ev

        t_rope_done = None
        for oc in range(6):
            for ti in range(4):
                pp = (oc * 4 + ti) % 2
                pq = psA if pp == 0 else psB
                w = [t_xT, t_w]
                if last_qkv_psum_read[pp]:
                    w.append(last_qkv_psum_read[pp])
                def qkvmm(E, oc=oc, ti=ti, pq=pq):
                    last = None
                    for c in range(NC):
                        last = E.matmul(
                            pq[:],
                            xT[:, c * 512 + ti * 128: c * 512 + (ti + 1) * 128],
                            wq[:, c * 3072 + oc * 512: c * 3072 + (oc + 1) * 512],
                            start=(c == 0), stop=(c == NC - 1))
                    return last
                t_mm = ctx.add("tensor", qkvmm, waits=w, inc="te")

                if oc < 4:
                    gt = s * 4 + ti
                    qk = qkb[:, (pp) * 512:(pp + 1) * 512]
                    def rope(E, pq=pq, gt=gt, qk=qk):
                        pA = pq[:].rearrange("p (h dd) -> p h dd", dd=64)[:, :, 0:32]
                        pB = pq[:].rearrange("p (h dd) -> p h dd", dd=64)[:, :, 32:64]
                        cosA = rcos[:, gt * 64: gt * 64 + 32].unsqueeze(1).broadcast_to([128, 8, 32])
                        cosB = rcos[:, gt * 64 + 32: gt * 64 + 64].unsqueeze(1).broadcast_to([128, 8, 32])
                        sinA = rsin[:, gt * 64: gt * 64 + 32].unsqueeze(1).broadcast_to([128, 8, 32])
                        sinB = rsin[:, gt * 64 + 32: gt * 64 + 64].unsqueeze(1).broadcast_to([128, 8, 32])
                        r3 = lambda t: t.rearrange("p (h dd) -> p h dd", dd=32)
                        E.tensor_tensor(r3(ropet[:, 0:256]), pA, cosA, ALU.mult)
                        E.tensor_tensor(r3(ropet[:, 512:768]), pB, sinA, ALU.mult)
                        E.tensor_tensor(r3(ropet[:, 256:512]), pB, cosB, ALU.mult)
                        E.tensor_tensor(r3(ropet[:, 768:1024]), pA, sinB, ALU.mult)
                        qk3 = qk.rearrange("p (h dd) -> p h dd", dd=64)
                        E.tensor_tensor(qk3[:, :, 0:32], r3(ropet[:, 0:256]),
                                        r3(ropet[:, 512:768]), ALU.subtract)
                        return E.tensor_tensor(qk3[:, :, 32:64], r3(ropet[:, 256:512]),
                                               r3(ropet[:, 768:1024]), ALU.add)
                    t_rope = ctx.add("vector", rope, waits=[t_mm], inc="ve")
                    last_qkv_psum_read[pp] = t_rope
                    for j in range(4):
                        tp = j % 2
                        pst = psTa if tp == 0 else psTb
                        w2 = [t_rope]
                        if last_psT_evac[tp]:
                            w2.append(last_psT_evac[tp])
                        t_tr = ctx.add("tensor", lambda E, qk=qk, j=j, pst=pst: E.transpose(
                            pst[:], qk[:, j * 128:(j + 1) * 128], idb[:]), waits=w2, inc="te")
                        cq = (oc % 2) * 4 + j
                        if oc < 2:
                            dst = qT[:, cq * 512 + ti * 128: cq * 512 + (ti + 1) * 128]
                        else:
                            dst = kT[:, (sl * 8 + cq) * 512 + ti * 128: (sl * 8 + cq) * 512 + (ti + 1) * 128]
                        t_ev = ctx.add("vector", lambda E, dst=dst, pst=pst: E.tensor_copy(
                            dst, pst[:]), waits=[t_tr], inc="ve")
                        last_psT_evac[tp] = t_ev
                        t_rope_done = t_ev
                else:
                    dst = vr[:, (sl * 4 + ti) * 1024 + (oc - 4) * 512:
                             (sl * 4 + ti) * 1024 + (oc - 3) * 512]
                    t_ev = ctx.add("vector", lambda E, dst=dst, pq=pq: E.tensor_copy(
                        dst, pq[:]), waits=[t_mm], inc="ve")
                    last_qkv_psum_read[pp] = t_ev
                    t_rope_done = t_ev
        t_qkv = t_rope_done
        if DEBUG_A and s == 1:
            ctx.dma("sync", d["D_qT"], qT[:], waits=[t_qkv])
            ctx.dma("sync", d["D_kT"], kT[:, 8 * 512:16 * 512], waits=[t_qkv])
            ctx.dma("sync", d["D_vr"], vr[:, 4 * 1024:8 * 1024], waits=[t_qkv])

        for qi in range(2):
            qt = s * 2 + qi
            valid = [m for m in range(6) if qt * 2 - 4 + m >= 0]
            for h in range(NH):
                hh, ck = (h % 2) * 64, h // 2
                w = [t_qkv]
                if t_exp:
                    w.append(t_exp)
                if t_de_prev:
                    w.append(t_de_prev)
                if t_bce_prev:
                    w.append(t_bce_prev)
                def scoresmm(E, qi=qi, qt=qt, h=h, hh=hh, ck=ck, valid=tuple(valid)):
                    last = None
                    for m in valid:
                        jb = qt * 2 - 4 + m
                        slj = (jb // 4) % 2
                        off = (jb % 4) * 128
                        lhsT = kT[hh:hh + 64,
                                  (slj * 8 + ck) * 512 + off: (slj * 8 + ck) * 512 + off + 128]
                        rhs = qT[hh:hh + 64, ck * 512 + qi * 256: ck * 512 + (qi + 1) * 256]
                        if m < 4:
                            out = psS1[:, m * 256:(m + 1) * 256]
                        else:
                            out = psS2[:, (m - 4) * 256:(m - 3) * 256]
                        last = E.matmul(out, lhsT, rhs, start=True, stop=True)
                    return last
                t_sc = ctx.add("tensor", scoresmm, waits=w, inc="te")

                w = [t_sc]
                if t_dav_prev:
                    w.append(t_dav_prev)
                def expf(E, valid=tuple(valid)):
                    v1 = [m for m in valid if m < 4]
                    last = None
                    if v1:
                        a, b = min(v1), max(v1) + 1
                        last = E.activation(pt[:, a * 256:b * 256], psS1[:, a * 256:b * 256],
                                            AF.Exp, scale=0.125)
                    last = E.activation(pt[:, 1024:1536], psS2[:], AF.Exp, scale=0.125)
                    return last
                t_exp = ctx.add("scalar", expf, waits=w, inc="se")

                mask_of = {0: 0, 1: 1, 4: 2, 5: 3}
                def maskf(E, valid=tuple(valid)):
                    last = None
                    for m in valid:
                        if m in mask_of:
                            last = E.tensor_tensor(
                                pt[:, m * 256:(m + 1) * 256], pt[:, m * 256:(m + 1) * 256],
                                amask[:, mask_of[m] * 256:(mask_of[m] + 1) * 256], ALU.mult)
                    return last
                t_msk = ctx.add("vector", maskf, waits=[t_exp], inc="ve")

                w = [t_msk]
                if t_avn:
                    w.append(t_avn)
                def denav(E, s=s, qt=qt, h=h, valid=tuple(valid)):
                    for i, m in enumerate(valid):
                        E.matmul(psS2[0:1, 256:512], onescol[:], pt[:, m * 256:(m + 1) * 256],
                                 start=(i == 0), stop=(i == len(valid) - 1))
                    last = None
                    for i, m in enumerate(valid):
                        jb = qt * 2 - 4 + m
                        slj = (jb // 4) % 2
                        lhsT = vr[:, (slj * 4 + jb % 4) * 1024 + h * 64:
                                  (slj * 4 + jb % 4) * 1024 + (h + 1) * 64]
                        last = E.matmul(psAV[0:64, :],
                                        lhsT, pt[:, m * 256:(m + 1) * 256],
                                        start=(i == 0), stop=(i == len(valid) - 1))
                    return last
                t_dav = ctx.add("tensor", denav, waits=w, inc="te")
                t_dav_prev = t_dav
                if DEBUG_A and qt == 2 and h == 0:
                    ctx.dma("sync", d["D_pt"], pt[:], waits=[t_msk])

                w = [t_dav]
                if t_rc_prev:
                    w.append(t_rc_prev)
                t_de = ctx.add("scalar", lambda E: E.copy(densb[:], psS2[0:1, 256:512]),
                               waits=w, inc="se")
                t_de_prev = t_de
                if DEBUG_A and qt == 2 and h == 0:
                    ctx.dma("sync", d["D_den"], densb[:], waits=[t_de])
                w = [t_de]
                if t_bc_prev:
                    w.append(t_bc_prev)
                t_rc = ctx.add("vector", lambda E: E.reciprocal(recb[:], densb[:]),
                               waits=w, inc="ve")
                t_rc_prev = t_rc
                t_bc = ctx.add("tensor", lambda E: E.matmul(
                    psS1[0:64, 0:256], onesrowf[0:1, 0:64], recb[:],
                    start=True, stop=True), waits=[t_rc], inc="te")
                t_bc_prev = t_bc
                t_bce = ctx.add("vector", lambda E: E.tensor_copy(bcsb[:], psS1[0:64, 0:256]),
                                waits=[t_bc], inc="ve")
                t_bce_prev = t_bce
                w = [t_bce]
                if last_attnT_use:
                    w.append(last_attnT_use)
                t_avn = ctx.add("vector", lambda E, h=h, ck=ck, hh=hh: E.tensor_tensor(
                    attnT[hh:hh + 64, ck * 256:(ck + 1) * 256], psAV[0:64, :], bcsb[:], ALU.mult),
                    waits=w, inc="ve")

            if DEBUG_A and qt == 2:
                ctx.dma("sync", d["D_attnT"], attnT[:], waits=[t_avn])
            last_attnT_use = None
            for qsub in range(2):
                gt = qt * 2 + qsub
                for och in range(2):
                    pp = och % 2
                    pq = psA if pp == 0 else psB
                    w = [t_avn, t_wo]
                    if last_qkv_psum_read[pp]:
                        w.append(last_qkv_psum_read[pp])
                    def opmm(E, qsub=qsub, och=och, pq=pq):
                        last = None
                        for c in range(NC):
                            last = E.matmul(
                                pq[:], attnT[:, c * 256 + qsub * 128: c * 256 + qsub * 128 + 128],
                                wo[:, c * 1024 + och * 512: c * 1024 + (och + 1) * 512],
                                start=(c == 0), stop=(c == NC - 1))
                        return last
                    t_op = ctx.add("tensor", opmm, waits=w, inc="te")
                    w2 = [t_op, t_x]
                    if prev_ln_done:
                        w2.append(prev_ln_done)
                    t_res = ctx.add("vector", lambda E, sl=sl, qi=qi, qsub=qsub, och=och, pq=pq:
                                    E.tensor_tensor(
                                        rln[:, och * 512:(och + 1) * 512], pq[:],
                                        xf[:, sl * 4096 + (qi * 2 + qsub) * 1024 + och * 512:
                                           sl * 4096 + (qi * 2 + qsub) * 1024 + (och + 1) * 512],
                                        ALU.add), waits=w2, inc="ve")
                    last_qkv_psum_read[pp] = t_res
                    if och == 1:
                        t_res_done = t_res
                last_attnT_use = t_res_done
                last_xf_use[sl] = t_res_done
                if DEBUG_A and gt == 4:
                    ctx.dma("sync", d["D_rln"], rln[:], waits=[t_res_done])
                def lnf(E):
                    E.bn_stats(bnst[:, 0:6], rln[:, 0:512])
                    E.bn_stats(bnst[:, 6:12], rln[:, 512:1024])
                    E.drain()
                    E.bn_aggr(mv[:], bnst[:].rearrange("p (g k) -> p g k", k=6))
                    E.drain()
                    return E.nop()
                wbn = [t_res_done]
                if prev_ln_done:
                    wbn.append(prev_ln_done)
                t_bn = ctx.add("vector", lnf, waits=wbn, inc="ve")
                def rstdf(E):
                    E.activation(lnt[:, 0:1], mv[:, 1:2], AF.Ln, bias=1e-5)
                    return E.activation(lnt[:, 1:2], lnt[:, 0:1], AF.Exp, scale=-0.5)
                t_rs = ctx.add("scalar", rstdf, waits=[t_bn], inc="se")
                w3 = [t_rs]
                if t_x1b_dma:
                    w3.append(t_x1b_dma)
                t_ap = ctx.add("vector", lambda E: E.tensor_scalar(
                    x1b[:], rln[:], mv[:, 0:1], lnt[:, 1:2], ALU.subtract, ALU.mult),
                    waits=w3, inc="ve")
                prev_ln_done = t_ap
                if DEBUG_A and gt == 4:
                    ctx.dma("sync", d["D_mv"], mv[:], waits=[t_ap])
                    ctx.dma("sync", d["D_lnt"], lnt[:], waits=[t_ap])
                    ctx.dma("sync", d["D_x1b"], x1b[:], waits=[t_ap])
                t_x1b_dma = ctx.dma("sync", X1D.rearrange("(t p) dd -> p t dd", p=128)[:, gt, :],
                                    x1b[:], waits=[t_ap])
                t_evl = None
                for c in range(NC):
                    tp = c % 2
                    pst = psTa if tp == 0 else psTb
                    w4 = [t_ap]
                    if last_psT_evac[tp]:
                        w4.append(last_psT_evac[tp])
                    t_tr = ctx.add("tensor", lambda E, c=c, pst=pst: E.transpose(
                        pst[:], x1b[:, c * 128:(c + 1) * 128], idb[:]), waits=w4, inc="te")
                    w5 = [t_tr]
                    if t_x1t_dma:
                        w5.append(t_x1t_dma)
                    t_evl = ctx.add("vector", lambda E, c=c, pst=pst: E.tensor_copy(
                        x1t[:, c * 128:(c + 1) * 128], pst[:]), waits=w5, inc="ve")
                    last_psT_evac[tp] = t_evl
                t_x1t_dma = ctx.dma("sync",
                                    X1TD.rearrange("(c p) tt -> p c tt", p=128)[:, :, gt * 128:(gt + 1) * 128],
                                    x1t[:].rearrange("p (c tt) -> p c tt", tt=128),
                                    waits=[t_evl])
        last_xT_use = t_qkv
        last_qT_use = t_qkv
        last_kslot_use[sl] = None
        last_vslot_use[sl] = None

    return ctx.barrier_tok()



def alloc_passA(nc, stack, sb, ps):
    S = lambda name, shape, dt: sb.__setitem__(name, stack.enter_context(
        nc.sbuf_tensor("sb_" + name, shape, dt)))
    S("A_wq", [128, 8 * 3072], BF16)
    S("A_wo", [128, 8 * 1024], BF16)
    S("A_amask", [128, 1024], BF16)
    S("A_rcos", [128, 32 * 64], BF16)
    S("A_rsin", [128, 32 * 64], BF16)
    S("A_xf", [128, 2 * 4096], F32)
    S("A_xbf", [128, 4096], BF16)
    S("A_xT", [128, 8 * 512], BF16)
    S("A_qkb", [128, 2 * 512], BF16)
    S("A_ropet", [128, 4 * 256], F32)
    S("A_qT", [128, 8 * 512], BF16)
    S("A_kT", [128, 2 * 8 * 512], BF16)
    S("A_vr", [128, 2 * 4 * 1024], BF16)
    S("A_pt", [128, 1536], BF16)
    S("A_densb", [1, 256], F32)
    S("A_recb", [1, 256], F32)
    S("A_bcsb", [64, 256], BF16)
    S("A_attnT", [128, 8 * 256], BF16)
    S("A_rln", [128, 1024], F32)
    S("A_bnst", [128, 12], F32)
    S("A_mv", [128, 2], F32)
    S("A_lnt", [128, 2], F32)
    S("A_x1b", [128, 1024], BF16)
    S("A_x1t", [128, 8 * 128], BF16)


def alloc_common(nc, stack, sb):
    S = lambda name, shape, dt: sb.__setitem__(name, stack.enter_context(
        nc.sbuf_tensor("sb_" + name, shape, dt)))
    S("c_idb", [128, 128], BF16)
    S("c_fvals", [128, 4], F32)
    nc.const_aps.register(F32, 1e-5, sb["c_fvals"][:, 0:1])
    nc.const_aps.register(F32, 1e-12, sb["c_fvals"][:, 1:2])
    S("c_onescol", [128, 1], BF16)
    S("c_onesrow", [1, 128], BF16)
    S("c_onesrowf", [1, 128], F32)


def load_common(ctx, d, sb):
    toks = []
    toks.append(ctx.dma("sync", sb["c_idb"][:], d["c_id128b"]))
    toks.append(ctx.dma("sync", sb["c_fvals"][:], d["c_fvals"]))
    toks.append(ctx.dma("sync", sb["c_onescol"][:], d["c_ones_col"]))
    toks.append(ctx.dma("sync", sb["c_onesrow"][:], d["c_ones_row"]))
    toks.append(ctx.dma("sync", sb["c_onesrowf"][:], d["c_ones_rowf"]))
    return toks


def alloc_psum_A(nc, stack, ps):
    P = lambda name, shape, dt: ps.__setitem__(name, stack.enter_context(
        nc.psum_tensor("A" + name, shape, dt)))
    P("psA", [128, 512], F32)
    P("psB", [128, 512], F32)
    P("psTa", [128, 128], BF16)
    P("psTb", [128, 128], BF16)
    P("psS1", [128, 1024], F32)
    P("psS2", [128, 512], F32)
    P("psAV", [128, 256], F32)




def build_passA_program():
    import contextlib
    nc = bass.Bass("TRN2", target_bir_lowering=False)
    d = {}
    d["x"] = nc.dram_tensor("x", [T, DM], F32, kind="ExternalInput").ap()
    d["l0_wqkv"] = nc.dram_tensor("l0_wqkv", [DM, 3 * DM], F32, kind="ExternalInput").ap()
    d["l0_wout"] = nc.dram_tensor("l0_wout", [DM, DM], F32, kind="ExternalInput").ap()
    for k, v in CONSTS.items():
        dt = BF16 if v.dtype == BF else (F32 if v.dtype == np.float32 else U32)
        d[k] = nc.dram_tensor(k, list(v.shape), dt, kind="ExternalInput").ap()
    d["x1d"] = nc.dram_tensor("x1d", [T, DM], BF16, kind="ExternalOutput").ap()
    d["x1td"] = nc.dram_tensor("x1td", [DM, T], BF16, kind="ExternalOutput").ap()
    if DEBUG_A:
        d["D_qT"] = nc.dram_tensor("D_qT", [128, 8 * 512], BF16, kind="ExternalOutput").ap()
        d["D_kT"] = nc.dram_tensor("D_kT", [128, 8 * 512], BF16, kind="ExternalOutput").ap()
        d["D_vr"] = nc.dram_tensor("D_vr", [128, 4 * 1024], BF16, kind="ExternalOutput").ap()
        d["D_pt"] = nc.dram_tensor("D_pt", [128, 1536], BF16, kind="ExternalOutput").ap()
        d["D_den"] = nc.dram_tensor("D_den", [1, 256], F32, kind="ExternalOutput").ap()
        d["D_attnT"] = nc.dram_tensor("D_attnT", [128, 8 * 256], BF16, kind="ExternalOutput").ap()
        d["D_rln"] = nc.dram_tensor("D_rln", [128, 1024], F32, kind="ExternalOutput").ap()
        d["D_mv"] = nc.dram_tensor("D_mv", [128, 2], F32, kind="ExternalOutput").ap()
        d["D_lnt"] = nc.dram_tensor("D_lnt", [128, 2], F32, kind="ExternalOutput").ap()
        d["D_x1b"] = nc.dram_tensor("D_x1b", [128, 1024], BF16, kind="ExternalOutput").ap()

    ctx = Ctx(nc)
    with contextlib.ExitStack() as stack:
        ctx.make_sems(stack)
        sb, ps = {}, {}
        alloc_common(nc, stack, sb)
        alloc_passA(nc, stack, sb, ps)
        alloc_psum_A(nc, stack, ps)
        blk = stack.enter_context(nc.Block())
        load_common(ctx, d, sb)
        emit_passA(ctx, d, sb, ps)
        fin = ctx.barrier_tok()
        ctx.add("sync", lambda E: None, waits=fin)
        for eng in Ctx.ENGS:
            opsl = ctx.q[eng]
            getattr(blk, eng)(lambda E, opsl=opsl: [f(E) for f in opsl])
    return nc


def consts_inmap():
    return {k: np.asarray(v) for k, v in CONSTS.items()}




_PROGS = {}


def _get_prog(name):
    if name not in _PROGS:
        _PROGS[name] = {"A": build_passA_program, "B": build_passB_program,
                        "C": build_passC_program, "D": build_passD_program,
                        "F": build_fused_program}[name]()
    return _PROGS[name]


def kernel_unfused(**inputs):
    consts = consts_inmap()
    f32 = lambda k: np.ascontiguousarray(np.asarray(inputs[k], np.float32))
    X = f32("L0_attn_x")
    ncA = _get_prog("A")
    mapsA = [dict(x=np.ascontiguousarray(X[b]), l0_wqkv=f32("L0_attn_Wqkv"),
                  l0_wout=f32("L0_attn_Wout"), **consts) for b in range(8)]
    resA = run_bass_kernel_spmd(ncA, mapsA, core_ids=list(range(8))).results

    ncB = _get_prog("B")
    mapsB = [dict(x1d_in=resA[b]["x1d"], x1td_in=resA[b]["x1td"],
                  l0_wg=f32("L0_ffn_Wg"), l0_bg=f32("L0_ffn_bg"),
                  l0_we=f32("L0_ffn_We"), l0_be=f32("L0_ffn_be"), **consts)
             for b in range(8)]
    resB = run_bass_kernel_spmd(ncB, mapsB, core_ids=list(range(8))).results

    ncC = _get_prog("C")
    mapsC = [dict(x2d_in=resB[b]["x2d"], x2td_in=resB[b]["x2td"],
                  l1_wqkv=f32("L1_attn_w_qkv"), l1_wout=f32("L1_attn_w_out"),
                  l1_bout=f32("L1_attn_b_out"), l1_ng=f32("L1_attn_norm_g"),
                  l1_nb=f32("L1_attn_norm_b"), l1_rw=f32("L1_ffn_router_w"), **consts)
             for b in range(8)]
    resC = run_bass_kernel_spmd(ncC, mapsC, core_ids=list(range(8))).results

    ncD = _get_prog("D")
    mapsD = [dict(x3d_in=resC[b]["x3d"], logitsd_in=resC[b]["logitsd"],
                  l1_w0=f32("L1_ffn_w0_kernel"), l1_w1=f32("L1_ffn_w1_kernel"),
                  l1_wo=f32("L1_ffn_wo_kernel"), **consts) for b in range(8)]
    resD = run_bass_kernel_spmd(ncD, mapsD, core_ids=list(range(8))).results

    return np.stack([np.asarray(resD[b]["out"], np.float32) for b in range(8)], axis=0)


if __name__ == "__main__":
    nc = build_passA_program()
    print("built, instructions:", nc.n_instructions() if hasattr(nc, "n_instructions") else "?")



def alloc_passB(nc, stack, sb, ps):
    S = lambda name, shape, dt: sb.__setitem__(name, stack.enter_context(
        nc.sbuf_tensor("sb_" + name, shape, dt)))
    S("B_wg", [128, 8 * 1024], BF16)
    S("B_bg", [1, 1024], BF16)
    S("B_be", [1, 8 * 1024], BF16)
    S("B_we", [128, 2 * 8 * 1024], BF16)
    S("B_x1t", [128, 4 * 1024], BF16)
    S("B_maskp", [128, 32 * 8], F32)
    S("B_resmax", [128, 8], F32)
    S("B_m", [128, 4], F32)
    S("B_junk", [128, 1024], BF16)
    S("B_tt", [128, 4 * 512], BF16)
    S("B_acc", [128, 32 * 1024], BF16)
    S("B_x1b", [128, 2 * 1024], BF16)
    S("B_rln", [128, 1024], F32)
    S("B_bnst", [128, 12], F32)
    S("B_mv", [128, 2], F32)
    S("B_lnt", [128, 2], F32)
    S("B_x2b", [128, 1024], BF16)
    S("B_x2t", [128, 8 * 128], BF16)


def alloc_psum_B(nc, stack, ps):
    P = lambda name, shape, dt: ps.__setitem__(name, stack.enter_context(
        nc.psum_tensor("B" + name, shape, dt)))
    P("psL", [128, 1024], F32)
    P("psE0", [128, 512], F32)
    P("psE1", [128, 512], F32)
    P("psE2", [128, 512], F32)
    P("psE3", [128, 512], F32)
    P("psTa", [128, 128], BF16)
    P("psTb", [128, 128], BF16)


def emit_passB(ctx, d, sb, ps):
    nc = ctx.nc
    X1D, X1TD = d["x1d_in"], d["x1td_in"]
    WG, BG, WE, BE = d["l0_wg"], d["l0_bg"], d["l0_we"], d["l0_be"]
    X2D, X2TD = d["x2d"], d["x2td"]

    idb = sb["c_idb"]
    onesrow = sb["c_onesrow"]
    wg = sb["B_wg"]; bg = sb["B_bg"]; be8 = sb["B_be"]; we = sb["B_we"]
    x1t = sb["B_x1t"]; maskp = sb["B_maskp"]; resmax = sb["B_resmax"]
    mcol = sb["B_m"]; junk = sb["B_junk"]; tt = sb["B_tt"]; acc = sb["B_acc"]
    x1b = sb["B_x1b"]; rln = sb["B_rln"]; bnst = sb["B_bnst"]
    mv = sb["B_mv"]; lnt = sb["B_lnt"]; x2b = sb["B_x2b"]; x2t = sb["B_x2t"]
    psL = ps["psL"]
    psE = [ps["psE0"], ps["psE1"], ps["psE2"], ps["psE3"]]
    psTa, psTb = ps["psTa"], ps["psTb"]

    X1T_R = X1TD.rearrange("(c p) t -> p c t", p=128)
    X1D_R = X1D.rearrange("(t p) dd -> p t dd", p=128)

    t_wg = ctx.dma("gpsimd", wg[:].rearrange("p (c o) -> p c o", o=1024),
                   WG.rearrange("(c p) o -> p c o", p=128))
    t_bg = ctx.dma("gpsimd", bg[:], BG.unsqueeze(0))
    t_be = ctx.dma("gpsimd", be8[:], BE.rearrange("e o -> (e o)").unsqueeze(0))

    t_x1t_use = [None] * 4
    t_exp_prev = None
    t_red_prev = None
    t_gate_done = [None] * 32
    t_x1t_tok = [None] * 32
    for t in range(NT):
        sl = t % 4
        w = [t_x1t_use[sl]] if t_x1t_use[sl] else []
        t_ld = ctx.dma("sync", x1t[:].rearrange("p (s c tt) -> p s c tt", s=4, tt=128)[:, sl],
                       X1T_R[:, :, t * 128:(t + 1) * 128], waits=w)
        t_x1t_tok[t] = t_ld
        w = [t_ld, t_wg, t_bg]
        if t_exp_prev:
            w.append(t_exp_prev)
        if t_red_prev:
            w.append(t_red_prev)
        def gatemm(E, sl=sl):
            last = None
            for fc in range(2):
                for c in range(NC):
                    E.matmul(psL[:, fc * 512:(fc + 1) * 512],
                             x1t[:, (sl * 8 + c) * 128:(sl * 8 + c + 1) * 128],
                             wg[:, c * 1024 + fc * 512: c * 1024 + (fc + 1) * 512],
                             start=(c == 0), stop=False)
                last = E.matmul(psL[:, fc * 512:(fc + 1) * 512], onesrow[0:1, :],
                                bg[0:1, fc * 512:(fc + 1) * 512], start=False, stop=True)
            return last
        t_mm = ctx.add("tensor", gatemm, waits=w, inc="te")
        t_x1t_use[sl] = t_mm
        def redf(E, t=t):
            lv = psL[:].rearrange("p (g e) -> p e g", e=8)
            E.tensor_reduce(resmax[:], lv, AX.X, ALU.max)
            E.drain()
            E.tensor_reduce(mcol[:, 0:1], resmax[:], AX.X, ALU.max)
            E.drain()
            return E.tensor_scalar(mcol[:, 1:2], mcol[:, 0:1], -1.0, None, ALU.mult)
        t_red = ctx.add("vector", redf, waits=[t_mm], inc="ve")
        t_red_prev = t_red
        t_exp = ctx.add("scalar", lambda E: E.activation(
            junk[:], psL[:], AF.Exp, bias=mcol[:, 1:2], accum_out=mcol[:, 2:3]),
            waits=[t_red], inc="se")
        t_exp_prev = t_exp
        def pf(E, t=t):
            E.reciprocal(mcol[:, 3:4], mcol[:, 2:3])
            E.drain()
            return E.tensor_scalar(maskp[:, t * 8:(t + 1) * 8], resmax[:],
                                   mcol[:, 0:1], mcol[:, 3:4], ALU.is_equal, ALU.mult)
        t_gate_done[t] = ctx.add("vector", pf, waits=[t_exp], inc="ve")

    WEV = we[:].rearrange("p (s c o) -> p s c o", s=2, o=1024)
    X1TV = x1t[:].rearrange("p (s c tt) -> p s c tt", s=4, tt=128)

    def we_load(e, waits):
        return ctx.dma("gpsimd", WEV[:, e % 2],
                       WE[e].rearrange("(c p) o -> p c o", p=128), waits=waits)

    t_we = [we_load(0, []), None]
    t_we[1] = we_load(1, [])
    t_we_lastmm = [None, None]
    t_relu = {}
    t_add = {}
    t_tt_use = [None] * 4
    tt_i = 0
    for e in range(E0):
        sl = e % 2
        for t in range(NT):
            tsl = t % 4
            w = [t_x1t_use[tsl]] if t_x1t_use[tsl] else []
            t_ld = ctx.dma("sync", X1TV[:, tsl],
                           X1T_R[:, :, t * 128:(t + 1) * 128], waits=w)
            for fc in range(2):
                pe = psE[(t * 2 + fc) % 4]
                w = [t_ld, t_we[sl], t_be]
                prev = t_relu.get(((t * 2 + fc) % 4, "psum"))
                if prev:
                    w.append(prev)
                def emm(E, tsl=tsl, fc=fc, e=e, pe=pe, sl=sl):
                    for c in range(NC):
                        E.matmul(pe[:],
                                 x1t[:, (tsl * 8 + c) * 128:(tsl * 8 + c + 1) * 128],
                                 we[:, (sl * 8 + c) * 1024 + fc * 512:
                                    (sl * 8 + c) * 1024 + (fc + 1) * 512],
                                 start=(c == 0), stop=False)
                    return E.matmul(pe[:], onesrow[0:1, :],
                                    be8[0:1, e * 1024 + fc * 512: e * 1024 + (fc + 1) * 512],
                                    start=False, stop=True)
                t_mm = ctx.add("tensor", emm, waits=w, inc="te")
                t_x1t_use[tsl] = t_mm
                t_we_lastmm[sl] = t_mm
                if e == 0:
                    dst = acc[:, t * 1024 + fc * 512: t * 1024 + (fc + 1) * 512]
                    w2 = [t_mm, t_gate_done[t]]
                    t_rl = ctx.add("scalar", lambda E, dst=dst, pe=pe, t=t, e=e: E.activation(
                        dst, pe[:], AF.Relu, scale=maskp[:, t * 8 + e: t * 8 + e + 1]),
                        waits=w2, inc="se")
                    t_relu[((t * 2 + fc) % 4, "psum")] = t_rl
                    t_add[(t, fc)] = t_rl
                else:
                    ti = tt_i % 4
                    tt_i += 1
                    dst = tt[:, ti * 512:(ti + 1) * 512]
                    w2 = [t_mm, t_gate_done[t]]
                    if t_tt_use[ti]:
                        w2.append(t_tt_use[ti])
                    t_rl = ctx.add("scalar", lambda E, dst=dst, pe=pe, t=t, e=e: E.activation(
                        dst, pe[:], AF.Relu, scale=maskp[:, t * 8 + e: t * 8 + e + 1]),
                        waits=w2, inc="se")
                    t_relu[((t * 2 + fc) % 4, "psum")] = t_rl
                    aslice = acc[:, t * 1024 + fc * 512: t * 1024 + (fc + 1) * 512]
                    t_ad = ctx.add("vector", lambda E, aslice=aslice, dst=dst: E.tensor_tensor(
                        aslice, aslice, dst, ALU.add), waits=[t_rl, t_add[(t, fc)]], inc="ve")
                    t_tt_use[ti] = t_ad
                    t_add[(t, fc)] = t_ad
        if e + 2 < E0:
            we_next_tok = we_load(e + 2, [t_we_lastmm[sl]])
            t_we[sl] = we_next_tok

    t_x1b_use = [None, None]
    prev_ln_done = None
    t_x2b_dma = None
    t_x2t_dma = None
    last_psT_evac = {0: None, 1: None}
    t_rln_prev = None
    t_x2trans_prev = None
    for t in range(NT):
        sl = t % 2
        w = [t_x1b_use[sl]] if t_x1b_use[sl] else []
        t_ld = ctx.dma("sync", x1b[:, sl * 1024:(sl + 1) * 1024],
                       X1D_R[:, t, :], waits=w)
        w = [t_ld, t_add[(t, 0)], t_add[(t, 1)]]
        if t_rln_prev:
            w.append(t_rln_prev)
        t_res = ctx.add("vector", lambda E, t=t, sl=sl: E.tensor_tensor(
            rln[:], acc[:, t * 1024:(t + 1) * 1024],
            x1b[:, sl * 1024:(sl + 1) * 1024], ALU.add), waits=w, inc="ve")
        t_x1b_use[sl] = t_res
        def lnf(E):
            E.bn_stats(bnst[:, 0:6], rln[:, 0:512])
            E.bn_stats(bnst[:, 6:12], rln[:, 512:1024])
            E.drain()
            E.bn_aggr(mv[:], bnst[:].rearrange("p (g k) -> p g k", k=6))
            E.drain()
            return E.nop()
        wbn = [t_res]
        if prev_ln_done:
            wbn.append(prev_ln_done)
        t_bn = ctx.add("vector", lnf, waits=wbn, inc="ve")
        def rstdf(E):
            E.activation(lnt[:, 0:1], mv[:, 1:2], AF.Ln, bias=1e-5)
            return E.activation(lnt[:, 1:2], lnt[:, 0:1], AF.Exp, scale=-0.5)
        t_rs = ctx.add("scalar", rstdf, waits=[t_bn], inc="se")
        w3 = [t_rs]
        if t_x2b_dma:
            w3.append(t_x2b_dma)
        if t_x2trans_prev:
            w3.append(t_x2trans_prev)
        t_ap = ctx.add("vector", lambda E: E.tensor_scalar(
            x2b[:], rln[:], mv[:, 0:1], lnt[:, 1:2], ALU.subtract, ALU.mult),
            waits=w3, inc="ve")
        prev_ln_done = t_ap
        t_rln_prev = t_ap
        t_x2b_dma = ctx.dma("sync", X2D.rearrange("(t p) dd -> p t dd", p=128)[:, t, :],
                            x2b[:], waits=[t_ap])
        t_evl = None
        for c in range(NC):
            tp = c % 2
            pst = psTa if tp == 0 else psTb
            w4 = [t_ap]
            if last_psT_evac[tp]:
                w4.append(last_psT_evac[tp])
            t_tr = ctx.add("tensor", lambda E, c=c, pst=pst: E.transpose(
                pst[:], x2b[:, c * 128:(c + 1) * 128], idb[:]), waits=w4, inc="te")
            w5 = [t_tr]
            if t_x2t_dma:
                w5.append(t_x2t_dma)
            t_evl = ctx.add("vector", lambda E, c=c, pst=pst: E.tensor_copy(
                x2t[:, c * 128:(c + 1) * 128], pst[:]), waits=w5, inc="ve")
            last_psT_evac[tp] = t_evl
            t_x2trans_prev = t_tr
        t_x2t_dma = ctx.dma("sync",
                            X2TD.rearrange("(c p) tt -> p c tt", p=128)[:, :, t * 128:(t + 1) * 128],
                            x2t[:].rearrange("p (c tt) -> p c tt", tt=128),
                            waits=[t_evl])
    return ctx.barrier_tok()


def build_passB_program():
    import contextlib
    nc = bass.Bass("TRN2", target_bir_lowering=False)
    d = {}
    d["x1d_in"] = nc.dram_tensor("x1d_in", [T, DM], BF16, kind="ExternalInput").ap()
    d["x1td_in"] = nc.dram_tensor("x1td_in", [DM, T], BF16, kind="ExternalInput").ap()
    d["l0_wg"] = nc.dram_tensor("l0_wg", [DM, DM], F32, kind="ExternalInput").ap()
    d["l0_bg"] = nc.dram_tensor("l0_bg", [DM], F32, kind="ExternalInput").ap()
    d["l0_we"] = nc.dram_tensor("l0_we", [E0, DM, DM], F32, kind="ExternalInput").ap()
    d["l0_be"] = nc.dram_tensor("l0_be", [E0, DM], F32, kind="ExternalInput").ap()
    for k, v in CONSTS.items():
        dt = BF16 if v.dtype == BF else (F32 if v.dtype == np.float32 else U32)
        d[k] = nc.dram_tensor(k, list(v.shape), dt, kind="ExternalInput").ap()
    d["x2d"] = nc.dram_tensor("x2d", [T, DM], BF16, kind="ExternalOutput").ap()
    d["x2td"] = nc.dram_tensor("x2td", [DM, T], BF16, kind="ExternalOutput").ap()

    ctx = Ctx(nc)
    with contextlib.ExitStack() as stack:
        ctx.make_sems(stack)
        sb, ps = {}, {}
        alloc_common(nc, stack, sb)
        alloc_passB(nc, stack, sb, ps)
        alloc_psum_B(nc, stack, ps)
        blk = stack.enter_context(nc.Block())
        load_common(ctx, d, sb)
        emit_passB(ctx, d, sb, ps)
        fin = ctx.barrier_tok()
        ctx.add("sync", lambda E: None, waits=fin)
        for eng in Ctx.ENGS:
            opsl = ctx.q[eng]
            getattr(blk, eng)(lambda E, opsl=opsl: [f(E) for f in opsl])
    return nc



def alloc_passC(nc, stack, sb):
    S = lambda name, shape, dt: sb.__setitem__(name, stack.enter_context(
        nc.sbuf_tensor("sb_" + name, shape, dt)))
    S("C_wtmp", [128, 1024], BF16)
    S("C_wq", [128, 8 * 1024], BF16)
    S("C_wkv", [128, 8 * 2048], BF16)
    S("C_wo", [128, 8 * 1024], BF16)
    S("C_ng", [1, 16], F32)
    S("C_nb2rhs", [2, 8], F32)
    S("C_bout", [1, 1024], F32)
    S("C_rw", [128, 8 * 8], F32)
    S("C_x2t", [128, 4 * 1024], BF16)
    S("C_kt", [128, 1024], BF16)
    S("C_vt", [128, 1024], BF16)
    S("C_M", [128, 2 * 512], BF16)
    S("C_N", [128, 2 * 512], BF16)
    S("C_Macc", [128, 4 * 512], F32)
    S("C_Sacc", [128, 8], F32)
    S("C_S", [128, 8], F32)
    S("C_nb2", [128, 8], F32)
    S("C_bS", [128, 8], BF16)
    S("C_cvec", [1, 1024], F32)
    S("C_cvh", [1, 1024], BF16)
    S("C_x2s", [128, 8 * 512], BF16)
    S("C_qT", [128, 8 * 512], BF16)
    S("C_nsq", [64, 512], BF16)
    S("C_snorm", [1, 512], F32)
    S("C_arow", [1, 512], F32)
    S("C_bcsb", [64, 512], F32)
    S("C_attnT", [128, 8 * 512], BF16)
    S("C_x2b", [128, 2 * 1024], BF16)
    S("C_rln", [128, 1024], F32)
    S("C_bnst", [128, 12], F32)
    S("C_mv", [128, 2], F32)
    S("C_lnt", [128, 2], F32)
    S("C_x3f", [128, 1024], F32)
    S("C_x3b", [128, 1024], BF16)
    S("C_x3tf", [128, 128], F32)
    S("C_logits", [128, 32 * 8], F32)


def emit_passC(ctx, d, sb, stack):
    nc = ctx.nc
    X2D, X2TD = d["x2d_in"], d["x2td_in"]
    WQKV1, WOUT1, BOUT1 = d["l1_wqkv"], d["l1_wout"], d["l1_bout"]
    NG, NB, RW = d["l1_ng"], d["l1_nb"], d["l1_rw"]
    X3D, LOGD = d["x3d"], d["logitsd"]

    idb = sb["c_idb"]; idf = sb["c_idf"]
    onescol = sb["c_onescol"]; onesrowf = sb["c_onesrowf"]
    wtmp = sb["C_wtmp"]; wq = sb["C_wq"]; wkv = sb["C_wkv"]; wo = sb["C_wo"]
    ngr = sb["C_ng"]; nb2rhs = sb["C_nb2rhs"]; boutr = sb["C_bout"]; rw = sb["C_rw"]
    x2t = sb["C_x2t"]; kt = sb["C_kt"]; vt = sb["C_vt"]
    Msb = sb["C_M"]; Nsb = sb["C_N"]; Ssb = sb["C_S"]
    Macc = sb["C_Macc"]; Sacc = sb["C_Sacc"]
    nb2 = sb["C_nb2"]; bS = sb["C_bS"]; cvec = sb["C_cvec"]; cvh = sb["C_cvh"]
    x2s = sb["C_x2s"]; qTt = sb["C_qT"]; nsq = sb["C_nsq"]
    snorm = sb["C_snorm"]; arow = sb["C_arow"]; bcsb = sb["C_bcsb"]
    attnT = sb["C_attnT"]
    x2b = sb["C_x2b"]; rln = sb["C_rln"]; bnst = sb["C_bnst"]
    mv = sb["C_mv"]; lnt = sb["C_lnt"]; x3f = sb["C_x3f"]; x3b = sb["C_x3b"]
    x3tf = sb["C_x3tf"]; logits = sb["C_logits"]

    X2T_R = X2TD.rearrange("(c p) t -> p c t", p=128)
    X2D_R = X2D.rearrange("(t p) dd -> p t dd", p=128)

    t_ng = ctx.dma("sync", ngr[:], NG.rearrange("a h bb e -> a (h bb e)"))
    t_bout = ctx.dma("sync", boutr[:], BOUT1.unsqueeze(0))
    t_rw = ctx.dma("sync", rw[:].rearrange("p (c e) -> p c e", e=8),
                   RW.rearrange("(c p) e -> p c e", p=128))
    t_nb = ctx.dma("sync", nb2rhs[:], NB.rearrange("a (c i) bb e -> i (a c bb e)", i=2),
                   allow_slow_non_contiguous=True)

    with (
        nc.psum_tensor("psCTa", [128, 128], BF16) as psTa,
        nc.psum_tensor("psCTb", [128, 128], BF16) as psTb,
        nc.psum_tensor("psC_nb", [128, 8], F32) as psNB2,
    ):
        last_ev = {0: None, 1: None}
        t_wl_prev = None

        def wload_transpose(SRC, dst, oblocks):
            nonlocal t_wl_prev
            last = None
            for i, ob in enumerate(oblocks):
                w = [t_wl_prev] if t_wl_prev else []
                t_ld = ctx.dma("gpsimd", wtmp[:],
                               SRC[ob * 128:(ob + 1) * 128, :], waits=w)
                for c in range(NC):
                    tp = c % 2
                    w2 = [t_ld]
                    if last_ev[tp]:
                        w2.append(last_ev[tp])
                    pst = psTa if tp == 0 else psTb
                    t_tr = ctx.add("tensor", lambda E, c=c, pst=pst: E.transpose(
                        pst[:], wtmp[:, c * 128:(c + 1) * 128], idb[:]), waits=w2, inc="te")
                    t_ev = ctx.add("vector", lambda E, c=c, i=i, pst=pst, dst=dst: E.tensor_copy(
                        dst[:].rearrange("p (c o) -> p c o", c=NC)[:, c, i * 128:(i + 1) * 128],
                        pst[:]), waits=[t_tr], inc="ve")
                    last_ev[tp] = t_ev
                    last = t_ev
                t_wl_prev = last
            return last

        t_wq = wload_transpose(WQKV1, wq, list(range(8)))
        t_wkv = wload_transpose(WQKV1, wkv, list(range(8, 24)))
        t_wo = wload_transpose(WOUT1, wo, list(range(8)))
        t_nb2 = ctx.add("tensor", lambda E: E.matmul(
            psNB2[:], sb["c_bc2f"][:], nb2rhs[:], start=True, stop=True),
            waits=[t_nb, d["tok_bc2f"]], inc="te")
        t_nb2e = ctx.add("vector", lambda E: E.tensor_copy(nb2[:], psNB2[:]),
                         waits=[t_nb2], inc="ve")
    t_c0 = ctx.barrier_tok()

    with (
        nc.psum_tensor("psKV", [128, 512], F32) as psKV,
        nc.psum_tensor("psKV2", [128, 512], F32) as psKV2,
        nc.psum_tensor("psMA", [128, 512], F32) as psMA,
        nc.psum_tensor("psMB", [128, 512], F32) as psMB,
        nc.psum_tensor("psNA", [128, 512], F32) as psNA,
        nc.psum_tensor("psNBv", [128, 512], F32) as psNAB,
        nc.psum_tensor("psS", [128, 8], F32) as psS,
        nc.psum_tensor("psCV", [1, 512], F32) as psCV,
    ):
        t_x2t_use = [None] * 4
        t_kv_ev = {0: None, 1: None}
        t_stats_done = None
        for t in range(NT):
            tsl = t % 4
            w = [t_x2t_use[tsl]] if t_x2t_use[tsl] else []
            t_ld = ctx.dma("sync",
                           x2t[:].rearrange("p (s c tt) -> p s c tt", s=4, tt=128)[:, tsl],
                           X2T_R[:, :, t * 128:(t + 1) * 128], waits=w)
            t_mm_last = None
            for g in range(4):
                pp = g % 2
                pkv = psKV if pp == 0 else psKV2
                w2 = [t_ld, t_wkv]
                if t == 0 and g < 2:
                    w2 += t_c0
                if t_kv_ev[pp]:
                    w2.append(t_kv_ev[pp])
                def kvmm(E, tsl=tsl, g=g, pkv=pkv):
                    last = None
                    for c in range(NC):
                        last = E.matmul(
                            pkv[:],
                            x2t[:, (tsl * 8 + c) * 128:(tsl * 8 + c + 1) * 128],
                            wkv[:, c * 2048 + g * 512: c * 2048 + (g + 1) * 512],
                            start=(c == 0), stop=(c == NC - 1))
                    return last
                t_mm = ctx.add("tensor", kvmm, waits=w2, inc="te")
                dst = (kt if g < 2 else vt)[:, (g % 2) * 512:((g % 2) + 1) * 512]
                t_ev = ctx.add("vector", lambda E, dst=dst, pkv=pkv: E.tensor_copy(
                    dst, pkv[:]), waits=[t_mm], inc="ve")
                t_kv_ev[pp] = t_ev
                t_mm_last = t_ev
            def statmm(E, t=t):
                last = None
                for hp in range(8):
                    lk = kt[:, hp * 128:(hp + 1) * 128]
                    E.matmul(psMA[:, hp * 64:(hp + 1) * 64], lk,
                             kt[:, hp * 128: hp * 128 + 64], start=True, stop=True)
                    E.matmul(psMB[:, hp * 64:(hp + 1) * 64], lk,
                             kt[:, hp * 128 + 64: hp * 128 + 128], start=True, stop=True)
                    E.matmul(psNA[:, hp * 64:(hp + 1) * 64], lk,
                             vt[:, hp * 128: hp * 128 + 64], start=True, stop=True)
                    last = E.matmul(psNAB[:, hp * 64:(hp + 1) * 64], lk,
                                    vt[:, hp * 128 + 64: hp * 128 + 128], start=True, stop=True)
                for c in range(NC):
                    last = E.matmul(psS[:, c:c + 1], vt[:, c * 128:(c + 1) * 128],
                                    onescol[:], start=True, stop=True)
                return last
            w4 = [t_mm_last]
            if t_stats_done:
                w4.append(t_stats_done)
            t_st = ctx.add("tensor", statmm, waits=w4, inc="te")
            t_x2t_use[tsl] = t_st
            def stacc(E, t=t):
                if t == 0:
                    E.tensor_copy(Macc[:, 0:512], psMA[:])
                    E.tensor_copy(Macc[:, 512:1024], psMB[:])
                    E.tensor_copy(Macc[:, 1024:1536], psNA[:])
                    E.tensor_copy(Macc[:, 1536:2048], psNAB[:])
                    return E.tensor_copy(Sacc[:], psS[:])
                E.tensor_tensor(Macc[:, 0:512], Macc[:, 0:512], psMA[:], ALU.add)
                E.tensor_tensor(Macc[:, 512:1024], Macc[:, 512:1024], psMB[:], ALU.add)
                E.tensor_tensor(Macc[:, 1024:1536], Macc[:, 1024:1536], psNA[:], ALU.add)
                E.tensor_tensor(Macc[:, 1536:2048], Macc[:, 1536:2048], psNAB[:], ALU.add)
                return E.tensor_tensor(Sacc[:], Sacc[:], psS[:], ALU.add)
            t_stats_done = ctx.add("vector", stacc, waits=[t_st], inc="ve")

        def evst(E):
            E.tensor_copy(Msb[:, 0:512], Macc[:, 0:512])
            E.tensor_copy(Msb[:, 512:1024], Macc[:, 512:1024])
            E.tensor_copy(Nsb[:, 0:512], Macc[:, 1024:1536])
            E.tensor_copy(Nsb[:, 512:1024], Macc[:, 1536:2048])
            E.tensor_copy(Ssb[:], Sacc[:])
            E.drain()
            return E.tensor_tensor(bS[:], Ssb[:], nb2[:], ALU.mult)
        t_evst = ctx.add("vector", evst, waits=[t_stats_done, t_nb2e], inc="ve")

        t_cv_prev = None
        for oc in range(2):
            def cvh_mm(E, oc=oc):
                last = None
                for c in range(NC):
                    last = E.matmul(psCV[0:1, :], bS[:, c:c + 1],
                                    wo[:, c * 1024 + oc * 512: c * 1024 + (oc + 1) * 512],
                                    start=(c == 0), stop=(c == NC - 1))
                return last
            w4 = [t_evst, t_wo]
            if t_cv_prev:
                w4.append(t_cv_prev)
            t_m = ctx.add("tensor", cvh_mm, waits=w4, inc="te")
            t_cv_prev = ctx.add("vector", lambda E, oc=oc: E.tensor_tensor(
                cvec[0:1, oc * 512:(oc + 1) * 512], psCV[0:1, :],
                boutr[0:1, oc * 512:(oc + 1) * 512], ALU.add),
                waits=[t_m, t_bout], inc="ve")
        t_cvh = ctx.add("vector", lambda E: E.tensor_copy(cvh[:], cvec[:]),
                        waits=[t_cv_prev], inc="ve")
    t_c1 = ctx.barrier_tok()

    with (
        nc.psum_tensor("psQa", [128, 512], F32) as psQa,
        nc.psum_tensor("psQb", [128, 512], F32) as psQb,
        nc.psum_tensor("psT1", [128, 512], F32) as psT1,
        nc.psum_tensor("psO", [128, 512], F32) as psO,
        nc.psum_tensor("psM2", [128, 512], F32) as psM2,
        nc.psum_tensor("psBC", [64, 512], F32) as psBC,
        nc.psum_tensor("psY", [128, 512], F32) as psY,
        nc.psum_tensor("psRT", [128, 128], F32) as psRT,
    ):
        t_x2s_use = None
        t_q_ev = {0: None, 1: None}
        t_nsq_use = None
        t_sn = None
        t_ar_use = None
        t_bce = None
        t_avn = None
        t_attnT_use = None
        t_y_ev = None
        t_x2b_use = [None, None]
        prev_ln_done = None
        t_x3b_dma = None
        t_rln_prev = None
        t_x3trans_prev = None
        t_x3f_use = None
        t_rt_ev = None
        t_rl_ev_prev = None
        t_nsq_mm_prev = None
        t_norm_read_prev = None
        first = True
        for s in range(8):
            w = [t_x2s_use] if t_x2s_use else []
            t_lds = ctx.dma("sync", x2s[:].rearrange("p (c t) -> p c t", t=512),
                            X2T_R[:, :, s * 512:(s + 1) * 512], waits=w)
            t_qT = None
            for oc in range(NC):
                pp = oc % 2
                pq = psQa if pp == 0 else psQb
                w2 = [t_lds, t_wq]
                if first:
                    w2 += t_c1
                if t_q_ev[pp]:
                    w2.append(t_q_ev[pp])
                if t_rt_ev:
                    w2.append(t_rt_ev)
                if t_rl_ev_prev:
                    w2.append(t_rl_ev_prev)
                first = False
                def qmm(E, oc=oc, pq=pq):
                    last = None
                    for c in range(NC):
                        last = E.matmul(
                            pq[:], wq[:, c * 1024 + oc * 128: c * 1024 + (oc + 1) * 128],
                            x2s[:, c * 512:(c + 1) * 512],
                            start=(c == 0), stop=(c == NC - 1))
                    return last
                t_mm = ctx.add("tensor", qmm, waits=w2, inc="te")
                t_ev = ctx.add("vector", lambda E, oc=oc, pq=pq: E.tensor_copy(
                    qTt[:, oc * 512:(oc + 1) * 512], pq[:]), waits=[t_mm], inc="ve")
                t_q_ev[pp] = t_ev
                t_qT = t_ev
            for h in range(NH):
                hp, r0 = h // 2, (h % 2) * 64
                Mh = Msb[r0:r0 + 64, (h % 2) * 512 + hp * 64: (h % 2) * 512 + (hp + 1) * 64]
                Nh = Nsb[r0:r0 + 64, (h % 2) * 512 + hp * 64: (h % 2) * 512 + (hp + 1) * 64]
                qh = qTt[r0:r0 + 64, hp * 512:(hp + 1) * 512]
                w3 = [t_qT]
                if t_nsq_use:
                    w3.append(t_nsq_use)
                t_t1 = ctx.add("tensor", lambda E, Mh=Mh, qh=qh: E.matmul(
                    psT1[0:64, :], Mh, qh, start=True, stop=True), waits=w3, inc="te")
                w3b = [t_qT]
                if t_avn:
                    w3b.append(t_avn)
                t_ou = ctx.add("tensor", lambda E, Nh=Nh, qh=qh: E.matmul(
                    psO[0:64, :], Nh, qh, start=True, stop=True), waits=w3b, inc="te")
                w4 = [t_t1]
                if t_nsq_mm_prev:
                    w4.append(t_nsq_mm_prev)
                t_nsq_use = ctx.add("vector", lambda E, qh=qh: E.tensor_tensor(
                    nsq[:], psT1[0:64, :], qh, ALU.mult), waits=w4, inc="ve")
                w5 = [t_nsq_use]
                if t_norm_read_prev:
                    w5.append(t_norm_read_prev)
                t_n2 = ctx.add("tensor", lambda E: E.matmul(
                    psM2[0:1, 0:512], onescol[0:64, :], nsq[:], start=True, stop=True),
                    waits=w5, inc="te")
                t_nsq_mm_prev = t_n2
                w6 = [t_n2]
                if t_ar_use:
                    w6.append(t_ar_use)
                t_sn = ctx.add("scalar", lambda E: E.activation(
                    snorm[:], psM2[0:1, 0:512], AF.Sqrt), waits=w6, inc="se")
                t_norm_read_prev = t_sn
                def archain(E, h=h):
                    E.tensor_scalar(arow[:], snorm[:], 1e-12, None, ALU.max)
                    E.drain()
                    E.reciprocal(arow[:], arow[:])
                    E.drain()
                    return E.tensor_scalar(arow[:], arow[:], ngr[0:1, h:h + 1], None, ALU.mult)
                t_ar = ctx.add("vector", archain, waits=[t_sn, t_ng], inc="ve")
                w7 = [t_ar]
                if t_bce:
                    w7.append(t_bce)
                t_bc = ctx.add("tensor", lambda E: E.matmul(
                    psBC[:], onesrowf[0:1, 0:64], arow[:], start=True, stop=True),
                    waits=w7, inc="te")
                t_ar_use = t_bc
                t_bce = ctx.add("vector", lambda E: E.tensor_copy(bcsb[:], psBC[:]),
                                waits=[t_bc], inc="ve")
                w8 = [t_bce, t_ou]
                if t_attnT_use:
                    w8.append(t_attnT_use)
                t_avn = ctx.add("vector", lambda E, hp=hp, r0=r0: E.tensor_tensor(
                    attnT[r0:r0 + 64, hp * 512:(hp + 1) * 512], psO[0:64, :], bcsb[:],
                    ALU.mult), waits=w8, inc="ve")
            t_attnT_use = None
            for tsub in range(4):
                gt = s * 4 + tsub
                sl = gt % 2
                w = [t_x2b_use[sl]] if t_x2b_use[sl] else []
                t_ldr = ctx.dma("sync", x2b[:, sl * 1024:(sl + 1) * 1024],
                                X2D_R[:, gt, :], waits=w)
                t_res_done = None
                for och in range(2):
                    w2 = [t_avn, t_wo, t_cvh]
                    if t_y_ev:
                        w2.append(t_y_ev)
                    def ymm(E, tsub=tsub, och=och):
                        for c in range(NC):
                            E.matmul(psY[:],
                                     attnT[:, c * 512 + tsub * 128: c * 512 + tsub * 128 + 128],
                                     wo[:, c * 1024 + och * 512: c * 1024 + (och + 1) * 512],
                                     start=(c == 0), stop=False)
                        return E.matmul(psY[:], sb["c_onesrow"][0:1, :],
                                        cvh[0:1, och * 512:(och + 1) * 512], start=False, stop=True)
                    t_y = ctx.add("tensor", ymm, waits=w2, inc="te")
                    w3 = [t_y, t_ldr]
                    if t_rln_prev:
                        w3.append(t_rln_prev)
                    t_res = ctx.add("vector", lambda E, och=och, sl=sl: E.tensor_tensor(
                        rln[:, och * 512:(och + 1) * 512], psY[:],
                        x2b[:, sl * 1024 + och * 512: sl * 1024 + (och + 1) * 512],
                        ALU.add), waits=w3, inc="ve")
                    t_y_ev = t_res
                    t_res_done = t_res
                t_x2b_use[sl] = t_res_done
                t_attnT_use = t_res_done
                def lnf(E):
                    E.bn_stats(bnst[:, 0:6], rln[:, 0:512])
                    E.bn_stats(bnst[:, 6:12], rln[:, 512:1024])
                    E.drain()
                    E.bn_aggr(mv[:], bnst[:].rearrange("p (g k) -> p g k", k=6))
                    E.drain()
                    return E.nop()
                wbn = [t_res_done]
                if prev_ln_done:
                    wbn.append(prev_ln_done)
                t_bn = ctx.add("vector", lnf, waits=wbn, inc="ve")
                def rstdf(E):
                    E.activation(lnt[:, 0:1], mv[:, 1:2], AF.Ln, bias=1e-5)
                    return E.activation(lnt[:, 1:2], lnt[:, 0:1], AF.Exp, scale=-0.5)
                t_rs = ctx.add("scalar", rstdf, waits=[t_bn], inc="se")
                w4 = [t_rs]
                if t_x3f_use:
                    w4.append(t_x3f_use)
                t_apf = ctx.add("vector", lambda E: E.tensor_scalar(
                    x3f[:], rln[:], mv[:, 0:1], lnt[:, 1:2], ALU.subtract, ALU.mult),
                    waits=w4, inc="ve")
                prev_ln_done = t_apf
                t_rln_prev = t_apf
                w5 = [t_apf]
                if t_x3b_dma:
                    w5.append(t_x3b_dma)
                t_apb = ctx.add("vector", lambda E: E.tensor_copy(x3b[:], x3f[:]),
                                waits=w5, inc="ve")
                t_x3b_dma = ctx.dma("sync", X3D.rearrange("(t p) dd -> p t dd", p=128)[:, gt, :],
                                    x3b[:], waits=[t_apb])
                t_rt_prev_local = None
                for c in range(NC):
                    w6 = [t_apf, t_rw]
                    if t_rt_ev:
                        w6.append(t_rt_ev)
                    t_rt = ctx.add("tensor", lambda E, c=c: E.transpose(
                        psRT[:], x3f[:, c * 128:(c + 1) * 128], idf[:]), waits=w6, inc="te")
                    w7 = [t_rt]
                    if t_x3f_use:
                        w7.append(t_x3f_use)
                    t_rt_ev = ctx.add("vector", lambda E: E.tensor_copy(x3tf[:], psRT[:]),
                                      waits=[t_rt], inc="ve")
                    t_rl = ctx.add("tensor", lambda E, c=c, gt=gt: E.matmul(
                        psM2[:, 8:16], x3tf[:], rw[:, c * 8:(c + 1) * 8],
                        start=(c == 0), stop=(c == NC - 1)), waits=[t_rt_ev], inc="te")
                    t_rt_prev_local = t_rl
                t_x3f_use = t_rt_ev
                t_rl_ev_prev = ctx.add("vector", lambda E, gt=gt: E.tensor_copy(
                    logits[:, gt * 8:(gt + 1) * 8], psM2[:, 8:16]),
                    waits=[t_rt_prev_local], inc="ve")
    t_lg = ctx.dma("sync", LOGD.rearrange("(t p) e -> p t e", p=128),
                   logits[:].rearrange("p (t e) -> p t e", e=8), waits=[t_rl_ev_prev])
    return ctx.barrier_tok()


def build_passC_program():
    import contextlib
    nc = bass.Bass("TRN2", target_bir_lowering=False)
    d = {}
    d["x2d_in"] = nc.dram_tensor("x2d_in", [T, DM], BF16, kind="ExternalInput").ap()
    d["x2td_in"] = nc.dram_tensor("x2td_in", [DM, T], BF16, kind="ExternalInput").ap()
    d["l1_wqkv"] = nc.dram_tensor("l1_wqkv", [3 * DM, DM], F32, kind="ExternalInput").ap()
    d["l1_wout"] = nc.dram_tensor("l1_wout", [DM, DM], F32, kind="ExternalInput").ap()
    d["l1_bout"] = nc.dram_tensor("l1_bout", [DM], F32, kind="ExternalInput").ap()
    d["l1_ng"] = nc.dram_tensor("l1_ng", [1, 16, 1, 1], F32, kind="ExternalInput").ap()
    d["l1_nb"] = nc.dram_tensor("l1_nb", [1, 16, 1, 1], F32, kind="ExternalInput").ap()
    d["l1_rw"] = nc.dram_tensor("l1_rw", [DM, 8], F32, kind="ExternalInput").ap()
    for k, v in CONSTS.items():
        dt = BF16 if v.dtype == BF else (F32 if v.dtype == np.float32 else U32)
        d[k] = nc.dram_tensor(k, list(v.shape), dt, kind="ExternalInput").ap()
    d["x3d"] = nc.dram_tensor("x3d", [T, DM], BF16, kind="ExternalOutput").ap()
    d["logitsd"] = nc.dram_tensor("logitsd", [T, 8], F32, kind="ExternalOutput").ap()

    ctx = Ctx(nc)
    import contextlib as _cl
    with _cl.ExitStack() as stack:
        ctx.make_sems(stack)
        sb, ps = {}, {}
        alloc_common(nc, stack, sb)
        sb["c_idf"] = stack.enter_context(nc.sbuf_tensor("sb_c_idf", [128, 128], F32))
        sb["c_bc2f"] = stack.enter_context(nc.sbuf_tensor("sb_c_bc2f", [2, 128], F32))
        alloc_passC(nc, stack, sb)
        blk = stack.enter_context(nc.Block())
        load_common(ctx, d, sb)
        ctx.dma("sync", sb["c_idf"][:], d["c_id128f"])
        d["tok_bc2f"] = ctx.dma("sync", sb["c_bc2f"][:], d["c_bc2f"])
        emit_passC(ctx, d, sb, stack)
        fin = ctx.barrier_tok()
        ctx.add("sync", lambda E: None, waits=fin)
        for eng in Ctx.ENGS:
            opsl = ctx.q[eng]
            getattr(blk, eng)(lambda E, opsl=opsl: [f(E) for f in opsl])
    return nc



DEBUG_D = False
GATING_ONLY = False

def alloc_passD(nc, stack, sb):
    S = lambda name, shape, dt: sb.__setitem__(name, stack.enter_context(
        nc.sbuf_tensor("sb_" + name, shape, dt)))
    S("D_lg", [128, 32 * 8], F32)
    S("D_t0", [128, 32 * 8], F32)
    S("D_t1", [128, 32 * 8], F32)
    S("D_raw", [128, 32 * 8], F32)
    S("D_m1", [128, 32 * 8], F32)
    S("D_m2", [128, 32 * 8], F32)
    S("D_m1b", [128, 32 * 8], BF16)
    S("D_mx", [128, 32], F32)
    S("D_g1", [128, 32], F32)
    S("D_g2", [128, 32], F32)
    S("D_i1", [128, 32], F32)
    S("D_i2", [128, 32], F32)
    S("D_k1", [128, 32], F32)
    S("D_p1", [128, 32], F32)
    S("D_p2", [128, 32], F32)
    S("D_cnt", [1, 256], F32)
    S("D_cntT", [128, 2], BF16)
    S("D_cntT1c", [128, 2], BF16)
    S("D_offc", [128, 2], F32)
    S("D_offr", [1, 256], F32)
    S("D_offb", [128, 32 * 8], F32)
    S("D_px", [128, 32 * 8], F32)
    S("D_sl1", [128, 32], U32)
    S("D_sl2", [128, 32], U32)
    S("D_x3", [128, 2 * 1024], BF16)
    S("D_xe", [128, 1024], BF16)
    S("D_xeT", [128, 8 * 1024], BF16)
    S("D_w0", [128, 2 * 8 * 512], BF16)
    S("D_w1", [128, 2 * 8 * 512], BF16)
    S("D_gel", [128, 512], BF16)
    S("D_hid", [128, 2 * 512], BF16)
    S("D_wo", [128, 43 * 1024], BF16)
    S("D_hc", [128, 2 * 43 * 128], BF16)
    S("D_yb", [128, 2 * 512], BF16)
    S("D_A", [128, 1024], BF16)
    S("D_B", [128, 1024], BF16)
    S("D_mo", [128, 1024], F32)
    S("D_rln", [128, 1024], F32)
    S("D_bnst", [128, 12], F32)
    S("D_mv", [128, 2], F32)
    S("D_lnt", [128, 2], F32)
    S("D_out", [128, 2 * 1024], F32)
    S("D_dbg", [128, 8], F32)
    S("D_dbgr", [1, 512], F32)


def emit_passD(ctx, d, sb, stack):
    global GATING_ONLY
    nc = ctx.nc
    X3D, LOGD = d["x3d_in"], d["logitsd_in"]
    W0, W1, WO = d["l1_w0"], d["l1_w1"], d["l1_wo"]
    XD, YD, HIDD = d["Xd"], d["Yd"], d["Hidd"]
    OUT = d["out"]

    idb = sb["c_idb"]
    onescol = sb["c_onescol"]; onesrowf = sb["c_onesrowf"]
    u128 = sb["c_u128b"]; u8s = sb["c_u8sb"]; a8s = sb["c_a8sb"]
    iota8 = sb["c_iota8t"]; i8rev = sb["c_i8revt"]; idf = sb["c_idf"]
    lg = sb["D_lg"]; t0 = sb["D_t0"]; t1_ = sb["D_t1"]; raw = sb["D_raw"]
    m1 = sb["D_m1"]; m2 = sb["D_m2"]; m1b = sb["D_m1b"]
    mx = sb["D_mx"]; g1 = sb["D_g1"]; g2 = sb["D_g2"]
    i1 = sb["D_i1"]; i2 = sb["D_i2"]; k1 = sb["D_k1"]
    p1 = sb["D_p1"]; p2 = sb["D_p2"]
    cnt = sb["D_cnt"]; cntT = sb["D_cntT"]; offc = sb["D_offc"]; offr = sb["D_offr"]
    offb = sb["D_offb"]; px = sb["D_px"]
    sl1 = sb["D_sl1"]; sl2 = sb["D_sl2"]
    x3t = sb["D_x3"]; xe = sb["D_xe"]; xeT = sb["D_xeT"]
    w0b = sb["D_w0"]; w1b = sb["D_w1"]; gel = sb["D_gel"]; hid = sb["D_hid"]
    wor = sb["D_wo"]; hc = sb["D_hc"]; yb = sb["D_yb"]
    Ab = sb["D_A"]; Bb = sb["D_B"]; mo = sb["D_mo"]; rln = sb["D_rln"]
    bnst = sb["D_bnst"]; mv = sb["D_mv"]; lnt = sb["D_lnt"]; outb = sb["D_out"]

    X3R = X3D.rearrange("(t p) dd -> p t dd", p=128)
    V8 = lambda ap: ap.rearrange("p (t e) -> p t e", e=8)
    B32 = lambda col: col.unsqueeze(2).broadcast_to([128, 32, 8])

    t_lg = ctx.dma("sync", lg[:].rearrange("p (t e) -> p t e", e=8),
                   LOGD.rearrange("(t p) e -> p t e", p=128))
    BCREG = []
    ctx.add("gpsimd", lambda E: (BCREG.append(E.to_reg(8191)), None)[1], inc="ge")

    with (
        nc.psum_tensor("psG", [128, 256], F32) as psG,
        nc.psum_tensor("psGB", [128, 256], F32) as psGB,
        nc.psum_tensor("psCnt", [1, 256], F32) as psCnt,
        nc.psum_tensor("psOffs", [128, 2], F32) as psOffs,
        nc.psum_tensor("psTR", [128, 128], F32) as psTR,
    ):
        def soft(E):
            E.tensor_reduce(mx[:], V8(lg[:]), AX.X, ALU.max)
            E.drain()
            E.tensor_tensor(V8(t0[:]), V8(lg[:]), B32(mx[:]), ALU.subtract)
            return E.nop()
        t_s1 = ctx.add("vector", soft, waits=[t_lg], inc="ve")
        t_s2 = ctx.add("scalar", lambda E: E.activation(t1_[:], t0[:], AF.Exp),
                       waits=[t_s1], inc="se")
        def soft2(E):
            E.tensor_reduce(mx[:], V8(t1_[:]), AX.X, ALU.add)
            E.drain()
            E.reciprocal(mx[:], mx[:])
            E.drain()
            E.tensor_tensor(V8(raw[:]), V8(t1_[:]), B32(mx[:]), ALU.mult)
            E.drain()
            E.tensor_reduce(g1[:], V8(raw[:]), AX.X, ALU.max)
            E.drain()
            E.tensor_tensor(V8(m1[:]), V8(raw[:]), B32(g1[:]), ALU.is_equal)
            E.tensor_tensor(V8(t0[:]), V8(m1[:]), V8(i8rev[:]), ALU.mult)
            E.drain()
            E.tensor_reduce(i1[:], V8(t0[:]), AX.X, ALU.max)
            E.drain()
            E.tensor_scalar(i1[:], i1[:], -1.0, 7.0, ALU.mult, ALU.add)
            E.drain()
            E.tensor_tensor(V8(m1[:]), V8(iota8[:]), B32(i1[:]), ALU.is_equal)
            E.tensor_scalar(V8(t0[:]), V8(m1[:]), -1.0, 1.0, ALU.mult, ALU.add)
            E.tensor_tensor(V8(t1_[:]), V8(raw[:]), V8(t0[:]), ALU.mult)
            E.drain()
            E.tensor_reduce(g2[:], V8(t1_[:]), AX.X, ALU.max)
            E.drain()
            E.tensor_tensor(V8(m2[:]), V8(t1_[:]), B32(g2[:]), ALU.is_equal)
            E.tensor_tensor(V8(t0[:]), V8(m2[:]), V8(i8rev[:]), ALU.mult)
            E.drain()
            E.tensor_reduce(i2[:], V8(t0[:]), AX.X, ALU.max)
            E.drain()
            E.tensor_scalar(i2[:], i2[:], -1.0, 7.0, ALU.mult, ALU.add)
            E.drain()
            E.tensor_tensor(V8(m2[:]), V8(iota8[:]), B32(i2[:]), ALU.is_equal)
            E.drain()
            return E.tensor_copy(m1b[:], m1[:])
        t_s3 = ctx.add("vector", soft2, waits=[t_s2, d["tok_gconsts"]], inc="ve", fence=True)

        def cums(which, msrc, mbsrc, extra_c1=None, extra_tok=None, after=()):
            t_a = ctx.add("tensor", lambda E: E.matmul(
                psG[:], u128[:], mbsrc[:], start=True, stop=True),
                waits=[t_s3] + list(after), inc="te")
            t_b = ctx.add("tensor", lambda E: E.matmul(
                psCnt[:], onescol[:], mbsrc[:], start=True, stop=True),
                waits=[t_s3] + list(after), inc="te")
            t_c = ctx.add("vector", lambda E: E.tensor_copy(cnt[:], psCnt[:]),
                          waits=[t_b], inc="ve", fence=True)
            toks = []
            for j in range(2):
                t_t = ctx.add("tensor", lambda E, j=j: E.transpose(
                    psTR[:, 0:1], cnt[0:1, j * 128:(j + 1) * 128], idf[0:1, 0:1]),
                    waits=[t_c] + toks, inc="te")
                t_e = ctx.add("vector", lambda E, j=j: E.tensor_copy(
                    cntT[:, j:j + 1], psTR[:, 0:1]), waits=[t_t], inc="ve", fence=True)
                toks = [t_e]
            t_cT = toks[0]
            def offmm(E):
                E.matmul(psOffs[:], u8s[:], cntT[:], start=True, stop=True)
                E.matmul(psTR[:, 0:1], a8s[:], cntT[:, 0:1], start=True, stop=True)
                if extra_c1 is not None:
                    E.matmul(psTR[:, 1:2], a8s[:], extra_c1[:, 0:1], start=True, stop=True)
                    return E.matmul(psTR[:, 2:3], a8s[:], extra_c1[:, 1:2], start=True, stop=True)
                return None
            w = [t_cT]
            if extra_tok is not None:
                w.append(extra_tok)
            t_o = ctx.add("tensor", offmm, waits=w, inc="te")
            if extra_c1 is None and "D_dbg_cnt" in d and GATING_ONLY:
                ctx.add("vector", lambda E: E.tensor_copy(sb["D_dbg"][:, 4:6], psOffs[:]),
                        waits=[t_o], inc="ve")
                ctx.add("vector", lambda E: E.tensor_copy(sb["D_dbg"][:, 6:8], cntT[:]),
                        waits=[t_o], inc="ve")
            def offcomb(E):
                E.tensor_copy(offc[:], psOffs[:])
                E.drain()
                E.tensor_tensor(offc[:, 1:2], offc[:, 1:2], psTR[:, 0:1], ALU.add)
                E.drain()
                if extra_c1 is not None:
                    E.tensor_tensor(offc[:, 0:1], offc[:, 0:1], psTR[:, 1:2], ALU.add)
                    E.tensor_tensor(offc[:, 1:2], offc[:, 1:2], psTR[:, 1:2], ALU.add)
                    E.drain()
                    E.tensor_tensor(offc[:, 0:1], offc[:, 0:1], psTR[:, 2:3], ALU.add)
                    E.tensor_tensor(offc[:, 1:2], offc[:, 1:2], psTR[:, 2:3], ALU.add)
                    E.drain()
                return E.nop()
            t_oe = ctx.add("vector", offcomb, waits=[t_o], inc="ve", fence=True)
            if extra_c1 is None and "D_dbg_cnt" in d:
                ctx.add("vector", lambda E: E.tensor_copy(sb["D_dbg"][:, 0:2], cntT[:]),
                        waits=[t_cT], inc="ve")
                ctx.add("vector", lambda E: E.tensor_copy(sb["D_dbgr"][0:1, 0:256], cnt[:]),
                        waits=[t_cT], inc="ve")
                ctx.add("vector", lambda E: E.tensor_copy(sb["D_dbg"][:, 2:4], offc[:]),
                        waits=[t_oe], inc="ve")
            toks = []
            for j in range(2):
                t_t = ctx.add("tensor", lambda E, j=j: E.transpose(
                    psTR[0:1, :], offc[:, j:j + 1], idf[:]), waits=[t_oe] + toks, inc="te")
                t_e = ctx.add("vector", lambda E, j=j: E.tensor_copy(
                    offr[0:1, j * 128:(j + 1) * 128], psTR[0:1, :]), waits=[t_t], inc="ve", fence=True)
                toks = [t_e]
            if extra_c1 is None and "D_dbg_cnt" in d:
                ctx.add("vector", lambda E: E.tensor_copy(sb["D_dbgr"][0:1, 256:512], offr[:]),
                        waits=toks, inc="ve")
            t_bb = ctx.add("tensor", lambda E: E.matmul(
                psGB[:], onesrowf[0:1, :], offr[:], start=True, stop=True),
                waits=toks, inc="te")
            t_be = ctx.add("vector", lambda E: E.tensor_copy(offb[:, 0:256], psGB[:]),
                           waits=[t_bb], inc="ve", fence=True)
            t_px = ctx.add("vector", lambda E: E.tensor_tensor(
                px[:, 0:256], psG[:], offb[:, 0:256], ALU.add), waits=[t_a, t_be], inc="ve")
            return t_px

        t_px1 = cums("m1", m1, m1b)
        def cap1(E):
            E.tensor_scalar(t0[:, 0:256], px[:, 0:256], 1024.0, None, ALU.is_lt)
            E.drain()
            E.tensor_tensor(m1[:, 0:256], m1[:, 0:256], t0[:, 0:256], ALU.mult)
            E.drain()
            E.tensor_tensor(t0[:, 0:256], px[:, 0:256], m1[:, 0:256], ALU.mult)
            E.drain()
            E.tensor_reduce(p1[:], V8(t0[:]), AX.X, ALU.add)
            E.tensor_reduce(k1[:], V8(m1[:]), AX.X, ALU.add)
            E.drain()
            E.tensor_tensor(g1[:], g1[:], k1[:], ALU.mult)
            E.drain()
            return E.tensor_copy(m1b[:], m1[:])
        t_c1 = ctx.add("vector", cap1, waits=[t_px1], inc="ve", fence=True)
        t_cb = ctx.add("tensor", lambda E: E.matmul(
            psCnt[:], onescol[:], m1b[:], start=True, stop=True), waits=[t_c1], inc="te")
        t_cc = ctx.add("vector", lambda E: E.tensor_copy(cnt[:], psCnt[:]),
                       waits=[t_cb], inc="ve", fence=True)
        toks = []
        for j in range(2):
            t_t = ctx.add("tensor", lambda E, j=j: E.transpose(
                psTR[:, 0:1], cnt[0:1, j * 128:(j + 1) * 128], idf[0:1, 0:1]),
                waits=[t_cc] + toks, inc="te")
            t_e = ctx.add("vector", lambda E, j=j: E.tensor_copy(
                sb["D_cntT1c"][:, j:j + 1], psTR[:, 0:1]), waits=[t_t], inc="ve", fence=True)
            toks = [t_e]
        extra_c1_tok = toks
        def slot1f(E):
            E.tensor_scalar(t0[:, 0:32], i1[:], 1024.0, None, ALU.mult)
            E.drain()
            E.tensor_tensor(t0[:, 0:32], t0[:, 0:32], p1[:], ALU.add)
            E.tensor_scalar(t0[:, 32:64], k1[:], -1048576.0, 1048576.0, ALU.mult, ALU.add)
            E.drain()
            E.tensor_tensor(t0[:, 0:32], t0[:, 0:32], t0[:, 32:64], ALU.add)
            E.drain()
            return E.tensor_copy(sl1[:], t0[:, 0:32])
        t_sl1 = ctx.add("vector", slot1f, waits=[t_c1], inc="ve", fence=True)

        t_m2b = ctx.add("vector", lambda E: E.tensor_copy(m1b[:], m2[:]),
                        waits=[t_cb, t_sl1], inc="ve", fence=True)
        t_px2 = cums("m2", m2, m1b, extra_c1=sb["D_cntT1c"],
                     extra_tok=extra_c1_tok[0], after=[t_sl1, extra_c1_tok[0], t_m2b])
        def cap2(E):
            E.tensor_scalar(t0[:, 0:256], px[:, 0:256], 1024.0, None, ALU.is_lt)
            E.drain()
            E.tensor_tensor(m2[:, 0:256], m2[:, 0:256], t0[:, 0:256], ALU.mult)
            E.drain()
            E.tensor_tensor(t0[:, 0:256], px[:, 0:256], m2[:, 0:256], ALU.mult)
            E.drain()
            E.tensor_reduce(p2[:], V8(t0[:]), AX.X, ALU.add)
            E.tensor_reduce(k1[:], V8(m2[:]), AX.X, ALU.add)
            E.drain()
            E.tensor_tensor(g2[:], g2[:], k1[:], ALU.mult)
            E.drain()
            E.tensor_scalar(t0[:, 0:32], i2[:], 1024.0, None, ALU.mult)
            E.drain()
            E.tensor_tensor(t0[:, 0:32], t0[:, 0:32], p2[:], ALU.add)
            E.tensor_scalar(t0[:, 32:64], k1[:], -1048576.0, 1048576.0, ALU.mult, ALU.add)
            E.drain()
            E.tensor_tensor(t0[:, 0:32], t0[:, 0:32], t0[:, 32:64], ALU.add)
            E.drain()
            return E.tensor_copy(sl2[:], t0[:, 0:32])
        t_sl2 = ctx.add("vector", cap2, waits=[t_px2], inc="ve", fence=True)
        if GATING_ONLY:
            ctx.dma("sync", d["D_dbg_cnt"], sb["D_dbg"][:], waits=[t_sl2])
            ctx.dma("sync", d["D_dbg_rows"], sb["D_dbgr"][:], waits=[t_sl2])
            ctx.dma("sync", d["D_dbg_px"], px[:], waits=[t_px1])
            ctx.dma("sync", d["D_dbg_sl1"], sl1[:], waits=[t_sl1])
            ctx.dma("sync", d["D_dbg_sl2"], sl2[:], waits=[t_sl2])
            ctx.dma("sync", d["D_dbg_g1"], g1[:], waits=[t_sl2])
            ctx.dma("sync", d["D_dbg_g2"], g2[:], waits=[t_sl2])
            ctx.dma("sync", d["D_dbg_i1"], i1[:], waits=[t_sl2])
            ctx.dma("sync", d["D_dbg_p1"], p1[:], waits=[t_sl2])
            return ctx.barrier_tok()
        if "D_dbg_sl1" in d:
            ctx.dma("sync", d["D_dbg_cnt"], sb["D_dbg"][:], waits=[t_sl2])
            ctx.dma("sync", d["D_dbg_rows"], sb["D_dbgr"][:], waits=[t_sl2])
            ctx.dma("sync", d["D_dbg_px"], px[:], waits=[t_px1])
            ctx.dma("sync", d["D_dbg_sl1"], sl1[:], waits=[t_sl1])
            ctx.dma("sync", d["D_dbg_sl2"], sl2[:], waits=[t_sl2])
            ctx.dma("sync", d["D_dbg_g1"], g1[:], waits=[t_sl2])
            ctx.dma("sync", d["D_dbg_g2"], g2[:], waits=[t_sl2])
            ctx.dma("sync", d["D_dbg_i1"], i1[:], waits=[t_sl2])
            ctx.dma("sync", d["D_dbg_p1"], p1[:], waits=[t_sl2])

        t_x3_use = [None, None]
        t_scat = None
        for t in range(NT):
            sl = t % 2
            w = [t_x3_use[sl]] if t_x3_use[sl] else []
            t_ld = ctx.dma("sync", x3t[:, sl * 1024:(sl + 1) * 1024], X3R[:, t, :], waits=w)
            ctx.add("gpsimd", lambda E, t=t, sl=sl: E.indirect_dma_start(
                XD, IndirectOffsetOnAxis(sl1[:, t:t + 1], 0),
                x3t[:, sl * 1024:(sl + 1) * 1024], None,
                bounds_check=BCREG[0], oob_is_err=False),
                waits=[t_ld, t_sl1, t_sl2, d["tok_xzero"]], inc="dge", n=16)
            t_scat = ctx.add("gpsimd", lambda E, t=t, sl=sl: E.indirect_dma_start(
                XD, IndirectOffsetOnAxis(sl2[:, t:t + 1], 0),
                x3t[:, sl * 1024:(sl + 1) * 1024], None,
                bounds_check=BCREG[0], oob_is_err=False),
                waits=[], inc="dge", n=16)
            t_x3_use[sl] = t_scat
    t_gate_end = ctx.barrier_tok()

    NHC = 11
    hcw = [512] * 10 + [344]
    hboff = [0, 4, 8, 12, 16, 20, 24, 28, 32, 36, 40]
    hbn = [4] * 10 + [3]
    hbw_last = {10: [128, 128, 88]}
    with (
        nc.psum_tensor("psZ0a", [128, 512], F32) as psZ0a,
        nc.psum_tensor("psZ1a", [128, 512], F32) as psZ1a,
        nc.psum_tensor("psZ0b", [128, 512], F32) as psZ0b,
        nc.psum_tensor("psZ1b", [128, 512], F32) as psZ1b,
        nc.psum_tensor("psDa", [128, 512], F32) as psDa,
        nc.psum_tensor("psDb", [128, 512], F32) as psDb,
        nc.psum_tensor("psTx", [128, 128], BF16) as psTx,
        nc.psum_tensor("psTy", [128, 128], BF16) as psTy,
    ):
        t_wo_use = None
        t_hc_use = [None, None]
        t_xeT_use = None
        t_hid_use = [None, None]
        t_psT_ev = {0: None, 1: None}
        t_z_ev = {0: None, 1: None}
        t_hid_dma = None
        t_gel_use = None
        t_d_ev = {0: None, 1: None}
        t_yb_use = [None, None]
        t_w01c = {}
        t_w01c_use = {}
        t_wo_tok = None
        t_comb_gather = None

        for e in range(E1):
            t_xe_tr = None
            for ti in range(8):
                w = []
                if e == 0 and ti == 0:
                    w = list(t_gate_end)
                if t_xeT_use and ti == 0:
                    w.append(t_xeT_use)
                if t_xe_tr:
                    w.append(t_xe_tr)
                t_ldx = ctx.dma("sync", xe[:],
                                XD[e * 1024 + ti * 128: e * 1024 + (ti + 1) * 128, :],
                                waits=w + ([t_scat] if t_scat else []))
                for c in range(NC):
                    tp = c % 2
                    pst = psTx if tp == 0 else psTy
                    w2 = [t_ldx]
                    if t_psT_ev[tp]:
                        w2.append(t_psT_ev[tp])
                    t_tr = ctx.add("tensor", lambda E, c=c, pst=pst: E.transpose(
                        pst[:], xe[:, c * 128:(c + 1) * 128], idb[:]), waits=w2, inc="te")
                    t_ev = ctx.add("vector", lambda E, c=c, ti=ti, pst=pst: E.tensor_copy(
                        xeT[:, c * 1024 + ti * 128: c * 1024 + (ti + 1) * 128], pst[:]),
                        waits=[t_tr], inc="ve")
                    t_psT_ev[tp] = t_ev
                    t_xe_tr = t_ev
            w = [t_wo_use] if t_wo_use else []
            ctx.dma("gpsimd",
                    wor[:].rearrange("p (hb m) -> p hb m", m=1024)[:, 0:42, :],
                    WO[e][0:5376, :].rearrange("(hb p) m -> p hb m", p=128),
                    waits=w)
            t_wo_tok = ctx.dma("gpsimd", wor[0:88, 42 * 1024:43 * 1024],
                               WO[e][5376:5464, :], waits=[])
            for hci in range(NHC):
                sl = hci % 2
                wv = [t_w01c_use.get((sl,))] if t_w01c_use.get((sl,)) else []
                t_l0 = ctx.dma("gpsimd",
                               w0b[:].rearrange("p (s c h) -> p s c h", s=2, h=512)[:, sl, :, 0:hcw[hci]],
                               W0[e].rearrange("(c p) h -> p c h", p=128)[:, :, hci * 512: hci * 512 + hcw[hci]],
                               waits=wv)
                t_l1 = ctx.dma("gpsimd",
                               w1b[:].rearrange("p (s c h) -> p s c h", s=2, h=512)[:, sl, :, 0:hcw[hci]],
                               W1[e].rearrange("(c p) h -> p c h", p=128)[:, :, hci * 512: hci * 512 + hcw[hci]],
                               waits=wv)
                for hb in range(hbn[hci]):
                    hbw = 128 if (hci < 10 or hb < 2) else 88
                    for cc in range(2):
                        zp = (hb * 2 + cc) % 2
                        pz0 = psZ0a if zp == 0 else psZ0b
                        pz1 = psZ1a if zp == 0 else psZ1b
                        w3 = [t_l0, t_l1, t_xe_tr]
                        if t_z_ev[zp]:
                            w3.append(t_z_ev[zp])
                        def zmm(E, sl=sl, hb=hb, cc=cc, hbw=hbw, pz0=pz0, pz1=pz1):
                            for c in range(NC):
                                E.matmul(pz0[0:hbw, 0:512],
                                         w0b[:, (sl * 8 + c) * 512 + hb * 128:
                                             (sl * 8 + c) * 512 + hb * 128 + hbw],
                                         xeT[:, c * 1024 + cc * 512: c * 1024 + (cc + 1) * 512],
                                         start=(c == 0), stop=(c == NC - 1))
                            last = None
                            for c in range(NC):
                                last = E.matmul(pz1[0:hbw, 0:512],
                                                w1b[:, (sl * 8 + c) * 512 + hb * 128:
                                                    (sl * 8 + c) * 512 + hb * 128 + hbw],
                                                xeT[:, c * 1024 + cc * 512: c * 1024 + (cc + 1) * 512],
                                                start=(c == 0), stop=(c == NC - 1))
                            return last
                        t_z = ctx.add("tensor", zmm, waits=w3, inc="te")
                        w4 = [t_z]
                        if t_gel_use:
                            w4.append(t_gel_use)
                        t_g = ctx.add("scalar", lambda E, pz0=pz0, hbw=hbw: E.activation(
                            gel[0:hbw, :], pz0[0:hbw, 0:512], AF.Gelu_apprx_tanh),
                            waits=w4, inc="se")
                        hsl = (hb * 2 + cc) % 2
                        w5 = [t_g]
                        if t_hid_use[hsl]:
                            w5.append(t_hid_use[hsl])
                        t_h = ctx.add("vector", lambda E, pz1=pz1, hbw=hbw, hsl=hsl: E.tensor_tensor(
                            hid[0:hbw, hsl * 512: hsl * 512 + 512], gel[0:hbw, :],
                            pz1[0:hbw, 0:512], ALU.mult), waits=w5, inc="ve")
                        t_gel_use = t_h
                        t_z_ev[zp] = t_h
                        hbg = hboff[hci] + hb
                        w6 = [t_h]
                        if t_hid_dma:
                            w6.append(t_hid_dma)
                        t_hid_dma = ctx.dma("sync",
                                            HIDD[hbg * 128: hbg * 128 + hbw, cc * 512:(cc + 1) * 512],
                                            hid[0:hbw, hsl * 512: hsl * 512 + 512],
                                            waits=w6)
                        t_hid_use[hsl] = t_hid_dma
                t_w01c_use[(sl,)] = t_z
            for cb in range(8):
                csl = cb % 2
                w = [t_hc_use[csl]] if t_hc_use[csl] else []
                t_lhc = ctx.dma("sync",
                                hc[:].rearrange("p (s hb cc) -> p s hb cc", s=2, cc=128)[:, csl],
                                HIDD.rearrange("(hb p) c -> p hb c", p=128)[:, :, cb * 128:(cb + 1) * 128],
                                waits=w + [t_hid_dma])
                for mc in range(2):
                    dp = (cb * 2 + mc) % 2
                    pd = psDa if dp == 0 else psDb
                    w2 = [t_lhc, t_wo_tok]
                    if t_d_ev[dp]:
                        w2.append(t_d_ev[dp])
                    def dmm(E, csl=csl, mc=mc, pd=pd):
                        last = None
                        for hb in range(43):
                            hbw = 128 if hb < 42 else 88
                            last = E.matmul(pd[:],
                                            hc[0:hbw, (csl * 43 + hb) * 128:(csl * 43 + hb) * 128 + 128],
                                            wor[0:hbw, hb * 1024 + mc * 512: hb * 1024 + (mc + 1) * 512],
                                            start=(hb == 0), stop=(hb == 42))
                        return last
                    t_d = ctx.add("tensor", dmm, waits=w2, inc="te")
                    t_hc_use[csl] = t_d
                    ysl = (cb * 2 + mc) % 2
                    w3 = [t_d]
                    if t_yb_use[ysl]:
                        w3.append(t_yb_use[ysl])
                    t_ye = ctx.add("vector", lambda E, ysl=ysl, pd=pd: E.tensor_copy(
                        yb[:, ysl * 512:(ysl + 1) * 512], pd[:]), waits=w3, inc="ve")
                    t_d_ev[dp] = t_ye
                    t_yd = ctx.dma("sync",
                                   YD[e * 1024 + cb * 128: e * 1024 + (cb + 1) * 128,
                                      mc * 512:(mc + 1) * 512],
                                   yb[:, ysl * 512:(ysl + 1) * 512], waits=[t_ye])
                    t_yb_use[ysl] = t_yd
            t_xeT_use = t_z
            t_wo_use = t_d
        t_experts_done = ctx.barrier_tok()

    with nc.psum_tensor("psCb", [128, 128], F32) as _unused:
        t_AB_use = None
        prev_ln = None
        t_out_dma = [None, None]
        t_rln_prev = None
        t_x3_use2 = [None, None]
        t_memA = ctx.add("gpsimd", lambda E: E.memset(Ab[:], 0.0), inc="ge")
        t_memB = ctx.add("gpsimd", lambda E: E.memset(Bb[:], 0.0), inc="ge")
        for t in range(NT):
            sl = t % 2
            w = [t_x3_use2[sl]] if t_x3_use2[sl] else []
            t_ldr = ctx.dma("sync", x3t[:, sl * 1024:(sl + 1) * 1024], X3R[:, t, :], waits=w)
            w2 = list(t_experts_done) if t == 0 else []
            w2 += [t_memA, t_memB]
            if t_AB_use:
                w2.append(t_AB_use)
            ctx.add("gpsimd", lambda E, t=t: E.indirect_dma_start(
                Ab[:], None, YD, IndirectOffsetOnAxis(sl1[:, t:t + 1], 0),
                bounds_check=BCREG[0], oob_is_err=False), waits=w2, inc="dge", n=16)
            t_ga = ctx.add("gpsimd", lambda E, t=t: E.indirect_dma_start(
                Bb[:], None, YD, IndirectOffsetOnAxis(sl2[:, t:t + 1], 0),
                bounds_check=BCREG[0], oob_is_err=False), waits=[], inc="dge", n=16)
            def comb(E, t=t, sl=sl):
                E.tensor_scalar(mo[:], Ab[:], g1[:, t:t + 1], None, ALU.mult)
                E.drain()
                E.scalar_tensor_tensor(mo[:], Bb[:], g2[:, t:t + 1], mo[:], ALU.mult, ALU.add)
                E.drain()
                return E.tensor_tensor(rln[:], mo[:], x3t[:, sl * 1024:(sl + 1) * 1024], ALU.add)
            w3 = [t_ga, t_ldr]
            if t_rln_prev:
                w3.append(t_rln_prev)
            t_res = ctx.add("vector", comb, waits=w3, inc="ve")
            t_AB_use = t_res
            t_x3_use2[sl] = t_res
            def lnf(E):
                E.bn_stats(bnst[:, 0:6], rln[:, 0:512])
                E.bn_stats(bnst[:, 6:12], rln[:, 512:1024])
                E.drain()
                E.bn_aggr(mv[:], bnst[:].rearrange("p (g k) -> p g k", k=6))
                E.drain()
                return E.nop()
            wbn = [t_res]
            if prev_ln:
                wbn.append(prev_ln)
            t_bn = ctx.add("vector", lnf, waits=wbn, inc="ve")
            def rstdf(E):
                E.activation(lnt[:, 0:1], mv[:, 1:2], AF.Ln, bias=1e-5)
                return E.activation(lnt[:, 1:2], lnt[:, 0:1], AF.Exp, scale=-0.5)
            t_rs = ctx.add("scalar", rstdf, waits=[t_bn], inc="se")
            osl = t % 2
            w4 = [t_rs]
            if t_out_dma[osl]:
                w4.append(t_out_dma[osl])
            t_ap = ctx.add("vector", lambda E, osl=osl: E.tensor_scalar(
                outb[:, osl * 1024:(osl + 1) * 1024], rln[:], mv[:, 0:1], lnt[:, 1:2],
                ALU.subtract, ALU.mult), waits=w4, inc="ve")
            prev_ln = t_ap
            t_rln_prev = t_ap
            t_out_dma[osl] = ctx.dma("sync", OUT.rearrange("(t p) dd -> p t dd", p=128)[:, t, :],
                                     outb[:, osl * 1024:(osl + 1) * 1024], waits=[t_ap])
    return ctx.barrier_tok()


def build_passD_program():
    import contextlib as _cl
    nc = bass.Bass("TRN2", target_bir_lowering=False)
    d = {}
    d["x3d_in"] = nc.dram_tensor("x3d_in", [T, DM], BF16, kind="ExternalInput").ap()
    d["logitsd_in"] = nc.dram_tensor("logitsd_in", [T, 8], F32, kind="ExternalInput").ap()
    d["l1_w0"] = nc.dram_tensor("l1_w0", [E1, DM, HFF], F32, kind="ExternalInput").ap()
    d["l1_w1"] = nc.dram_tensor("l1_w1", [E1, DM, HFF], F32, kind="ExternalInput").ap()
    d["l1_wo"] = nc.dram_tensor("l1_wo", [E1, HFF, DM], F32, kind="ExternalInput").ap()
    for k, v in CONSTS.items():
        dt = BF16 if v.dtype == BF else (F32 if v.dtype == np.float32 else U32)
        d[k] = nc.dram_tensor(k, list(v.shape), dt, kind="ExternalInput").ap()
    d["Xd"] = nc.dram_tensor("Xd", [E1 * CAP, DM], BF16,
                             kind="ExternalOutput" if DEBUG_D else "Internal").ap()
    d["Yd"] = nc.dram_tensor("Yd", [E1 * CAP, DM], BF16,
                             kind="ExternalOutput" if DEBUG_D else "Internal").ap()
    d["Hidd"] = nc.dram_tensor("Hidd", [5504, DM], BF16, kind="Internal").ap()
    d["out"] = nc.dram_tensor("out", [T, DM], F32, kind="ExternalOutput").ap()
    if DEBUG_D:
        for nm, dt in (("D_dbg_sl1", U32), ("D_dbg_sl2", U32), ("D_dbg_g1", F32),
                       ("D_dbg_g2", F32), ("D_dbg_i1", F32), ("D_dbg_p1", F32)):
            d[nm] = nc.dram_tensor(nm, [128, 32], dt, kind="ExternalOutput").ap()
        d["D_dbg_cnt"] = nc.dram_tensor("D_dbg_cnt", [128, 8], F32, kind="ExternalOutput").ap()
        d["D_dbg_rows"] = nc.dram_tensor("D_dbg_rows", [1, 512], F32, kind="ExternalOutput").ap()
        d["D_dbg_px"] = nc.dram_tensor("D_dbg_px", [128, 256], F32, kind="ExternalOutput").ap()
        d["D_dbg_u8s"] = nc.dram_tensor("D_dbg_u8s", [128, 128], F32, kind="ExternalOutput").ap()

    ctx = Ctx(nc)
    with _cl.ExitStack() as stack:
        ctx.make_sems(stack)
        sb, ps = {}, {}
        alloc_common(nc, stack, sb)
        sb["c_idf"] = stack.enter_context(nc.sbuf_tensor("sb_c_idf", [128, 128], F32))
        for nm, shape, dt in (("c_u128b", [128, 128], BF16), ("c_u8sb", [128, 128], BF16),
                              ("c_a8sb", [128, 128], BF16), ("c_iota8t", [128, 256], F32),
                              ("c_i8revt", [128, 256], F32)):
            sb[nm] = stack.enter_context(nc.sbuf_tensor("sb_" + nm, shape, dt))
        alloc_passD(nc, stack, sb)
        blk = stack.enter_context(nc.Block())
        load_common(ctx, d, sb)
        ctx.dma("sync", sb["c_idf"][:], d["c_id128f"])
        toks = []
        for nm in ("c_u128b", "c_u8sb", "c_a8sb", "c_iota8t", "c_i8revt"):
            toks.append(ctx.dma("sync", sb[nm][:], d[nm]))
        d["tok_gconsts"] = toks[-1]
        t_z0 = ctx.add("gpsimd", lambda E: E.memset(sb["D_xe"][:], 0.0), inc="ge")
        t_xz = None
        for r in range(64):
            t_xz = ctx.dma("sync", d["Xd"][r * 128:(r + 1) * 128, :], sb["D_xe"][:],
                           waits=[t_z0])
        d["tok_xzero"] = t_xz
        emit_passD(ctx, d, sb, stack)
        fin = ctx.barrier_tok()
        ctx.add("sync", lambda E: None, waits=fin)
        for eng in Ctx.ENGS:
            opsl = ctx.q[eng]
            getattr(blk, eng)(lambda E, opsl=opsl: [f(E) for f in opsl])
    return nc



def build_fused_program():
    import contextlib as _cl
    nc = bass.Bass("TRN2", target_bir_lowering=False)
    d = {}
    d["x"] = nc.dram_tensor("x", [T, DM], F32, kind="ExternalInput").ap()
    d["l0_wqkv"] = nc.dram_tensor("l0_wqkv", [DM, 3 * DM], F32, kind="ExternalInput").ap()
    d["l0_wout"] = nc.dram_tensor("l0_wout", [DM, DM], F32, kind="ExternalInput").ap()
    d["l0_wg"] = nc.dram_tensor("l0_wg", [DM, DM], F32, kind="ExternalInput").ap()
    d["l0_bg"] = nc.dram_tensor("l0_bg", [DM], F32, kind="ExternalInput").ap()
    d["l0_we"] = nc.dram_tensor("l0_we", [E0, DM, DM], F32, kind="ExternalInput").ap()
    d["l0_be"] = nc.dram_tensor("l0_be", [E0, DM], F32, kind="ExternalInput").ap()
    d["l1_wqkv"] = nc.dram_tensor("l1_wqkv", [3 * DM, DM], F32, kind="ExternalInput").ap()
    d["l1_wout"] = nc.dram_tensor("l1_wout", [DM, DM], F32, kind="ExternalInput").ap()
    d["l1_bout"] = nc.dram_tensor("l1_bout", [DM], F32, kind="ExternalInput").ap()
    d["l1_ng"] = nc.dram_tensor("l1_ng", [1, 16, 1, 1], F32, kind="ExternalInput").ap()
    d["l1_nb"] = nc.dram_tensor("l1_nb", [1, 16, 1, 1], F32, kind="ExternalInput").ap()
    d["l1_rw"] = nc.dram_tensor("l1_rw", [DM, 8], F32, kind="ExternalInput").ap()
    d["l1_w0"] = nc.dram_tensor("l1_w0", [E1, DM, HFF], F32, kind="ExternalInput").ap()
    d["l1_w1"] = nc.dram_tensor("l1_w1", [E1, DM, HFF], F32, kind="ExternalInput").ap()
    d["l1_wo"] = nc.dram_tensor("l1_wo", [E1, HFF, DM], F32, kind="ExternalInput").ap()
    for k, v in CONSTS.items():
        dt = BF16 if v.dtype == BF else (F32 if v.dtype == np.float32 else U32)
        d[k] = nc.dram_tensor(k, list(v.shape), dt, kind="ExternalInput").ap()
    d["x1d"] = nc.dram_tensor("x1d", [T, DM], BF16, kind="Internal").ap()
    d["x1td"] = nc.dram_tensor("x1td", [DM, T], BF16, kind="Internal").ap()
    d["x2d"] = nc.dram_tensor("x2d", [T, DM], BF16, kind="Internal").ap()
    d["x2td"] = nc.dram_tensor("x2td", [DM, T], BF16, kind="Internal").ap()
    d["x3d"] = nc.dram_tensor("x3d", [T, DM], BF16, kind="Internal").ap()
    d["logitsd"] = nc.dram_tensor("logitsd", [T, 8], F32, kind="Internal").ap()
    d["Xd"] = nc.dram_tensor("Xd", [E1 * CAP, DM], BF16, kind="Internal").ap()
    d["Yd"] = nc.dram_tensor("Yd", [E1 * CAP, DM], BF16, kind="Internal").ap()
    d["Hidd"] = nc.dram_tensor("Hidd", [5504, DM], BF16, kind="Internal").ap()
    d["out"] = nc.dram_tensor("out", [T, DM], F32, kind="ExternalOutput").ap()
    d["x1d_in"] = d["x1d"]; d["x1td_in"] = d["x1td"]
    d["x2d_in"] = d["x2d"]; d["x2td_in"] = d["x2td"]
    d["x3d_in"] = d["x3d"]; d["logitsd_in"] = d["logitsd"]

    ctx = Ctx(nc)
    with _cl.ExitStack() as stack:
        ctx.make_sems(stack)
        sb = {}
        alloc_common(nc, stack, sb)
        sb["c_idf"] = stack.enter_context(nc.sbuf_tensor("sb_c_idf", [128, 128], F32))
        sb["c_bc2f"] = stack.enter_context(nc.sbuf_tensor("sb_c_bc2f", [2, 128], F32))
        for nm, shape, dt in (("c_u128b", [128, 128], BF16), ("c_u8sb", [128, 128], BF16),
                              ("c_a8sb", [128, 128], BF16), ("c_iota8t", [128, 256], F32),
                              ("c_i8revt", [128, 256], F32)):
            sb[nm] = stack.enter_context(nc.sbuf_tensor("sb_" + nm, shape, dt))
        blk = stack.enter_context(nc.Block())
        load_common(ctx, d, sb)
        ctx.dma("sync", sb["c_idf"][:], d["c_id128f"])
        d["tok_bc2f"] = ctx.dma("sync", sb["c_bc2f"][:], d["c_bc2f"])
        toks = []
        for nm in ("c_u128b", "c_u8sb", "c_a8sb", "c_iota8t", "c_i8revt"):
            toks.append(ctx.dma("sync", sb[nm][:], d[nm]))
        d["tok_gconsts"] = toks[-1]

        def fence():
            bar = ctx.barrier_tok()
            ftoks = []
            for eng, ch in (("tensor", "te"), ("vector", "ve"), ("scalar", "se"),
                            ("gpsimd", "ge"), ("sync", None)):
                if ch is None:
                    ftoks.append(ctx.add(eng, lambda E: None, waits=bar))
                else:
                    ftoks.append(ctx.add(eng, lambda E: E.nop(), waits=bar, inc=ch))
            return ctx.barrier_tok()

        with _cl.ExitStack() as stA:
            sbA, psA_ = dict(sb), {}
            alloc_passA(nc, stA, sbA, psA_)
            alloc_psum_A(nc, stA, psA_)
            emit_passA(ctx, d, sbA, psA_)
        fence()
        with _cl.ExitStack() as stB:
            sbB, psB_ = dict(sb), {}
            alloc_passB(nc, stB, sbB, psB_)
            alloc_psum_B(nc, stB, psB_)
            emit_passB(ctx, d, sbB, psB_)
        fence()
        with _cl.ExitStack() as stC:
            sbC = dict(sb)
            alloc_passC(nc, stC, sbC)
            emit_passC(ctx, d, sbC, stC)
        fence()
        with _cl.ExitStack() as stD:
            sbD = dict(sb)
            alloc_passD(nc, stD, sbD)
            t_z0 = ctx.add("gpsimd", lambda E: E.memset(sbD["D_xe"][:], 0.0), inc="ge")
            t_xz = None
            for r in range(64):
                t_xz = ctx.dma("sync", d["Xd"][r * 128:(r + 1) * 128, :], sbD["D_xe"][:],
                               waits=[t_z0])
            d["tok_xzero"] = t_xz
            emit_passD(ctx, d, sbD, stD)
        fin = ctx.barrier_tok()
        ctx.add("sync", lambda E: None, waits=fin)
        for eng in Ctx.ENGS:
            opsl = ctx.q[eng]
            getattr(blk, eng)(lambda E, opsl=opsl: [f(E) for f in opsl])
    return nc


def kernel(**inputs):
    consts = consts_inmap()
    f32 = lambda k: np.ascontiguousarray(np.asarray(inputs[k], np.float32))
    X = f32("L0_attn_x")
    ncF = _get_prog("F")
    maps = [dict(x=np.ascontiguousarray(X[b]),
                 l0_wqkv=f32("L0_attn_Wqkv"), l0_wout=f32("L0_attn_Wout"),
                 l0_wg=f32("L0_ffn_Wg"), l0_bg=f32("L0_ffn_bg"),
                 l0_we=f32("L0_ffn_We"), l0_be=f32("L0_ffn_be"),
                 l1_wqkv=f32("L1_attn_w_qkv"), l1_wout=f32("L1_attn_w_out"),
                 l1_bout=f32("L1_attn_b_out"), l1_ng=f32("L1_attn_norm_g"),
                 l1_nb=f32("L1_attn_norm_b"), l1_rw=f32("L1_ffn_router_w"),
                 l1_w0=f32("L1_ffn_w0_kernel"), l1_w1=f32("L1_ffn_w1_kernel"),
                 l1_wo=f32("L1_ffn_wo_kernel"), **consts) for b in range(8)]
    res = run_bass_kernel_spmd(ncF, maps, core_ids=list(range(8))).results
    return np.stack([np.asarray(res[b]["out"], np.float32) for b in range(8)], axis=0)


# version 03
# speedup vs baseline: 1.0335x; 1.0259x over previous
import math
import numpy as np
import ml_dtypes
import concourse.bass as bass
import concourse.mybir as mybir
from concourse.bass_utils import run_bass_kernel_spmd
from concourse.bass_core_rust import IndirectOffsetOnAxis

F32 = mybir.dt.float32
BF16 = mybir.dt.bfloat16
U32 = mybir.dt.uint32
AF = mybir.ActivationFunctionType
ALU = mybir.AluOpType
AX = mybir.AxisListType
BF = ml_dtypes.bfloat16

T = 4096
DM = 1024
NT = T // 128
NC = DM // 128
NH = 16
HD = 64
WIN = 512
E0 = 8
E1 = 8
HFF = 5464
CAP = 1024
NHB = (HFF + 127) // 128
LASTHB = HFF - 128 * (NHB - 1)



def make_consts():
    c = {}
    c["c_id128b"] = np.eye(128, dtype=BF)
    c["c_id128f"] = np.eye(128, dtype=np.float32)
    c["c_ones_col"] = np.ones((128, 1), dtype=BF)
    c["c_ones_row"] = np.ones((1, 128), dtype=BF)
    c["c_ones_rowf"] = np.ones((1, 128), dtype=np.float32)
    iu = np.arange(128)
    c["c_u128"] = (iu[:, None] < iu[None, :]).astype(BF)
    cc, pp = np.meshgrid(iu, iu, indexing="ij")
    c["c_u8s"] = (((cc % 8) == (pp % 8)) & (cc < pp)).astype(np.float32)
    c["c_a8s"] = ((cc % 8) == (pp % 8)).astype(np.float32)
    c["c_iota8"] = np.tile(np.arange(8, dtype=np.float32), (128, 1))
    c["c_iota8t"] = np.tile(np.arange(8, dtype=np.float32), (128, 32))
    c["c_i8revt"] = np.tile(7.0 - np.arange(8, dtype=np.float32), (128, 32))
    c["c_u128b"] = (iu[:, None] < iu[None, :]).astype(BF)
    c["c_u8sb"] = (((cc % 8) == (pp % 8)) & (cc < pp)).astype(BF)
    c["c_a8sb"] = ((cc % 8) == (pp % 8)).astype(BF)
    p = np.arange(128)[:, None]
    f = np.arange(256)[None, :]
    m0 = (f < p)
    m1 = (f < p + 128)
    m4 = (p <= f)
    m5 = (p + 128 <= f)
    c["c_amask"] = np.concatenate(
        [m.astype(BF) for m in (m0, m1, m4, m5)], axis=1)
    half = HD // 2
    theta = (1.0 / 10000.0 ** (np.arange(half, dtype=np.float64) / half))
    pos = np.arange(T, dtype=np.float64)
    fr = pos[:, None] * theta[None, :]
    fr2 = np.concatenate([fr, fr], axis=1)
    c["c_rcos"] = np.cos(fr2).astype(BF)
    c["c_rsin"] = np.sin(fr2).astype(BF)
    iu2 = np.arange(128)
    c["c_bc2f"] = np.stack([(iu2 < 64), (iu2 >= 64)]).astype(np.float32)
    c["c_fvals"] = np.tile(np.array([[1e-5, 1e-12, 0.0, 0.0]], dtype=np.float32), (128, 1))
    return c


CONSTS = make_consts()



class Ctx:

    ENGS = ("tensor", "vector", "scalar", "gpsimd", "sync")

    def __init__(self, nc):
        self.nc = nc
        self.q = {e: [] for e in self.ENGS}
        self.sems = {}
        self.cnt = {}
        self._sem_cms = []

    def make_sems(self, stack):
        for name in ("te", "ve", "se", "ge", "dsy", "dge", "dse", "dve"):
            self.sems[name] = stack.enter_context(self.nc.semaphore(name))
            self.cnt[name] = 0

    def add(self, eng, body, waits=(), inc=None, n=1, fence=False):
        waits = tuple(waits)
        tok = None
        if inc is not None:
            self.cnt[inc] += n
            tok = (inc, self.cnt[inc])
        sems = self.sems
        def closure(E, waits=waits, inc=inc, n=n, body=body, fence=fence):
            for (chn, v) in waits:
                E.wait_ge(sems[chn], v)
            last = body(E)
            if inc is not None:
                if fence:
                    E.drain()
                    last = E.nop()
                if last is None:
                    last = E.nop()
                last.then_inc(sems[inc], n)
        self.q[eng].append(closure)
        return tok

    def dma(self, eng, out, in_, waits=(), **kw):
        ch = {"sync": "dsy", "gpsimd": "dge", "scalar": "dse", "vector": "dve"}[eng]
        return self.add(eng, lambda E: E.dma_start(out=out, in_=in_, **kw),
                        waits=waits, inc=ch, n=16)

    def barrier_tok(self):
        return [(chn, v) for chn, v in self.cnt.items() if v > 0]


def ts(i, n):
    return slice(i * n, (i + 1) * n)



DEBUG_A = False

def emit_passA(ctx, d, sb, ps):
    nc = ctx.nc
    X, WQKV, WOUT = d["x"], d["l0_wqkv"], d["l0_wout"]
    X1D, X1TD = d["x1d"], d["x1td"]

    wq = sb["A_wq"]
    wo = sb["A_wo"]
    idb = sb["c_idb"]
    onescol = sb["c_onescol"]
    onesrow = sb["c_onesrow"]
    onesrowf = sb["c_onesrowf"]
    amask = sb["A_amask"]
    rcos = sb["A_rcos"]
    rsin = sb["A_rsin"]
    xf = sb["A_xf"]
    xbf = sb["A_xbf"]
    xT = sb["A_xT"]
    qkb = sb["A_qkb"]
    ropet = sb["A_ropet"]
    qT = sb["A_qT"]
    kT = sb["A_kT"]
    vr = sb["A_vr"]
    pt = sb["A_pt"]
    densb = sb["A_densb"]
    recb = sb["A_recb"]
    bcsb = sb["A_bcsb"]
    attnT = sb["A_attnT"]
    rln = sb["A_rln"]
    bnst = sb["A_bnst"]
    mv = sb["A_mv"]
    lnt = sb["A_lnt"]
    x1b = sb["A_x1b"]
    x1t = sb["A_x1t"]

    psA, psB = ps["psA"], ps["psB"]
    psTa, psTb = ps["psTa"], ps["psTb"]
    psS1 = ps["psS1"]
    psS2 = ps["psS2"]
    psAV = ps["psAV"]

    t_w = ctx.dma("gpsimd",
                  wq[:].rearrange("p (c o) -> p c o", o=3072),
                  WQKV.rearrange("(c p) o -> p c o", p=128))
    t_wo = ctx.dma("gpsimd",
                   wo[:].rearrange("p (c o) -> p c o", o=1024),
                   WOUT.rearrange("(c p) o -> p c o", p=128))
    t_ct = []
    t_ct.append(ctx.dma("sync", amask[:], d["c_amask"]))
    t_ct.append(ctx.dma("sync",
                        rcos[:].rearrange("p (t dd) -> p t dd", dd=64),
                        d["c_rcos"].rearrange("(t p) dd -> p t dd", p=128)))
    t_ct.append(ctx.dma("sync",
                        rsin[:].rearrange("p (t dd) -> p t dd", dd=64),
                        d["c_rsin"].rearrange("(t p) dd -> p t dd", p=128)))

    last_qkv_psum_read = {0: None, 1: None}
    last_xT_use = None
    last_qT_use = None
    last_kslot_use = {0: None, 1: None}
    last_vslot_use = {0: None, 1: None}
    last_xf_use = {0: None, 1: None}
    last_psT_evac = {0: None, 1: None}
    last_attnT_use = None
    prev_ln_done = None
    t_exp = None
    t_avn = None
    t_x1t_dma = None
    t_x1b_dma = None
    t_dav_prev = None
    t_de_prev = None
    t_rc_prev = None
    t_bc_prev = None
    t_bce_prev = None

    XR = X.rearrange("(t p) dd -> p t dd", p=128)

    for s in range(8):
        sl = s % 2
        w = [t for t in (last_xf_use[sl],) if t]
        t_x = ctx.dma("sync", xf[:].rearrange("p (a t dd) -> p a t dd", a=2, dd=1024)[:, sl],
                      XR[:, s * 4:(s + 1) * 4, :], waits=w)

        w = [t_x]
        if last_xT_use:
            w.append(last_xT_use)
        t_cast = ctx.add("vector", lambda E, s=s, sl=sl: E.tensor_copy(
            xbf[:], xf[:, sl * 4096:(sl + 1) * 4096]), waits=w, inc="ve")

        t_ev = None
        for ti in range(4):
            for c in range(NC):
                pp = (ti * NC + c) % 2
                pst = psTa if pp == 0 else psTb
                w = [t_cast]
                if last_psT_evac[pp]:
                    w.append(last_psT_evac[pp])
                t_tr = ctx.add("tensor", lambda E, ti=ti, c=c, pst=pst: E.transpose(
                    pst[:], xbf[:, ti * 1024 + c * 128: ti * 1024 + (c + 1) * 128], idb[:]),
                    waits=w, inc="te")
                t_ev = ctx.add("vector", lambda E, ti=ti, c=c, pst=pst: E.tensor_copy(
                    xT[:, c * 512 + ti * 128: c * 512 + (ti + 1) * 128], pst[:]),
                    waits=[t_tr], inc="ve")
                last_psT_evac[pp] = t_ev
        t_xT = t_ev

        t_rope_done = None
        for oc in range(6):
            for ti in range(4):
                pp = (oc * 4 + ti) % 2
                pq = psA if pp == 0 else psB
                w = [t_xT, t_w]
                if last_qkv_psum_read[pp]:
                    w.append(last_qkv_psum_read[pp])
                def qkvmm(E, oc=oc, ti=ti, pq=pq):
                    last = None
                    for c in range(NC):
                        last = E.matmul(
                            pq[:],
                            xT[:, c * 512 + ti * 128: c * 512 + (ti + 1) * 128],
                            wq[:, c * 3072 + oc * 512: c * 3072 + (oc + 1) * 512],
                            start=(c == 0), stop=(c == NC - 1))
                    return last
                t_mm = ctx.add("tensor", qkvmm, waits=w, inc="te")

                if oc < 4:
                    gt = s * 4 + ti
                    qk = qkb[:, (pp) * 512:(pp + 1) * 512]
                    def rope(E, pq=pq, gt=gt, qk=qk):
                        pA = pq[:].rearrange("p (h dd) -> p h dd", dd=64)[:, :, 0:32]
                        pB = pq[:].rearrange("p (h dd) -> p h dd", dd=64)[:, :, 32:64]
                        cosA = rcos[:, gt * 64: gt * 64 + 32].unsqueeze(1).broadcast_to([128, 8, 32])
                        cosB = rcos[:, gt * 64 + 32: gt * 64 + 64].unsqueeze(1).broadcast_to([128, 8, 32])
                        sinA = rsin[:, gt * 64: gt * 64 + 32].unsqueeze(1).broadcast_to([128, 8, 32])
                        sinB = rsin[:, gt * 64 + 32: gt * 64 + 64].unsqueeze(1).broadcast_to([128, 8, 32])
                        r3 = lambda t: t.rearrange("p (h dd) -> p h dd", dd=32)
                        E.tensor_tensor(r3(ropet[:, 0:256]), pA, cosA, ALU.mult)
                        E.tensor_tensor(r3(ropet[:, 512:768]), pB, sinA, ALU.mult)
                        E.tensor_tensor(r3(ropet[:, 256:512]), pB, cosB, ALU.mult)
                        E.tensor_tensor(r3(ropet[:, 768:1024]), pA, sinB, ALU.mult)
                        qk3 = qk.rearrange("p (h dd) -> p h dd", dd=64)
                        E.tensor_tensor(qk3[:, :, 0:32], r3(ropet[:, 0:256]),
                                        r3(ropet[:, 512:768]), ALU.subtract)
                        return E.tensor_tensor(qk3[:, :, 32:64], r3(ropet[:, 256:512]),
                                               r3(ropet[:, 768:1024]), ALU.add)
                    t_rope = ctx.add("vector", rope, waits=[t_mm], inc="ve")
                    last_qkv_psum_read[pp] = t_rope
                    for j in range(4):
                        tp = j % 2
                        pst = psTa if tp == 0 else psTb
                        w2 = [t_rope]
                        if last_psT_evac[tp]:
                            w2.append(last_psT_evac[tp])
                        t_tr = ctx.add("tensor", lambda E, qk=qk, j=j, pst=pst: E.transpose(
                            pst[:], qk[:, j * 128:(j + 1) * 128], idb[:]), waits=w2, inc="te")
                        cq = (oc % 2) * 4 + j
                        if oc < 2:
                            dst = qT[:, cq * 512 + ti * 128: cq * 512 + (ti + 1) * 128]
                        else:
                            dst = kT[:, (sl * 8 + cq) * 512 + ti * 128: (sl * 8 + cq) * 512 + (ti + 1) * 128]
                        t_ev = ctx.add("vector", lambda E, dst=dst, pst=pst: E.tensor_copy(
                            dst, pst[:]), waits=[t_tr], inc="ve")
                        last_psT_evac[tp] = t_ev
                        t_rope_done = t_ev
                else:
                    dst = vr[:, (sl * 4 + ti) * 1024 + (oc - 4) * 512:
                             (sl * 4 + ti) * 1024 + (oc - 3) * 512]
                    t_ev = ctx.add("vector", lambda E, dst=dst, pq=pq: E.tensor_copy(
                        dst, pq[:]), waits=[t_mm], inc="ve")
                    last_qkv_psum_read[pp] = t_ev
                    t_rope_done = t_ev
        t_qkv = t_rope_done
        if DEBUG_A and s == 1:
            ctx.dma("sync", d["D_qT"], qT[:], waits=[t_qkv])
            ctx.dma("sync", d["D_kT"], kT[:, 8 * 512:16 * 512], waits=[t_qkv])
            ctx.dma("sync", d["D_vr"], vr[:, 4 * 1024:8 * 1024], waits=[t_qkv])

        for qi in range(2):
            qt = s * 2 + qi
            valid = [m for m in range(6) if qt * 2 - 4 + m >= 0]
            for h in range(NH):
                hh, ck = (h % 2) * 64, h // 2
                w = [t_qkv]
                if t_exp:
                    w.append(t_exp)
                def scoresmm(E, qi=qi, qt=qt, h=h, hh=hh, ck=ck, valid=tuple(valid)):
                    last = None
                    for m in valid:
                        jb = qt * 2 - 4 + m
                        slj = (jb // 4) % 2
                        off = (jb % 4) * 128
                        lhsT = kT[hh:hh + 64,
                                  (slj * 8 + ck) * 512 + off: (slj * 8 + ck) * 512 + off + 128]
                        rhs = qT[hh:hh + 64, ck * 512 + qi * 256: ck * 512 + (qi + 1) * 256]
                        if m < 4:
                            out = psS1[:, m * 256:(m + 1) * 256]
                        else:
                            out = psS2[:, (m - 4) * 256:(m - 3) * 256]
                        last = E.matmul(out, lhsT, rhs, start=True, stop=True)
                    return last
                t_sc = ctx.add("tensor", scoresmm, waits=w, inc="te")

                w = [t_sc]
                if t_dav_prev:
                    w.append(t_dav_prev)
                def expf(E, valid=tuple(valid)):
                    v1 = [m for m in valid if m < 4]
                    last = None
                    if v1:
                        a, b = min(v1), max(v1) + 1
                        last = E.activation(pt[:, a * 256:b * 256], psS1[:, a * 256:b * 256],
                                            AF.Exp, scale=0.125)
                    last = E.activation(pt[:, 1024:1536], psS2[:], AF.Exp, scale=0.125)
                    return last
                t_exp = ctx.add("scalar", expf, waits=w, inc="se")

                mask_of = {0: 0, 1: 1, 4: 2, 5: 3}
                def maskf(E, valid=tuple(valid)):
                    last = None
                    for m in valid:
                        if m in mask_of:
                            last = E.tensor_tensor(
                                pt[:, m * 256:(m + 1) * 256], pt[:, m * 256:(m + 1) * 256],
                                amask[:, mask_of[m] * 256:(mask_of[m] + 1) * 256], ALU.mult)
                    return last
                t_msk = ctx.add("vector", maskf, waits=[t_exp], inc="ve")

                w = [t_msk]
                if t_avn:
                    w.append(t_avn)
                def denav(E, s=s, qt=qt, h=h, valid=tuple(valid)):
                    for i, m in enumerate(valid):
                        E.matmul(psAV[0:1, 256:512], onescol[:], pt[:, m * 256:(m + 1) * 256],
                                 start=(i == 0), stop=(i == len(valid) - 1))
                    last = None
                    for i, m in enumerate(valid):
                        jb = qt * 2 - 4 + m
                        slj = (jb // 4) % 2
                        lhsT = vr[:, (slj * 4 + jb % 4) * 1024 + h * 64:
                                  (slj * 4 + jb % 4) * 1024 + (h + 1) * 64]
                        last = E.matmul(psAV[0:64, 0:256],
                                        lhsT, pt[:, m * 256:(m + 1) * 256],
                                        start=(i == 0), stop=(i == len(valid) - 1))
                    return last
                t_dav = ctx.add("tensor", denav, waits=w, inc="te")
                t_dav_prev = t_dav
                if DEBUG_A and qt == 2 and h == 0:
                    ctx.dma("sync", d["D_pt"], pt[:], waits=[t_msk])

                w = [t_dav]
                if t_rc_prev:
                    w.append(t_rc_prev)
                t_de = ctx.add("scalar", lambda E: E.copy(densb[:], psAV[0:1, 256:512]),
                               waits=w, inc="se")
                t_de_prev = t_de
                if DEBUG_A and qt == 2 and h == 0:
                    ctx.dma("sync", d["D_den"], densb[:], waits=[t_de])
                w = [t_de]
                if t_bc_prev:
                    w.append(t_bc_prev)
                t_rc = ctx.add("vector", lambda E: E.reciprocal(recb[:], densb[:]),
                               waits=w, inc="ve")
                t_rc_prev = t_rc
                t_bc = ctx.add("tensor", lambda E: E.matmul(
                    psAV[64:128, 0:256], onesrowf[0:1, 0:64], recb[:],
                    start=True, stop=True), waits=[t_rc], inc="te")
                t_bc_prev = t_bc
                t_bce = ctx.add("scalar", lambda E: E.copy(bcsb[:], psAV[64:128, 0:256]),
                                waits=[t_bc], inc="se")
                t_bce_prev = t_bce
                w = [t_bce, t_dav]
                if last_attnT_use:
                    w.append(last_attnT_use)
                t_avn = ctx.add("vector", lambda E, h=h, ck=ck, hh=hh: E.tensor_tensor(
                    attnT[hh:hh + 64, ck * 256:(ck + 1) * 256], psAV[0:64, 0:256], bcsb[:], ALU.mult),
                    waits=w, inc="ve")

            if DEBUG_A and qt == 2:
                ctx.dma("sync", d["D_attnT"], attnT[:], waits=[t_avn])
            last_attnT_use = None
            for qsub in range(2):
                gt = qt * 2 + qsub
                for och in range(2):
                    pp = och % 2
                    pq = psA if pp == 0 else psB
                    w = [t_avn, t_wo]
                    if last_qkv_psum_read[pp]:
                        w.append(last_qkv_psum_read[pp])
                    def opmm(E, qsub=qsub, och=och, pq=pq):
                        last = None
                        for c in range(NC):
                            last = E.matmul(
                                pq[:], attnT[:, c * 256 + qsub * 128: c * 256 + qsub * 128 + 128],
                                wo[:, c * 1024 + och * 512: c * 1024 + (och + 1) * 512],
                                start=(c == 0), stop=(c == NC - 1))
                        return last
                    t_op = ctx.add("tensor", opmm, waits=w, inc="te")
                    w2 = [t_op, t_x]
                    if prev_ln_done:
                        w2.append(prev_ln_done)
                    t_res = ctx.add("vector", lambda E, sl=sl, qi=qi, qsub=qsub, och=och, pq=pq:
                                    E.tensor_tensor(
                                        rln[:, och * 512:(och + 1) * 512], pq[:],
                                        xf[:, sl * 4096 + (qi * 2 + qsub) * 1024 + och * 512:
                                           sl * 4096 + (qi * 2 + qsub) * 1024 + (och + 1) * 512],
                                        ALU.add), waits=w2, inc="ve")
                    last_qkv_psum_read[pp] = t_res
                    if och == 1:
                        t_res_done = t_res
                last_attnT_use = t_res_done
                last_xf_use[sl] = t_res_done
                if DEBUG_A and gt == 4:
                    ctx.dma("sync", d["D_rln"], rln[:], waits=[t_res_done])
                def lnf(E):
                    E.bn_stats(bnst[:, 0:6], rln[:, 0:512])
                    E.bn_stats(bnst[:, 6:12], rln[:, 512:1024])
                    E.drain()
                    E.bn_aggr(mv[:], bnst[:].rearrange("p (g k) -> p g k", k=6))
                    E.drain()
                    return E.nop()
                wbn = [t_res_done]
                if prev_ln_done:
                    wbn.append(prev_ln_done)
                t_bn = ctx.add("vector", lnf, waits=wbn, inc="ve")
                def rstdf(E):
                    E.activation(lnt[:, 0:1], mv[:, 1:2], AF.Ln, bias=1e-5)
                    return E.activation(lnt[:, 1:2], lnt[:, 0:1], AF.Exp, scale=-0.5)
                t_rs = ctx.add("scalar", rstdf, waits=[t_bn], inc="se")
                w3 = [t_rs]
                if t_x1b_dma:
                    w3.append(t_x1b_dma)
                t_ap = ctx.add("vector", lambda E: E.tensor_scalar(
                    x1b[:], rln[:], mv[:, 0:1], lnt[:, 1:2], ALU.subtract, ALU.mult),
                    waits=w3, inc="ve")
                prev_ln_done = t_ap
                if DEBUG_A and gt == 4:
                    ctx.dma("sync", d["D_mv"], mv[:], waits=[t_ap])
                    ctx.dma("sync", d["D_lnt"], lnt[:], waits=[t_ap])
                    ctx.dma("sync", d["D_x1b"], x1b[:], waits=[t_ap])
                t_x1b_dma = ctx.dma("sync", X1D.rearrange("(t p) dd -> p t dd", p=128)[:, gt, :],
                                    x1b[:], waits=[t_ap])
                t_evl = None
                for c in range(NC):
                    tp = c % 2
                    pst = psTa if tp == 0 else psTb
                    w4 = [t_ap]
                    if last_psT_evac[tp]:
                        w4.append(last_psT_evac[tp])
                    t_tr = ctx.add("tensor", lambda E, c=c, pst=pst: E.transpose(
                        pst[:], x1b[:, c * 128:(c + 1) * 128], idb[:]), waits=w4, inc="te")
                    w5 = [t_tr]
                    if t_x1t_dma:
                        w5.append(t_x1t_dma)
                    t_evl = ctx.add("vector", lambda E, c=c, pst=pst: E.tensor_copy(
                        x1t[:, c * 128:(c + 1) * 128], pst[:]), waits=w5, inc="ve")
                    last_psT_evac[tp] = t_evl
                t_x1t_dma = ctx.dma("sync",
                                    X1TD.rearrange("(c p) tt -> p c tt", p=128)[:, :, gt * 128:(gt + 1) * 128],
                                    x1t[:].rearrange("p (c tt) -> p c tt", tt=128),
                                    waits=[t_evl])
        last_xT_use = t_qkv
        last_qT_use = t_qkv
        last_kslot_use[sl] = None
        last_vslot_use[sl] = None

    return ctx.barrier_tok()



def alloc_passA(nc, stack, sb, ps):
    S = lambda name, shape, dt: sb.__setitem__(name, stack.enter_context(
        nc.sbuf_tensor("sb_" + name, shape, dt)))
    S("A_wq", [128, 8 * 3072], BF16)
    S("A_wo", [128, 8 * 1024], BF16)
    S("A_amask", [128, 1024], BF16)
    S("A_rcos", [128, 32 * 64], BF16)
    S("A_rsin", [128, 32 * 64], BF16)
    S("A_xf", [128, 2 * 4096], F32)
    S("A_xbf", [128, 4096], BF16)
    S("A_xT", [128, 8 * 512], BF16)
    S("A_qkb", [128, 2 * 512], BF16)
    S("A_ropet", [128, 4 * 256], F32)
    S("A_qT", [128, 8 * 512], BF16)
    S("A_kT", [128, 2 * 8 * 512], BF16)
    S("A_vr", [128, 2 * 4 * 1024], BF16)
    S("A_pt", [128, 1536], BF16)
    S("A_densb", [1, 256], F32)
    S("A_recb", [1, 256], F32)
    S("A_bcsb", [64, 256], BF16)
    S("A_attnT", [128, 8 * 256], BF16)
    S("A_rln", [128, 1024], F32)
    S("A_bnst", [128, 12], F32)
    S("A_mv", [128, 2], F32)
    S("A_lnt", [128, 2], F32)
    S("A_x1b", [128, 1024], BF16)
    S("A_x1t", [128, 8 * 128], BF16)


def alloc_common(nc, stack, sb):
    S = lambda name, shape, dt: sb.__setitem__(name, stack.enter_context(
        nc.sbuf_tensor("sb_" + name, shape, dt)))
    S("c_idb", [128, 128], BF16)
    S("c_fvals", [128, 4], F32)
    nc.const_aps.register(F32, 1e-5, sb["c_fvals"][:, 0:1])
    nc.const_aps.register(F32, 1e-12, sb["c_fvals"][:, 1:2])
    S("c_onescol", [128, 1], BF16)
    S("c_onesrow", [1, 128], BF16)
    S("c_onesrowf", [1, 128], F32)


def load_common(ctx, d, sb):
    toks = []
    toks.append(ctx.dma("sync", sb["c_idb"][:], d["c_id128b"]))
    toks.append(ctx.dma("sync", sb["c_fvals"][:], d["c_fvals"]))
    toks.append(ctx.dma("sync", sb["c_onescol"][:], d["c_ones_col"]))
    toks.append(ctx.dma("sync", sb["c_onesrow"][:], d["c_ones_row"]))
    toks.append(ctx.dma("sync", sb["c_onesrowf"][:], d["c_ones_rowf"]))
    return toks


def alloc_psum_A(nc, stack, ps):
    P = lambda name, shape, dt: ps.__setitem__(name, stack.enter_context(
        nc.psum_tensor("A" + name, shape, dt)))
    P("psA", [128, 512], F32)
    P("psB", [128, 512], F32)
    P("psTa", [128, 128], BF16)
    P("psTb", [128, 128], BF16)
    P("psS1", [128, 1024], F32)
    P("psS2", [128, 512], F32)
    P("psAV", [128, 512], F32)




def build_passA_program():
    import contextlib
    nc = bass.Bass("TRN2", target_bir_lowering=False)
    d = {}
    d["x"] = nc.dram_tensor("x", [T, DM], F32, kind="ExternalInput").ap()
    d["l0_wqkv"] = nc.dram_tensor("l0_wqkv", [DM, 3 * DM], F32, kind="ExternalInput").ap()
    d["l0_wout"] = nc.dram_tensor("l0_wout", [DM, DM], F32, kind="ExternalInput").ap()
    for k, v in CONSTS.items():
        dt = BF16 if v.dtype == BF else (F32 if v.dtype == np.float32 else U32)
        d[k] = nc.dram_tensor(k, list(v.shape), dt, kind="ExternalInput").ap()
    d["x1d"] = nc.dram_tensor("x1d", [T, DM], BF16, kind="ExternalOutput").ap()
    d["x1td"] = nc.dram_tensor("x1td", [DM, T], BF16, kind="ExternalOutput").ap()
    if DEBUG_A:
        d["D_qT"] = nc.dram_tensor("D_qT", [128, 8 * 512], BF16, kind="ExternalOutput").ap()
        d["D_kT"] = nc.dram_tensor("D_kT", [128, 8 * 512], BF16, kind="ExternalOutput").ap()
        d["D_vr"] = nc.dram_tensor("D_vr", [128, 4 * 1024], BF16, kind="ExternalOutput").ap()
        d["D_pt"] = nc.dram_tensor("D_pt", [128, 1536], BF16, kind="ExternalOutput").ap()
        d["D_den"] = nc.dram_tensor("D_den", [1, 256], F32, kind="ExternalOutput").ap()
        d["D_attnT"] = nc.dram_tensor("D_attnT", [128, 8 * 256], BF16, kind="ExternalOutput").ap()
        d["D_rln"] = nc.dram_tensor("D_rln", [128, 1024], F32, kind="ExternalOutput").ap()
        d["D_mv"] = nc.dram_tensor("D_mv", [128, 2], F32, kind="ExternalOutput").ap()
        d["D_lnt"] = nc.dram_tensor("D_lnt", [128, 2], F32, kind="ExternalOutput").ap()
        d["D_x1b"] = nc.dram_tensor("D_x1b", [128, 1024], BF16, kind="ExternalOutput").ap()

    ctx = Ctx(nc)
    with contextlib.ExitStack() as stack:
        ctx.make_sems(stack)
        sb, ps = {}, {}
        alloc_common(nc, stack, sb)
        alloc_passA(nc, stack, sb, ps)
        alloc_psum_A(nc, stack, ps)
        blk = stack.enter_context(nc.Block())
        load_common(ctx, d, sb)
        emit_passA(ctx, d, sb, ps)
        fin = ctx.barrier_tok()
        ctx.add("sync", lambda E: None, waits=fin)
        for eng in Ctx.ENGS:
            opsl = ctx.q[eng]
            getattr(blk, eng)(lambda E, opsl=opsl: [f(E) for f in opsl])
    return nc


def consts_inmap():
    return {k: np.asarray(v) for k, v in CONSTS.items()}




_PROGS = {}


def _get_prog(name):
    if name not in _PROGS:
        _PROGS[name] = {"A": build_passA_program, "B": build_passB_program,
                        "C": build_passC_program, "D": build_passD_program,
                        "F": build_fused_program}[name]()
    return _PROGS[name]


def kernel_unfused(**inputs):
    consts = consts_inmap()
    f32 = lambda k: np.ascontiguousarray(np.asarray(inputs[k], np.float32))
    X = f32("L0_attn_x")
    ncA = _get_prog("A")
    mapsA = [dict(x=np.ascontiguousarray(X[b]), l0_wqkv=f32("L0_attn_Wqkv"),
                  l0_wout=f32("L0_attn_Wout"), **consts) for b in range(8)]
    resA = run_bass_kernel_spmd(ncA, mapsA, core_ids=list(range(8))).results

    ncB = _get_prog("B")
    mapsB = [dict(x1d_in=resA[b]["x1d"], x1td_in=resA[b]["x1td"],
                  l0_wg=f32("L0_ffn_Wg"), l0_bg=f32("L0_ffn_bg"),
                  l0_we=f32("L0_ffn_We"), l0_be=f32("L0_ffn_be"), **consts)
             for b in range(8)]
    resB = run_bass_kernel_spmd(ncB, mapsB, core_ids=list(range(8))).results

    ncC = _get_prog("C")
    mapsC = [dict(x2d_in=resB[b]["x2d"], x2td_in=resB[b]["x2td"],
                  l1_wqkv=f32("L1_attn_w_qkv"), l1_wout=f32("L1_attn_w_out"),
                  l1_bout=f32("L1_attn_b_out"), l1_ng=f32("L1_attn_norm_g"),
                  l1_nb=f32("L1_attn_norm_b"), l1_rw=f32("L1_ffn_router_w"), **consts)
             for b in range(8)]
    resC = run_bass_kernel_spmd(ncC, mapsC, core_ids=list(range(8))).results

    ncD = _get_prog("D")
    mapsD = [dict(x3d_in=resC[b]["x3d"], logitsd_in=resC[b]["logitsd"],
                  l1_w0=f32("L1_ffn_w0_kernel"), l1_w1=f32("L1_ffn_w1_kernel"),
                  l1_wo=f32("L1_ffn_wo_kernel"), **consts) for b in range(8)]
    resD = run_bass_kernel_spmd(ncD, mapsD, core_ids=list(range(8))).results

    return np.stack([np.asarray(resD[b]["out"], np.float32) for b in range(8)], axis=0)


if __name__ == "__main__":
    nc = build_passA_program()
    print("built, instructions:", nc.n_instructions() if hasattr(nc, "n_instructions") else "?")



def alloc_passB(nc, stack, sb, ps):
    S = lambda name, shape, dt: sb.__setitem__(name, stack.enter_context(
        nc.sbuf_tensor("sb_" + name, shape, dt)))
    S("B_wg", [128, 8 * 1024], BF16)
    S("B_bg", [1, 1024], BF16)
    S("B_be", [1, 8 * 1024], BF16)
    S("B_we", [128, 2 * 8 * 1024], BF16)
    S("B_x1t", [128, 4 * 1024], BF16)
    S("B_maskp", [128, 32 * 8], F32)
    S("B_resmax", [128, 8], F32)
    S("B_m", [128, 4], F32)
    S("B_junk", [128, 1024], BF16)
    S("B_tt", [128, 4 * 512], BF16)
    S("B_acc", [128, 32 * 1024], BF16)
    S("B_x1b", [128, 2 * 1024], BF16)
    S("B_rln", [128, 1024], F32)
    S("B_bnst", [128, 12], F32)
    S("B_mv", [128, 2], F32)
    S("B_lnt", [128, 2], F32)
    S("B_x2b", [128, 1024], BF16)
    S("B_x2t", [128, 8 * 128], BF16)


def alloc_psum_B(nc, stack, ps):
    P = lambda name, shape, dt: ps.__setitem__(name, stack.enter_context(
        nc.psum_tensor("B" + name, shape, dt)))
    P("psL", [128, 1024], F32)
    P("psE0", [128, 512], F32)
    P("psE1", [128, 512], F32)
    P("psE2", [128, 512], F32)
    P("psE3", [128, 512], F32)
    P("psTa", [128, 128], BF16)
    P("psTb", [128, 128], BF16)


def emit_passB(ctx, d, sb, ps):
    nc = ctx.nc
    X1D, X1TD = d["x1d_in"], d["x1td_in"]
    WG, BG, WE, BE = d["l0_wg"], d["l0_bg"], d["l0_we"], d["l0_be"]
    X2D, X2TD = d["x2d"], d["x2td"]

    idb = sb["c_idb"]
    onesrow = sb["c_onesrow"]
    wg = sb["B_wg"]; bg = sb["B_bg"]; be8 = sb["B_be"]; we = sb["B_we"]
    x1t = sb["B_x1t"]; maskp = sb["B_maskp"]; resmax = sb["B_resmax"]
    mcol = sb["B_m"]; junk = sb["B_junk"]; tt = sb["B_tt"]; acc = sb["B_acc"]
    x1b = sb["B_x1b"]; rln = sb["B_rln"]; bnst = sb["B_bnst"]
    mv = sb["B_mv"]; lnt = sb["B_lnt"]; x2b = sb["B_x2b"]; x2t = sb["B_x2t"]
    psL = ps["psL"]
    psE = [ps["psE0"], ps["psE1"], ps["psE2"], ps["psE3"]]
    psTa, psTb = ps["psTa"], ps["psTb"]

    X1T_R = X1TD.rearrange("(c p) t -> p c t", p=128)
    X1D_R = X1D.rearrange("(t p) dd -> p t dd", p=128)

    t_wg = ctx.dma("gpsimd", wg[:].rearrange("p (c o) -> p c o", o=1024),
                   WG.rearrange("(c p) o -> p c o", p=128))
    t_bg = ctx.dma("gpsimd", bg[:], BG.unsqueeze(0))
    t_be = ctx.dma("gpsimd", be8[:], BE.rearrange("e o -> (e o)").unsqueeze(0))

    t_x1t_use = [None] * 4
    t_exp_prev = None
    t_red_prev = None
    t_gate_done = [None] * 32
    t_x1t_tok = [None] * 32
    for t in range(NT):
        sl = t % 4
        w = [t_x1t_use[sl]] if t_x1t_use[sl] else []
        t_ld = ctx.dma("sync", x1t[:].rearrange("p (s c tt) -> p s c tt", s=4, tt=128)[:, sl],
                       X1T_R[:, :, t * 128:(t + 1) * 128], waits=w)
        t_x1t_tok[t] = t_ld
        w = [t_ld, t_wg, t_bg]
        if t_exp_prev:
            w.append(t_exp_prev)
        if t_red_prev:
            w.append(t_red_prev)
        def gatemm(E, sl=sl):
            last = None
            for fc in range(2):
                for c in range(NC):
                    E.matmul(psL[:, fc * 512:(fc + 1) * 512],
                             x1t[:, (sl * 8 + c) * 128:(sl * 8 + c + 1) * 128],
                             wg[:, c * 1024 + fc * 512: c * 1024 + (fc + 1) * 512],
                             start=(c == 0), stop=False)
                last = E.matmul(psL[:, fc * 512:(fc + 1) * 512], onesrow[0:1, :],
                                bg[0:1, fc * 512:(fc + 1) * 512], start=False, stop=True)
            return last
        t_mm = ctx.add("tensor", gatemm, waits=w, inc="te")
        t_x1t_use[sl] = t_mm
        def redf(E, t=t):
            lv = psL[:].rearrange("p (g e) -> p e g", e=8)
            E.tensor_reduce(resmax[:], lv, AX.X, ALU.max)
            E.drain()
            E.tensor_reduce(mcol[:, 0:1], resmax[:], AX.X, ALU.max)
            E.drain()
            return E.tensor_scalar(mcol[:, 1:2], mcol[:, 0:1], -1.0, None, ALU.mult)
        t_red = ctx.add("vector", redf, waits=[t_mm], inc="ve")
        t_red_prev = t_red
        t_exp = ctx.add("scalar", lambda E: E.activation(
            junk[:], psL[:], AF.Exp, bias=mcol[:, 1:2], accum_out=mcol[:, 2:3]),
            waits=[t_red], inc="se")
        t_exp_prev = t_exp
        def pf(E, t=t):
            E.reciprocal(mcol[:, 3:4], mcol[:, 2:3])
            E.drain()
            return E.tensor_scalar(maskp[:, t * 8:(t + 1) * 8], resmax[:],
                                   mcol[:, 0:1], mcol[:, 3:4], ALU.is_equal, ALU.mult)
        t_gate_done[t] = ctx.add("vector", pf, waits=[t_exp], inc="ve")

    WEV = we[:].rearrange("p (s c o) -> p s c o", s=2, o=1024)
    X1TV = x1t[:].rearrange("p (s c tt) -> p s c tt", s=4, tt=128)

    def we_load(e, waits):
        return ctx.dma("gpsimd", WEV[:, e % 2],
                       WE[e].rearrange("(c p) o -> p c o", p=128), waits=waits)

    t_we = [we_load(0, []), None]
    t_we[1] = we_load(1, [])
    t_we_lastmm = [None, None]
    t_relu = {}
    t_add = {}
    t_tt_use = [None] * 4
    tt_i = 0
    for e in range(E0):
        sl = e % 2
        for t in range(NT):
            tsl = t % 4
            w = [t_x1t_use[tsl]] if t_x1t_use[tsl] else []
            t_ld = ctx.dma("sync", X1TV[:, tsl],
                           X1T_R[:, :, t * 128:(t + 1) * 128], waits=w)
            for fc in range(2):
                pe = psE[(t * 2 + fc) % 4]
                w = [t_ld, t_we[sl], t_be]
                prev = t_relu.get(((t * 2 + fc) % 4, "psum"))
                if prev:
                    w.append(prev)
                def emm(E, tsl=tsl, fc=fc, e=e, pe=pe, sl=sl):
                    for c in range(NC):
                        E.matmul(pe[:],
                                 x1t[:, (tsl * 8 + c) * 128:(tsl * 8 + c + 1) * 128],
                                 we[:, (sl * 8 + c) * 1024 + fc * 512:
                                    (sl * 8 + c) * 1024 + (fc + 1) * 512],
                                 start=(c == 0), stop=False)
                    return E.matmul(pe[:], onesrow[0:1, :],
                                    be8[0:1, e * 1024 + fc * 512: e * 1024 + (fc + 1) * 512],
                                    start=False, stop=True)
                t_mm = ctx.add("tensor", emm, waits=w, inc="te")
                t_x1t_use[tsl] = t_mm
                t_we_lastmm[sl] = t_mm
                if e == 0:
                    dst = acc[:, t * 1024 + fc * 512: t * 1024 + (fc + 1) * 512]
                    w2 = [t_mm, t_gate_done[t]]
                    t_rl = ctx.add("scalar", lambda E, dst=dst, pe=pe, t=t, e=e: E.activation(
                        dst, pe[:], AF.Relu, scale=maskp[:, t * 8 + e: t * 8 + e + 1]),
                        waits=w2, inc="se")
                    t_relu[((t * 2 + fc) % 4, "psum")] = t_rl
                    t_add[(t, fc)] = t_rl
                else:
                    ti = tt_i % 4
                    tt_i += 1
                    dst = tt[:, ti * 512:(ti + 1) * 512]
                    w2 = [t_mm, t_gate_done[t]]
                    if t_tt_use[ti]:
                        w2.append(t_tt_use[ti])
                    t_rl = ctx.add("scalar", lambda E, dst=dst, pe=pe, t=t, e=e: E.activation(
                        dst, pe[:], AF.Relu, scale=maskp[:, t * 8 + e: t * 8 + e + 1]),
                        waits=w2, inc="se")
                    t_relu[((t * 2 + fc) % 4, "psum")] = t_rl
                    aslice = acc[:, t * 1024 + fc * 512: t * 1024 + (fc + 1) * 512]
                    t_ad = ctx.add("vector", lambda E, aslice=aslice, dst=dst: E.tensor_tensor(
                        aslice, aslice, dst, ALU.add), waits=[t_rl, t_add[(t, fc)]], inc="ve")
                    t_tt_use[ti] = t_ad
                    t_add[(t, fc)] = t_ad
        if e + 2 < E0:
            we_next_tok = we_load(e + 2, [t_we_lastmm[sl]])
            t_we[sl] = we_next_tok

    t_x1b_use = [None, None]
    prev_ln_done = None
    t_x2b_dma = None
    t_x2t_dma = None
    last_psT_evac = {0: None, 1: None}
    t_rln_prev = None
    t_x2trans_prev = None
    for t in range(NT):
        sl = t % 2
        w = [t_x1b_use[sl]] if t_x1b_use[sl] else []
        t_ld = ctx.dma("sync", x1b[:, sl * 1024:(sl + 1) * 1024],
                       X1D_R[:, t, :], waits=w)
        w = [t_ld, t_add[(t, 0)], t_add[(t, 1)]]
        if t_rln_prev:
            w.append(t_rln_prev)
        t_res = ctx.add("vector", lambda E, t=t, sl=sl: E.tensor_tensor(
            rln[:], acc[:, t * 1024:(t + 1) * 1024],
            x1b[:, sl * 1024:(sl + 1) * 1024], ALU.add), waits=w, inc="ve")
        t_x1b_use[sl] = t_res
        def lnf(E):
            E.bn_stats(bnst[:, 0:6], rln[:, 0:512])
            E.bn_stats(bnst[:, 6:12], rln[:, 512:1024])
            E.drain()
            E.bn_aggr(mv[:], bnst[:].rearrange("p (g k) -> p g k", k=6))
            E.drain()
            return E.nop()
        wbn = [t_res]
        if prev_ln_done:
            wbn.append(prev_ln_done)
        t_bn = ctx.add("vector", lnf, waits=wbn, inc="ve")
        def rstdf(E):
            E.activation(lnt[:, 0:1], mv[:, 1:2], AF.Ln, bias=1e-5)
            return E.activation(lnt[:, 1:2], lnt[:, 0:1], AF.Exp, scale=-0.5)
        t_rs = ctx.add("scalar", rstdf, waits=[t_bn], inc="se")
        w3 = [t_rs]
        if t_x2b_dma:
            w3.append(t_x2b_dma)
        if t_x2trans_prev:
            w3.append(t_x2trans_prev)
        t_ap = ctx.add("vector", lambda E: E.tensor_scalar(
            x2b[:], rln[:], mv[:, 0:1], lnt[:, 1:2], ALU.subtract, ALU.mult),
            waits=w3, inc="ve")
        prev_ln_done = t_ap
        t_rln_prev = t_ap
        t_x2b_dma = ctx.dma("sync", X2D.rearrange("(t p) dd -> p t dd", p=128)[:, t, :],
                            x2b[:], waits=[t_ap])
        t_evl = None
        for c in range(NC):
            tp = c % 2
            pst = psTa if tp == 0 else psTb
            w4 = [t_ap]
            if last_psT_evac[tp]:
                w4.append(last_psT_evac[tp])
            t_tr = ctx.add("tensor", lambda E, c=c, pst=pst: E.transpose(
                pst[:], x2b[:, c * 128:(c + 1) * 128], idb[:]), waits=w4, inc="te")
            w5 = [t_tr]
            if t_x2t_dma:
                w5.append(t_x2t_dma)
            t_evl = ctx.add("vector", lambda E, c=c, pst=pst: E.tensor_copy(
                x2t[:, c * 128:(c + 1) * 128], pst[:]), waits=w5, inc="ve")
            last_psT_evac[tp] = t_evl
            t_x2trans_prev = t_tr
        t_x2t_dma = ctx.dma("sync",
                            X2TD.rearrange("(c p) tt -> p c tt", p=128)[:, :, t * 128:(t + 1) * 128],
                            x2t[:].rearrange("p (c tt) -> p c tt", tt=128),
                            waits=[t_evl])
    return ctx.barrier_tok()


def build_passB_program():
    import contextlib
    nc = bass.Bass("TRN2", target_bir_lowering=False)
    d = {}
    d["x1d_in"] = nc.dram_tensor("x1d_in", [T, DM], BF16, kind="ExternalInput").ap()
    d["x1td_in"] = nc.dram_tensor("x1td_in", [DM, T], BF16, kind="ExternalInput").ap()
    d["l0_wg"] = nc.dram_tensor("l0_wg", [DM, DM], F32, kind="ExternalInput").ap()
    d["l0_bg"] = nc.dram_tensor("l0_bg", [DM], F32, kind="ExternalInput").ap()
    d["l0_we"] = nc.dram_tensor("l0_we", [E0, DM, DM], F32, kind="ExternalInput").ap()
    d["l0_be"] = nc.dram_tensor("l0_be", [E0, DM], F32, kind="ExternalInput").ap()
    for k, v in CONSTS.items():
        dt = BF16 if v.dtype == BF else (F32 if v.dtype == np.float32 else U32)
        d[k] = nc.dram_tensor(k, list(v.shape), dt, kind="ExternalInput").ap()
    d["x2d"] = nc.dram_tensor("x2d", [T, DM], BF16, kind="ExternalOutput").ap()
    d["x2td"] = nc.dram_tensor("x2td", [DM, T], BF16, kind="ExternalOutput").ap()

    ctx = Ctx(nc)
    with contextlib.ExitStack() as stack:
        ctx.make_sems(stack)
        sb, ps = {}, {}
        alloc_common(nc, stack, sb)
        alloc_passB(nc, stack, sb, ps)
        alloc_psum_B(nc, stack, ps)
        blk = stack.enter_context(nc.Block())
        load_common(ctx, d, sb)
        emit_passB(ctx, d, sb, ps)
        fin = ctx.barrier_tok()
        ctx.add("sync", lambda E: None, waits=fin)
        for eng in Ctx.ENGS:
            opsl = ctx.q[eng]
            getattr(blk, eng)(lambda E, opsl=opsl: [f(E) for f in opsl])
    return nc



def alloc_passC(nc, stack, sb):
    S = lambda name, shape, dt: sb.__setitem__(name, stack.enter_context(
        nc.sbuf_tensor("sb_" + name, shape, dt)))
    S("C_wtmp", [128, 1024], BF16)
    S("C_wq", [128, 8 * 1024], BF16)
    S("C_wkv", [128, 8 * 2048], BF16)
    S("C_wo", [128, 8 * 1024], BF16)
    S("C_ng", [1, 16], F32)
    S("C_nb2rhs", [2, 8], F32)
    S("C_bout", [1, 1024], F32)
    S("C_rw", [128, 8 * 8], F32)
    S("C_x2t", [128, 4 * 1024], BF16)
    S("C_kt", [128, 1024], BF16)
    S("C_vt", [128, 1024], BF16)
    S("C_M", [128, 2 * 512], BF16)
    S("C_N", [128, 2 * 512], BF16)
    S("C_Macc", [128, 4 * 512], F32)
    S("C_Sacc", [128, 8], F32)
    S("C_S", [128, 8], F32)
    S("C_nb2", [128, 8], F32)
    S("C_bS", [128, 8], BF16)
    S("C_cvec", [1, 1024], F32)
    S("C_cvh", [1, 1024], BF16)
    S("C_x2s", [128, 8 * 512], BF16)
    S("C_qT", [128, 8 * 512], BF16)
    S("C_nsq", [64, 512], BF16)
    S("C_snorm", [1, 512], F32)
    S("C_arow", [1, 512], F32)
    S("C_bcsb", [64, 512], F32)
    S("C_attnT", [128, 8 * 512], BF16)
    S("C_x2b", [128, 2 * 1024], BF16)
    S("C_rln", [128, 1024], F32)
    S("C_bnst", [128, 12], F32)
    S("C_mv", [128, 2], F32)
    S("C_lnt", [128, 2], F32)
    S("C_x3f", [128, 1024], F32)
    S("C_x3b", [128, 1024], BF16)
    S("C_x3tf", [128, 128], F32)
    S("C_logits", [128, 32 * 8], F32)


def emit_passC(ctx, d, sb, stack):
    nc = ctx.nc
    X2D, X2TD = d["x2d_in"], d["x2td_in"]
    WQKV1, WOUT1, BOUT1 = d["l1_wqkv"], d["l1_wout"], d["l1_bout"]
    NG, NB, RW = d["l1_ng"], d["l1_nb"], d["l1_rw"]
    X3D, LOGD = d["x3d"], d["logitsd"]

    idb = sb["c_idb"]; idf = sb["c_idf"]
    onescol = sb["c_onescol"]; onesrowf = sb["c_onesrowf"]
    wtmp = sb["C_wtmp"]; wq = sb["C_wq"]; wkv = sb["C_wkv"]; wo = sb["C_wo"]
    ngr = sb["C_ng"]; nb2rhs = sb["C_nb2rhs"]; boutr = sb["C_bout"]; rw = sb["C_rw"]
    x2t = sb["C_x2t"]; kt = sb["C_kt"]; vt = sb["C_vt"]
    Msb = sb["C_M"]; Nsb = sb["C_N"]; Ssb = sb["C_S"]
    Macc = sb["C_Macc"]; Sacc = sb["C_Sacc"]
    nb2 = sb["C_nb2"]; bS = sb["C_bS"]; cvec = sb["C_cvec"]; cvh = sb["C_cvh"]
    x2s = sb["C_x2s"]; qTt = sb["C_qT"]; nsq = sb["C_nsq"]
    snorm = sb["C_snorm"]; arow = sb["C_arow"]; bcsb = sb["C_bcsb"]
    attnT = sb["C_attnT"]
    x2b = sb["C_x2b"]; rln = sb["C_rln"]; bnst = sb["C_bnst"]
    mv = sb["C_mv"]; lnt = sb["C_lnt"]; x3f = sb["C_x3f"]; x3b = sb["C_x3b"]
    x3tf = sb["C_x3tf"]; logits = sb["C_logits"]

    X2T_R = X2TD.rearrange("(c p) t -> p c t", p=128)
    X2D_R = X2D.rearrange("(t p) dd -> p t dd", p=128)

    t_ng = ctx.dma("sync", ngr[:], NG.rearrange("a h bb e -> a (h bb e)"))
    t_bout = ctx.dma("sync", boutr[:], BOUT1.unsqueeze(0))
    t_rw = ctx.dma("sync", rw[:].rearrange("p (c e) -> p c e", e=8),
                   RW.rearrange("(c p) e -> p c e", p=128))
    t_nb = ctx.dma("sync", nb2rhs[:], NB.rearrange("a (c i) bb e -> i (a c bb e)", i=2),
                   allow_slow_non_contiguous=True)

    with (
        nc.psum_tensor("psCTa", [128, 128], BF16) as psTa,
        nc.psum_tensor("psCTb", [128, 128], BF16) as psTb,
        nc.psum_tensor("psC_nb", [128, 8], F32) as psNB2,
    ):
        last_ev = {0: None, 1: None}
        t_wl_prev = None

        def wload_transpose(SRC, dst, oblocks):
            nonlocal t_wl_prev
            last = None
            for i, ob in enumerate(oblocks):
                w = [t_wl_prev] if t_wl_prev else []
                t_ld = ctx.dma("gpsimd", wtmp[:],
                               SRC[ob * 128:(ob + 1) * 128, :], waits=w)
                for c in range(NC):
                    tp = c % 2
                    w2 = [t_ld]
                    if last_ev[tp]:
                        w2.append(last_ev[tp])
                    pst = psTa if tp == 0 else psTb
                    t_tr = ctx.add("tensor", lambda E, c=c, pst=pst: E.transpose(
                        pst[:], wtmp[:, c * 128:(c + 1) * 128], idb[:]), waits=w2, inc="te")
                    t_ev = ctx.add("vector", lambda E, c=c, i=i, pst=pst, dst=dst: E.tensor_copy(
                        dst[:].rearrange("p (c o) -> p c o", c=NC)[:, c, i * 128:(i + 1) * 128],
                        pst[:]), waits=[t_tr], inc="ve")
                    last_ev[tp] = t_ev
                    last = t_ev
                t_wl_prev = last
            return last

        t_wq = wload_transpose(WQKV1, wq, list(range(8)))
        t_wkv = wload_transpose(WQKV1, wkv, list(range(8, 24)))
        t_wo = wload_transpose(WOUT1, wo, list(range(8)))
        t_nb2 = ctx.add("tensor", lambda E: E.matmul(
            psNB2[:], sb["c_bc2f"][:], nb2rhs[:], start=True, stop=True),
            waits=[t_nb, d["tok_bc2f"]], inc="te")
        t_nb2e = ctx.add("vector", lambda E: E.tensor_copy(nb2[:], psNB2[:]),
                         waits=[t_nb2], inc="ve")
    t_c0 = ctx.barrier_tok()

    with (
        nc.psum_tensor("psKV", [128, 512], F32) as psKV,
        nc.psum_tensor("psKV2", [128, 512], F32) as psKV2,
        nc.psum_tensor("psMA", [128, 512], F32) as psMA,
        nc.psum_tensor("psMB", [128, 512], F32) as psMB,
        nc.psum_tensor("psNA", [128, 512], F32) as psNA,
        nc.psum_tensor("psNBv", [128, 512], F32) as psNAB,
        nc.psum_tensor("psS", [128, 8], F32) as psS,
        nc.psum_tensor("psCV", [1, 512], F32) as psCV,
    ):
        t_x2t_use = [None] * 4
        t_kv_ev = {0: None, 1: None}
        t_stats_done = None
        for t in range(NT):
            tsl = t % 4
            w = [t_x2t_use[tsl]] if t_x2t_use[tsl] else []
            t_ld = ctx.dma("sync",
                           x2t[:].rearrange("p (s c tt) -> p s c tt", s=4, tt=128)[:, tsl],
                           X2T_R[:, :, t * 128:(t + 1) * 128], waits=w)
            t_mm_last = None
            for g in range(4):
                pp = g % 2
                pkv = psKV if pp == 0 else psKV2
                w2 = [t_ld, t_wkv]
                if t == 0 and g < 2:
                    w2 += t_c0
                if t_kv_ev[pp]:
                    w2.append(t_kv_ev[pp])
                def kvmm(E, tsl=tsl, g=g, pkv=pkv):
                    last = None
                    for c in range(NC):
                        last = E.matmul(
                            pkv[:],
                            x2t[:, (tsl * 8 + c) * 128:(tsl * 8 + c + 1) * 128],
                            wkv[:, c * 2048 + g * 512: c * 2048 + (g + 1) * 512],
                            start=(c == 0), stop=(c == NC - 1))
                    return last
                t_mm = ctx.add("tensor", kvmm, waits=w2, inc="te")
                dst = (kt if g < 2 else vt)[:, (g % 2) * 512:((g % 2) + 1) * 512]
                t_ev = ctx.add("scalar", lambda E, dst=dst, pkv=pkv: E.copy(
                    dst, pkv[:]), waits=[t_mm], inc="se")
                t_kv_ev[pp] = t_ev
                t_mm_last = t_ev
            def statmm(E, t=t):
                last = None
                for hp in range(8):
                    lk = kt[:, hp * 128:(hp + 1) * 128]
                    E.matmul(psMA[:, hp * 64:(hp + 1) * 64], lk,
                             kt[:, hp * 128: hp * 128 + 64], start=True, stop=True)
                    E.matmul(psMB[:, hp * 64:(hp + 1) * 64], lk,
                             kt[:, hp * 128 + 64: hp * 128 + 128], start=True, stop=True)
                    E.matmul(psNA[:, hp * 64:(hp + 1) * 64], lk,
                             vt[:, hp * 128: hp * 128 + 64], start=True, stop=True)
                    last = E.matmul(psNAB[:, hp * 64:(hp + 1) * 64], lk,
                                    vt[:, hp * 128 + 64: hp * 128 + 128], start=True, stop=True)
                for c in range(NC):
                    last = E.matmul(psS[:, c:c + 1], vt[:, c * 128:(c + 1) * 128],
                                    onescol[:], start=True, stop=True)
                return last
            w4 = [t_mm_last]
            if t_stats_done:
                w4.append(t_stats_done)
            t_st = ctx.add("tensor", statmm, waits=w4, inc="te")
            t_x2t_use[tsl] = t_st
            def stacc(E, t=t):
                if t == 0:
                    E.tensor_copy(Macc[:, 0:512], psMA[:])
                    E.tensor_copy(Macc[:, 512:1024], psMB[:])
                    E.tensor_copy(Macc[:, 1024:1536], psNA[:])
                    E.tensor_copy(Macc[:, 1536:2048], psNAB[:])
                    return E.tensor_copy(Sacc[:], psS[:])
                E.tensor_tensor(Macc[:, 0:512], Macc[:, 0:512], psMA[:], ALU.add)
                E.tensor_tensor(Macc[:, 512:1024], Macc[:, 512:1024], psMB[:], ALU.add)
                E.tensor_tensor(Macc[:, 1024:1536], Macc[:, 1024:1536], psNA[:], ALU.add)
                E.tensor_tensor(Macc[:, 1536:2048], Macc[:, 1536:2048], psNAB[:], ALU.add)
                return E.tensor_tensor(Sacc[:], Sacc[:], psS[:], ALU.add)
            t_stats_done = ctx.add("vector", stacc, waits=[t_st], inc="ve")

        def evst(E):
            E.tensor_copy(Msb[:, 0:512], Macc[:, 0:512])
            E.tensor_copy(Msb[:, 512:1024], Macc[:, 512:1024])
            E.tensor_copy(Nsb[:, 0:512], Macc[:, 1024:1536])
            E.tensor_copy(Nsb[:, 512:1024], Macc[:, 1536:2048])
            E.tensor_copy(Ssb[:], Sacc[:])
            E.drain()
            return E.tensor_tensor(bS[:], Ssb[:], nb2[:], ALU.mult)
        t_evst = ctx.add("vector", evst, waits=[t_stats_done, t_nb2e], inc="ve")

        t_cv_prev = None
        for oc in range(2):
            def cvh_mm(E, oc=oc):
                last = None
                for c in range(NC):
                    last = E.matmul(psCV[0:1, :], bS[:, c:c + 1],
                                    wo[:, c * 1024 + oc * 512: c * 1024 + (oc + 1) * 512],
                                    start=(c == 0), stop=(c == NC - 1))
                return last
            w4 = [t_evst, t_wo]
            if t_cv_prev:
                w4.append(t_cv_prev)
            t_m = ctx.add("tensor", cvh_mm, waits=w4, inc="te")
            t_cv_prev = ctx.add("vector", lambda E, oc=oc: E.tensor_tensor(
                cvec[0:1, oc * 512:(oc + 1) * 512], psCV[0:1, :],
                boutr[0:1, oc * 512:(oc + 1) * 512], ALU.add),
                waits=[t_m, t_bout], inc="ve")
        t_cvh = ctx.add("vector", lambda E: E.tensor_copy(cvh[:], cvec[:]),
                        waits=[t_cv_prev], inc="ve")
    t_c1 = ctx.barrier_tok()

    with (
        nc.psum_tensor("psQa", [128, 512], F32) as psQa,
        nc.psum_tensor("psQb", [128, 512], F32) as psQb,
        nc.psum_tensor("psT1", [128, 512], F32) as psT1,
        nc.psum_tensor("psO", [128, 512], F32) as psO,
        nc.psum_tensor("psM2", [128, 512], F32) as psM2,
        nc.psum_tensor("psBC", [64, 512], F32) as psBC,
        nc.psum_tensor("psY", [128, 512], F32) as psY,
        nc.psum_tensor("psRT", [128, 128], F32) as psRT,
    ):
        t_x2s_use = None
        t_q_ev = {0: None, 1: None}
        t_nsq_use = None
        t_sn = None
        t_ar_use = None
        t_bce = None
        t_avn = None
        t_attnT_use = None
        t_y_ev = None
        t_x2b_use = [None, None]
        prev_ln_done = None
        t_x3b_dma = None
        t_rln_prev = None
        t_x3trans_prev = None
        t_x3f_use = None
        t_rt_ev = None
        t_rl_ev_prev = None
        t_nsq_mm_prev = None
        t_norm_read_prev = None
        first = True
        for s in range(8):
            w = [t_x2s_use] if t_x2s_use else []
            t_lds = ctx.dma("sync", x2s[:].rearrange("p (c t) -> p c t", t=512),
                            X2T_R[:, :, s * 512:(s + 1) * 512], waits=w)
            t_qT = None
            for oc in range(NC):
                pp = oc % 2
                pq = psQa if pp == 0 else psQb
                w2 = [t_lds, t_wq]
                if first:
                    w2 += t_c1
                if t_q_ev[pp]:
                    w2.append(t_q_ev[pp])
                if t_rt_ev:
                    w2.append(t_rt_ev)
                if t_rl_ev_prev:
                    w2.append(t_rl_ev_prev)
                first = False
                def qmm(E, oc=oc, pq=pq):
                    last = None
                    for c in range(NC):
                        last = E.matmul(
                            pq[:], wq[:, c * 1024 + oc * 128: c * 1024 + (oc + 1) * 128],
                            x2s[:, c * 512:(c + 1) * 512],
                            start=(c == 0), stop=(c == NC - 1))
                    return last
                t_mm = ctx.add("tensor", qmm, waits=w2, inc="te")
                t_ev = ctx.add("scalar", lambda E, oc=oc, pq=pq: E.copy(
                    qTt[:, oc * 512:(oc + 1) * 512], pq[:]), waits=[t_mm], inc="se")
                t_q_ev[pp] = t_ev
                t_qT = t_ev
            for h in range(NH):
                hp, r0 = h // 2, (h % 2) * 64
                Mh = Msb[r0:r0 + 64, (h % 2) * 512 + hp * 64: (h % 2) * 512 + (hp + 1) * 64]
                Nh = Nsb[r0:r0 + 64, (h % 2) * 512 + hp * 64: (h % 2) * 512 + (hp + 1) * 64]
                qh = qTt[r0:r0 + 64, hp * 512:(hp + 1) * 512]
                w3 = [t_qT]
                if t_nsq_use:
                    w3.append(t_nsq_use)
                t_t1 = ctx.add("tensor", lambda E, Mh=Mh, qh=qh: E.matmul(
                    psT1[0:64, :], Mh, qh, start=True, stop=True), waits=w3, inc="te")
                w3b = [t_qT]
                if t_avn:
                    w3b.append(t_avn)
                t_ou = ctx.add("tensor", lambda E, Nh=Nh, qh=qh: E.matmul(
                    psO[0:64, :], Nh, qh, start=True, stop=True), waits=w3b, inc="te")
                w4 = [t_t1]
                if t_nsq_mm_prev:
                    w4.append(t_nsq_mm_prev)
                t_nsq_use = ctx.add("vector", lambda E, qh=qh: E.tensor_tensor(
                    nsq[:], psT1[0:64, :], qh, ALU.mult), waits=w4, inc="ve")
                w5 = [t_nsq_use]
                if t_norm_read_prev:
                    w5.append(t_norm_read_prev)
                t_n2 = ctx.add("tensor", lambda E: E.matmul(
                    psM2[0:1, 0:512], onescol[0:64, :], nsq[:], start=True, stop=True),
                    waits=w5, inc="te")
                t_nsq_mm_prev = t_n2
                w6 = [t_n2]
                if t_ar_use:
                    w6.append(t_ar_use)
                t_sn = ctx.add("scalar", lambda E: E.activation(
                    snorm[:], psM2[0:1, 0:512], AF.Sqrt), waits=w6, inc="se")
                t_norm_read_prev = t_sn
                def archain(E, h=h):
                    E.tensor_scalar(arow[:], snorm[:], 1e-12, None, ALU.max)
                    E.drain()
                    E.reciprocal(arow[:], arow[:])
                    E.drain()
                    return E.tensor_scalar(arow[:], arow[:], ngr[0:1, h:h + 1], None, ALU.mult)
                t_ar = ctx.add("vector", archain, waits=[t_sn, t_ng], inc="ve")
                w7 = [t_ar]
                if t_bce:
                    w7.append(t_bce)
                t_bc = ctx.add("tensor", lambda E: E.matmul(
                    psBC[:], onesrowf[0:1, 0:64], arow[:], start=True, stop=True),
                    waits=w7, inc="te")
                t_ar_use = t_bc
                t_bce = ctx.add("scalar", lambda E: E.copy(bcsb[:], psBC[:]),
                                waits=[t_bc], inc="se")
                w8 = [t_bce, t_ou]
                if t_attnT_use:
                    w8.append(t_attnT_use)
                t_avn = ctx.add("vector", lambda E, hp=hp, r0=r0: E.tensor_tensor(
                    attnT[r0:r0 + 64, hp * 512:(hp + 1) * 512], psO[0:64, :], bcsb[:],
                    ALU.mult), waits=w8, inc="ve")
            t_attnT_use = None
            for tsub in range(4):
                gt = s * 4 + tsub
                sl = gt % 2
                w = [t_x2b_use[sl]] if t_x2b_use[sl] else []
                t_ldr = ctx.dma("sync", x2b[:, sl * 1024:(sl + 1) * 1024],
                                X2D_R[:, gt, :], waits=w)
                t_res_done = None
                for och in range(2):
                    w2 = [t_avn, t_wo, t_cvh]
                    if t_y_ev:
                        w2.append(t_y_ev)
                    def ymm(E, tsub=tsub, och=och):
                        for c in range(NC):
                            E.matmul(psY[:],
                                     attnT[:, c * 512 + tsub * 128: c * 512 + tsub * 128 + 128],
                                     wo[:, c * 1024 + och * 512: c * 1024 + (och + 1) * 512],
                                     start=(c == 0), stop=False)
                        return E.matmul(psY[:], sb["c_onesrow"][0:1, :],
                                        cvh[0:1, och * 512:(och + 1) * 512], start=False, stop=True)
                    t_y = ctx.add("tensor", ymm, waits=w2, inc="te")
                    w3 = [t_y, t_ldr]
                    if t_rln_prev:
                        w3.append(t_rln_prev)
                    t_res = ctx.add("vector", lambda E, och=och, sl=sl: E.tensor_tensor(
                        rln[:, och * 512:(och + 1) * 512], psY[:],
                        x2b[:, sl * 1024 + och * 512: sl * 1024 + (och + 1) * 512],
                        ALU.add), waits=w3, inc="ve")
                    t_y_ev = t_res
                    t_res_done = t_res
                t_x2b_use[sl] = t_res_done
                t_attnT_use = t_res_done
                def lnf(E):
                    E.bn_stats(bnst[:, 0:6], rln[:, 0:512])
                    E.bn_stats(bnst[:, 6:12], rln[:, 512:1024])
                    E.drain()
                    E.bn_aggr(mv[:], bnst[:].rearrange("p (g k) -> p g k", k=6))
                    E.drain()
                    return E.nop()
                wbn = [t_res_done]
                if prev_ln_done:
                    wbn.append(prev_ln_done)
                t_bn = ctx.add("vector", lnf, waits=wbn, inc="ve")
                def rstdf(E):
                    E.activation(lnt[:, 0:1], mv[:, 1:2], AF.Ln, bias=1e-5)
                    return E.activation(lnt[:, 1:2], lnt[:, 0:1], AF.Exp, scale=-0.5)
                t_rs = ctx.add("scalar", rstdf, waits=[t_bn], inc="se")
                w4 = [t_rs]
                if t_x3f_use:
                    w4.append(t_x3f_use)
                t_apf = ctx.add("vector", lambda E: E.tensor_scalar(
                    x3f[:], rln[:], mv[:, 0:1], lnt[:, 1:2], ALU.subtract, ALU.mult),
                    waits=w4, inc="ve")
                prev_ln_done = t_apf
                t_rln_prev = t_apf
                w5 = [t_apf]
                if t_x3b_dma:
                    w5.append(t_x3b_dma)
                t_apb = ctx.add("vector", lambda E: E.tensor_copy(x3b[:], x3f[:]),
                                waits=w5, inc="ve")
                t_x3b_dma = ctx.dma("sync", X3D.rearrange("(t p) dd -> p t dd", p=128)[:, gt, :],
                                    x3b[:], waits=[t_apb])
                t_rt_prev_local = None
                for c in range(NC):
                    w6 = [t_apf, t_rw]
                    if t_rt_ev:
                        w6.append(t_rt_ev)
                    t_rt = ctx.add("tensor", lambda E, c=c: E.transpose(
                        psRT[:], x3f[:, c * 128:(c + 1) * 128], idf[:]), waits=w6, inc="te")
                    w7 = [t_rt]
                    if t_x3f_use:
                        w7.append(t_x3f_use)
                    t_rt_ev = ctx.add("scalar", lambda E: E.copy(x3tf[:], psRT[:]),
                                      waits=[t_rt], inc="se")
                    t_rl = ctx.add("tensor", lambda E, c=c, gt=gt: E.matmul(
                        psM2[:, 8:16], x3tf[:], rw[:, c * 8:(c + 1) * 8],
                        start=(c == 0), stop=(c == NC - 1)), waits=[t_rt_ev], inc="te")
                    t_rt_prev_local = t_rl
                t_x3f_use = t_rt_ev
                t_rl_ev_prev = ctx.add("vector", lambda E, gt=gt: E.tensor_copy(
                    logits[:, gt * 8:(gt + 1) * 8], psM2[:, 8:16]),
                    waits=[t_rt_prev_local], inc="ve")
    t_lg = ctx.dma("sync", LOGD.rearrange("(t p) e -> p t e", p=128),
                   logits[:].rearrange("p (t e) -> p t e", e=8), waits=[t_rl_ev_prev])
    return ctx.barrier_tok()


def build_passC_program():
    import contextlib
    nc = bass.Bass("TRN2", target_bir_lowering=False)
    d = {}
    d["x2d_in"] = nc.dram_tensor("x2d_in", [T, DM], BF16, kind="ExternalInput").ap()
    d["x2td_in"] = nc.dram_tensor("x2td_in", [DM, T], BF16, kind="ExternalInput").ap()
    d["l1_wqkv"] = nc.dram_tensor("l1_wqkv", [3 * DM, DM], F32, kind="ExternalInput").ap()
    d["l1_wout"] = nc.dram_tensor("l1_wout", [DM, DM], F32, kind="ExternalInput").ap()
    d["l1_bout"] = nc.dram_tensor("l1_bout", [DM], F32, kind="ExternalInput").ap()
    d["l1_ng"] = nc.dram_tensor("l1_ng", [1, 16, 1, 1], F32, kind="ExternalInput").ap()
    d["l1_nb"] = nc.dram_tensor("l1_nb", [1, 16, 1, 1], F32, kind="ExternalInput").ap()
    d["l1_rw"] = nc.dram_tensor("l1_rw", [DM, 8], F32, kind="ExternalInput").ap()
    for k, v in CONSTS.items():
        dt = BF16 if v.dtype == BF else (F32 if v.dtype == np.float32 else U32)
        d[k] = nc.dram_tensor(k, list(v.shape), dt, kind="ExternalInput").ap()
    d["x3d"] = nc.dram_tensor("x3d", [T, DM], BF16, kind="ExternalOutput").ap()
    d["logitsd"] = nc.dram_tensor("logitsd", [T, 8], F32, kind="ExternalOutput").ap()

    ctx = Ctx(nc)
    import contextlib as _cl
    with _cl.ExitStack() as stack:
        ctx.make_sems(stack)
        sb, ps = {}, {}
        alloc_common(nc, stack, sb)
        sb["c_idf"] = stack.enter_context(nc.sbuf_tensor("sb_c_idf", [128, 128], F32))
        sb["c_bc2f"] = stack.enter_context(nc.sbuf_tensor("sb_c_bc2f", [2, 128], F32))
        alloc_passC(nc, stack, sb)
        blk = stack.enter_context(nc.Block())
        load_common(ctx, d, sb)
        ctx.dma("sync", sb["c_idf"][:], d["c_id128f"])
        d["tok_bc2f"] = ctx.dma("sync", sb["c_bc2f"][:], d["c_bc2f"])
        emit_passC(ctx, d, sb, stack)
        fin = ctx.barrier_tok()
        ctx.add("sync", lambda E: None, waits=fin)
        for eng in Ctx.ENGS:
            opsl = ctx.q[eng]
            getattr(blk, eng)(lambda E, opsl=opsl: [f(E) for f in opsl])
    return nc



DEBUG_D = False
GATING_ONLY = False

def alloc_passD(nc, stack, sb):
    S = lambda name, shape, dt: sb.__setitem__(name, stack.enter_context(
        nc.sbuf_tensor("sb_" + name, shape, dt)))
    S("D_lg", [128, 32 * 8], F32)
    S("D_t0", [128, 32 * 8], F32)
    S("D_t1", [128, 32 * 8], F32)
    S("D_raw", [128, 32 * 8], F32)
    S("D_m1", [128, 32 * 8], F32)
    S("D_m2", [128, 32 * 8], F32)
    S("D_m1b", [128, 32 * 8], BF16)
    S("D_mx", [128, 32], F32)
    S("D_g1", [128, 32], F32)
    S("D_g2", [128, 32], F32)
    S("D_i1", [128, 32], F32)
    S("D_i2", [128, 32], F32)
    S("D_k1", [128, 32], F32)
    S("D_p1", [128, 32], F32)
    S("D_p2", [128, 32], F32)
    S("D_cnt", [1, 256], F32)
    S("D_cntT", [128, 2], BF16)
    S("D_cntT1c", [128, 2], BF16)
    S("D_offc", [128, 2], F32)
    S("D_offr", [1, 256], F32)
    S("D_offb", [128, 32 * 8], F32)
    S("D_px", [128, 32 * 8], F32)
    S("D_sl1", [128, 32], U32)
    S("D_sl2", [128, 32], U32)
    S("D_x3", [128, 2 * 1024], BF16)
    S("D_xe", [128, 1024], BF16)
    S("D_xeT", [128, 8 * 1024], BF16)
    S("D_w0", [128, 2 * 8 * 512], BF16)
    S("D_w1", [128, 2 * 8 * 512], BF16)
    S("D_gel", [128, 2 * 512], BF16)
    S("D_hid", [128, 4 * 512], BF16)
    S("D_wo", [128, 43 * 1024], BF16)
    S("D_hc", [128, 2 * 43 * 128], BF16)
    S("D_yb", [128, 2 * 512], BF16)
    S("D_A", [128, 1024], BF16)
    S("D_B", [128, 1024], BF16)
    S("D_mo", [128, 1024], F32)
    S("D_rln", [128, 1024], F32)
    S("D_bnst", [128, 12], F32)
    S("D_mv", [128, 2], F32)
    S("D_lnt", [128, 2], F32)
    S("D_out", [128, 2 * 1024], F32)
    if DEBUG_D:
        S("D_dbg", [128, 8], F32)
        S("D_dbgr", [1, 512], F32)


def emit_passD(ctx, d, sb, stack):
    global GATING_ONLY
    nc = ctx.nc
    X3D, LOGD = d["x3d_in"], d["logitsd_in"]
    W0, W1, WO = d["l1_w0"], d["l1_w1"], d["l1_wo"]
    XD, YD, HIDD = d["Xd"], d["Yd"], d["Hidd"]
    OUT = d["out"]

    idb = sb["c_idb"]
    onescol = sb["c_onescol"]; onesrowf = sb["c_onesrowf"]
    u128 = sb["c_u128b"]; u8s = sb["c_u8sb"]; a8s = sb["c_a8sb"]
    iota8 = sb["c_iota8t"]; i8rev = sb["c_i8revt"]; idf = sb["c_idf"]
    lg = sb["D_lg"]; t0 = sb["D_t0"]; t1_ = sb["D_t1"]; raw = sb["D_raw"]
    m1 = sb["D_m1"]; m2 = sb["D_m2"]; m1b = sb["D_m1b"]
    mx = sb["D_mx"]; g1 = sb["D_g1"]; g2 = sb["D_g2"]
    i1 = sb["D_i1"]; i2 = sb["D_i2"]; k1 = sb["D_k1"]
    p1 = sb["D_p1"]; p2 = sb["D_p2"]
    cnt = sb["D_cnt"]; cntT = sb["D_cntT"]; offc = sb["D_offc"]; offr = sb["D_offr"]
    offb = sb["D_offb"]; px = sb["D_px"]
    sl1 = sb["D_sl1"]; sl2 = sb["D_sl2"]
    x3t = sb["D_x3"]; xe = sb["D_xe"]; xeT = sb["D_xeT"]
    w0b = sb["D_w0"]; w1b = sb["D_w1"]; gel = sb["D_gel"]; hid = sb["D_hid"]
    wor = sb["D_wo"]; hc = sb["D_hc"]; yb = sb["D_yb"]
    Ab = sb["D_A"]; Bb = sb["D_B"]; mo = sb["D_mo"]; rln = sb["D_rln"]
    bnst = sb["D_bnst"]; mv = sb["D_mv"]; lnt = sb["D_lnt"]; outb = sb["D_out"]

    X3R = X3D.rearrange("(t p) dd -> p t dd", p=128)
    V8 = lambda ap: ap.rearrange("p (t e) -> p t e", e=8)
    B32 = lambda col: col.unsqueeze(2).broadcast_to([128, 32, 8])

    t_lg = ctx.dma("sync", lg[:].rearrange("p (t e) -> p t e", e=8),
                   LOGD.rearrange("(t p) e -> p t e", p=128))
    BCREG = []
    ctx.add("gpsimd", lambda E: (BCREG.append(E.to_reg(8191)), None)[1], inc="ge")

    with (
        nc.psum_tensor("psG", [128, 256], F32) as psG,
        nc.psum_tensor("psGB", [128, 256], F32) as psGB,
        nc.psum_tensor("psCnt", [1, 256], F32) as psCnt,
        nc.psum_tensor("psOffs", [128, 2], F32) as psOffs,
        nc.psum_tensor("psTR", [128, 128], F32) as psTR,
    ):
        def soft(E):
            E.tensor_reduce(mx[:], V8(lg[:]), AX.X, ALU.max)
            E.drain()
            E.tensor_tensor(V8(t0[:]), V8(lg[:]), B32(mx[:]), ALU.subtract)
            return E.nop()
        t_s1 = ctx.add("vector", soft, waits=[t_lg], inc="ve")
        t_s2 = ctx.add("scalar", lambda E: E.activation(t1_[:], t0[:], AF.Exp),
                       waits=[t_s1], inc="se")
        def soft2(E):
            E.tensor_reduce(mx[:], V8(t1_[:]), AX.X, ALU.add)
            E.drain()
            E.reciprocal(mx[:], mx[:])
            E.drain()
            E.tensor_tensor(V8(raw[:]), V8(t1_[:]), B32(mx[:]), ALU.mult)
            E.drain()
            E.tensor_reduce(g1[:], V8(raw[:]), AX.X, ALU.max)
            E.drain()
            E.tensor_tensor(V8(m1[:]), V8(raw[:]), B32(g1[:]), ALU.is_equal)
            E.tensor_tensor(V8(t0[:]), V8(m1[:]), V8(i8rev[:]), ALU.mult)
            E.drain()
            E.tensor_reduce(i1[:], V8(t0[:]), AX.X, ALU.max)
            E.drain()
            E.tensor_scalar(i1[:], i1[:], -1.0, 7.0, ALU.mult, ALU.add)
            E.drain()
            E.tensor_tensor(V8(m1[:]), V8(iota8[:]), B32(i1[:]), ALU.is_equal)
            E.tensor_scalar(V8(t0[:]), V8(m1[:]), -1.0, 1.0, ALU.mult, ALU.add)
            E.tensor_tensor(V8(t1_[:]), V8(raw[:]), V8(t0[:]), ALU.mult)
            E.drain()
            E.tensor_reduce(g2[:], V8(t1_[:]), AX.X, ALU.max)
            E.drain()
            E.tensor_tensor(V8(m2[:]), V8(t1_[:]), B32(g2[:]), ALU.is_equal)
            E.tensor_tensor(V8(t0[:]), V8(m2[:]), V8(i8rev[:]), ALU.mult)
            E.drain()
            E.tensor_reduce(i2[:], V8(t0[:]), AX.X, ALU.max)
            E.drain()
            E.tensor_scalar(i2[:], i2[:], -1.0, 7.0, ALU.mult, ALU.add)
            E.drain()
            E.tensor_tensor(V8(m2[:]), V8(iota8[:]), B32(i2[:]), ALU.is_equal)
            E.drain()
            return E.tensor_copy(m1b[:], m1[:])
        t_s3 = ctx.add("vector", soft2, waits=[t_s2, d["tok_gconsts"]], inc="ve", fence=True)

        def cums(which, msrc, mbsrc, extra_c1=None, extra_tok=None, after=()):
            t_a = ctx.add("tensor", lambda E: E.matmul(
                psG[:], u128[:], mbsrc[:], start=True, stop=True),
                waits=[t_s3] + list(after), inc="te")
            t_b = ctx.add("tensor", lambda E: E.matmul(
                psCnt[:], onescol[:], mbsrc[:], start=True, stop=True),
                waits=[t_s3] + list(after), inc="te")
            t_c = ctx.add("vector", lambda E: E.tensor_copy(cnt[:], psCnt[:]),
                          waits=[t_b], inc="ve", fence=True)
            toks = []
            for j in range(2):
                t_t = ctx.add("tensor", lambda E, j=j: E.transpose(
                    psTR[:, 0:1], cnt[0:1, j * 128:(j + 1) * 128], idf[0:1, 0:1]),
                    waits=[t_c] + toks, inc="te")
                t_e = ctx.add("vector", lambda E, j=j: E.tensor_copy(
                    cntT[:, j:j + 1], psTR[:, 0:1]), waits=[t_t], inc="ve", fence=True)
                toks = [t_e]
            t_cT = toks[0]
            def offmm(E):
                E.matmul(psOffs[:], u8s[:], cntT[:], start=True, stop=True)
                E.matmul(psTR[:, 0:1], a8s[:], cntT[:, 0:1], start=True, stop=True)
                if extra_c1 is not None:
                    E.matmul(psTR[:, 1:2], a8s[:], extra_c1[:, 0:1], start=True, stop=True)
                    return E.matmul(psTR[:, 2:3], a8s[:], extra_c1[:, 1:2], start=True, stop=True)
                return None
            w = [t_cT]
            if extra_tok is not None:
                w.append(extra_tok)
            t_o = ctx.add("tensor", offmm, waits=w, inc="te")
            if extra_c1 is None and "D_dbg_cnt" in d and GATING_ONLY:
                ctx.add("vector", lambda E: E.tensor_copy(sb["D_dbg"][:, 4:6], psOffs[:]),
                        waits=[t_o], inc="ve")
                ctx.add("vector", lambda E: E.tensor_copy(sb["D_dbg"][:, 6:8], cntT[:]),
                        waits=[t_o], inc="ve")
            def offcomb(E):
                E.tensor_copy(offc[:], psOffs[:])
                E.drain()
                E.tensor_tensor(offc[:, 1:2], offc[:, 1:2], psTR[:, 0:1], ALU.add)
                E.drain()
                if extra_c1 is not None:
                    E.tensor_tensor(offc[:, 0:1], offc[:, 0:1], psTR[:, 1:2], ALU.add)
                    E.tensor_tensor(offc[:, 1:2], offc[:, 1:2], psTR[:, 1:2], ALU.add)
                    E.drain()
                    E.tensor_tensor(offc[:, 0:1], offc[:, 0:1], psTR[:, 2:3], ALU.add)
                    E.tensor_tensor(offc[:, 1:2], offc[:, 1:2], psTR[:, 2:3], ALU.add)
                    E.drain()
                return E.nop()
            t_oe = ctx.add("vector", offcomb, waits=[t_o], inc="ve", fence=True)
            if extra_c1 is None and "D_dbg_cnt" in d:
                ctx.add("vector", lambda E: E.tensor_copy(sb["D_dbg"][:, 0:2], cntT[:]),
                        waits=[t_cT], inc="ve")
                ctx.add("vector", lambda E: E.tensor_copy(sb["D_dbgr"][0:1, 0:256], cnt[:]),
                        waits=[t_cT], inc="ve")
                ctx.add("vector", lambda E: E.tensor_copy(sb["D_dbg"][:, 2:4], offc[:]),
                        waits=[t_oe], inc="ve")
            toks = []
            for j in range(2):
                t_t = ctx.add("tensor", lambda E, j=j: E.transpose(
                    psTR[0:1, :], offc[:, j:j + 1], idf[:]), waits=[t_oe] + toks, inc="te")
                t_e = ctx.add("vector", lambda E, j=j: E.tensor_copy(
                    offr[0:1, j * 128:(j + 1) * 128], psTR[0:1, :]), waits=[t_t], inc="ve", fence=True)
                toks = [t_e]
            if extra_c1 is None and "D_dbg_cnt" in d:
                ctx.add("vector", lambda E: E.tensor_copy(sb["D_dbgr"][0:1, 256:512], offr[:]),
                        waits=toks, inc="ve")
            t_bb = ctx.add("tensor", lambda E: E.matmul(
                psGB[:], onesrowf[0:1, :], offr[:], start=True, stop=True),
                waits=toks, inc="te")
            t_be = ctx.add("vector", lambda E: E.tensor_copy(offb[:, 0:256], psGB[:]),
                           waits=[t_bb], inc="ve", fence=True)
            t_px = ctx.add("vector", lambda E: E.tensor_tensor(
                px[:, 0:256], psG[:], offb[:, 0:256], ALU.add), waits=[t_a, t_be], inc="ve")
            return t_px

        t_px1 = cums("m1", m1, m1b)
        def cap1(E):
            E.tensor_scalar(t0[:, 0:256], px[:, 0:256], 1024.0, None, ALU.is_lt)
            E.drain()
            E.tensor_tensor(m1[:, 0:256], m1[:, 0:256], t0[:, 0:256], ALU.mult)
            E.drain()
            E.tensor_tensor(t0[:, 0:256], px[:, 0:256], m1[:, 0:256], ALU.mult)
            E.drain()
            E.tensor_reduce(p1[:], V8(t0[:]), AX.X, ALU.add)
            E.tensor_reduce(k1[:], V8(m1[:]), AX.X, ALU.add)
            E.drain()
            E.tensor_tensor(g1[:], g1[:], k1[:], ALU.mult)
            E.drain()
            return E.tensor_copy(m1b[:], m1[:])
        t_c1 = ctx.add("vector", cap1, waits=[t_px1], inc="ve", fence=True)
        t_cb = ctx.add("tensor", lambda E: E.matmul(
            psCnt[:], onescol[:], m1b[:], start=True, stop=True), waits=[t_c1], inc="te")
        t_cc = ctx.add("vector", lambda E: E.tensor_copy(cnt[:], psCnt[:]),
                       waits=[t_cb], inc="ve", fence=True)
        toks = []
        for j in range(2):
            t_t = ctx.add("tensor", lambda E, j=j: E.transpose(
                psTR[:, 0:1], cnt[0:1, j * 128:(j + 1) * 128], idf[0:1, 0:1]),
                waits=[t_cc] + toks, inc="te")
            t_e = ctx.add("vector", lambda E, j=j: E.tensor_copy(
                sb["D_cntT1c"][:, j:j + 1], psTR[:, 0:1]), waits=[t_t], inc="ve", fence=True)
            toks = [t_e]
        extra_c1_tok = toks
        def slot1f(E):
            E.tensor_scalar(t0[:, 0:32], i1[:], 1024.0, None, ALU.mult)
            E.drain()
            E.tensor_tensor(t0[:, 0:32], t0[:, 0:32], p1[:], ALU.add)
            E.tensor_scalar(t0[:, 32:64], k1[:], -1048576.0, 1048576.0, ALU.mult, ALU.add)
            E.drain()
            E.tensor_tensor(t0[:, 0:32], t0[:, 0:32], t0[:, 32:64], ALU.add)
            E.drain()
            return E.tensor_copy(sl1[:], t0[:, 0:32])
        t_sl1 = ctx.add("vector", slot1f, waits=[t_c1], inc="ve", fence=True)

        t_m2b = ctx.add("vector", lambda E: E.tensor_copy(m1b[:], m2[:]),
                        waits=[t_cb, t_sl1], inc="ve", fence=True)
        t_px2 = cums("m2", m2, m1b, extra_c1=sb["D_cntT1c"],
                     extra_tok=extra_c1_tok[0], after=[t_sl1, extra_c1_tok[0], t_m2b])
        def cap2(E):
            E.tensor_scalar(t0[:, 0:256], px[:, 0:256], 1024.0, None, ALU.is_lt)
            E.drain()
            E.tensor_tensor(m2[:, 0:256], m2[:, 0:256], t0[:, 0:256], ALU.mult)
            E.drain()
            E.tensor_tensor(t0[:, 0:256], px[:, 0:256], m2[:, 0:256], ALU.mult)
            E.drain()
            E.tensor_reduce(p2[:], V8(t0[:]), AX.X, ALU.add)
            E.tensor_reduce(k1[:], V8(m2[:]), AX.X, ALU.add)
            E.drain()
            E.tensor_tensor(g2[:], g2[:], k1[:], ALU.mult)
            E.drain()
            E.tensor_scalar(t0[:, 0:32], i2[:], 1024.0, None, ALU.mult)
            E.drain()
            E.tensor_tensor(t0[:, 0:32], t0[:, 0:32], p2[:], ALU.add)
            E.tensor_scalar(t0[:, 32:64], k1[:], -1048576.0, 1048576.0, ALU.mult, ALU.add)
            E.drain()
            E.tensor_tensor(t0[:, 0:32], t0[:, 0:32], t0[:, 32:64], ALU.add)
            E.drain()
            return E.tensor_copy(sl2[:], t0[:, 0:32])
        t_sl2 = ctx.add("vector", cap2, waits=[t_px2], inc="ve", fence=True)
        if GATING_ONLY:
            ctx.dma("sync", d["D_dbg_cnt"], sb["D_dbg"][:], waits=[t_sl2])
            ctx.dma("sync", d["D_dbg_rows"], sb["D_dbgr"][:], waits=[t_sl2])
            ctx.dma("sync", d["D_dbg_px"], px[:], waits=[t_px1])
            ctx.dma("sync", d["D_dbg_sl1"], sl1[:], waits=[t_sl1])
            ctx.dma("sync", d["D_dbg_sl2"], sl2[:], waits=[t_sl2])
            ctx.dma("sync", d["D_dbg_g1"], g1[:], waits=[t_sl2])
            ctx.dma("sync", d["D_dbg_g2"], g2[:], waits=[t_sl2])
            ctx.dma("sync", d["D_dbg_i1"], i1[:], waits=[t_sl2])
            ctx.dma("sync", d["D_dbg_p1"], p1[:], waits=[t_sl2])
            return ctx.barrier_tok()
        if "D_dbg_sl1" in d:
            ctx.dma("sync", d["D_dbg_cnt"], sb["D_dbg"][:], waits=[t_sl2])
            ctx.dma("sync", d["D_dbg_rows"], sb["D_dbgr"][:], waits=[t_sl2])
            ctx.dma("sync", d["D_dbg_px"], px[:], waits=[t_px1])
            ctx.dma("sync", d["D_dbg_sl1"], sl1[:], waits=[t_sl1])
            ctx.dma("sync", d["D_dbg_sl2"], sl2[:], waits=[t_sl2])
            ctx.dma("sync", d["D_dbg_g1"], g1[:], waits=[t_sl2])
            ctx.dma("sync", d["D_dbg_g2"], g2[:], waits=[t_sl2])
            ctx.dma("sync", d["D_dbg_i1"], i1[:], waits=[t_sl2])
            ctx.dma("sync", d["D_dbg_p1"], p1[:], waits=[t_sl2])

        t_x3_use = [None, None]
        t_scat = None
        for t in range(NT):
            sl = t % 2
            w = [t_x3_use[sl]] if t_x3_use[sl] else []
            t_ld = ctx.dma("sync", x3t[:, sl * 1024:(sl + 1) * 1024], X3R[:, t, :], waits=w)
            ctx.add("gpsimd", lambda E, t=t, sl=sl: E.indirect_dma_start(
                XD, IndirectOffsetOnAxis(sl1[:, t:t + 1], 0),
                x3t[:, sl * 1024:(sl + 1) * 1024], None,
                bounds_check=BCREG[0], oob_is_err=False),
                waits=[t_ld, t_sl1, t_sl2, d["tok_xzero"]], inc="dge", n=16)
            t_scat = ctx.add("gpsimd", lambda E, t=t, sl=sl: E.indirect_dma_start(
                XD, IndirectOffsetOnAxis(sl2[:, t:t + 1], 0),
                x3t[:, sl * 1024:(sl + 1) * 1024], None,
                bounds_check=BCREG[0], oob_is_err=False),
                waits=[], inc="dge", n=16)
            t_x3_use[sl] = t_scat
    t_gate_end = ctx.barrier_tok()

    NHC = 11
    hcw = [512] * 10 + [344]
    hboff = [0, 4, 8, 12, 16, 20, 24, 28, 32, 36, 40]
    hbn = [4] * 10 + [3]
    hbw_last = {10: [128, 128, 88]}
    with (
        nc.psum_tensor("psZ0a", [128, 512], F32) as psZ0a,
        nc.psum_tensor("psZ1a", [128, 512], F32) as psZ1a,
        nc.psum_tensor("psZ0b", [128, 512], F32) as psZ0b,
        nc.psum_tensor("psZ1b", [128, 512], F32) as psZ1b,
        nc.psum_tensor("psDa", [128, 512], F32) as psDa,
        nc.psum_tensor("psDb", [128, 512], F32) as psDb,
        nc.psum_tensor("psTx", [128, 128], BF16) as psTx,
        nc.psum_tensor("psTy", [128, 128], BF16) as psTy,
    ):
        t_wo_use = None
        t_hc_use = [None, None]
        t_xeT_use = None
        t_hid_use = [None, None, None, None]
        t_psT_ev = {0: None, 1: None}
        t_z_ev = {0: None, 1: None}
        t_hid_dma = None
        t_gel_use = None
        t_d_ev = {0: None, 1: None}
        t_yb_use = [None, None]
        t_w01c = {}
        t_w01c_use = {}
        t_wo_tok = None
        t_comb_gather = None

        for e in range(E1):
            t_xe_tr = None
            for ti in range(8):
                w = []
                if e == 0 and ti == 0:
                    w = list(t_gate_end)
                if t_xeT_use and ti == 0:
                    w.append(t_xeT_use)
                if t_xe_tr:
                    w.append(t_xe_tr)
                t_ldx = ctx.dma("sync", xe[:],
                                XD[e * 1024 + ti * 128: e * 1024 + (ti + 1) * 128, :],
                                waits=w + ([t_scat] if t_scat else []))
                for c in range(NC):
                    tp = c % 2
                    pst = psTx if tp == 0 else psTy
                    w2 = [t_ldx]
                    if t_psT_ev[tp]:
                        w2.append(t_psT_ev[tp])
                    t_tr = ctx.add("tensor", lambda E, c=c, pst=pst: E.transpose(
                        pst[:], xe[:, c * 128:(c + 1) * 128], idb[:]), waits=w2, inc="te")
                    t_ev = ctx.add("vector", lambda E, c=c, ti=ti, pst=pst: E.tensor_copy(
                        xeT[:, c * 1024 + ti * 128: c * 1024 + (ti + 1) * 128], pst[:]),
                        waits=[t_tr], inc="ve")
                    t_psT_ev[tp] = t_ev
                    t_xe_tr = t_ev
            w = [t_wo_use] if t_wo_use else []
            ctx.dma("gpsimd",
                    wor[:].rearrange("p (hb m) -> p hb m", m=1024)[:, 0:42, :],
                    WO[e][0:5376, :].rearrange("(hb p) m -> p hb m", p=128),
                    waits=w)
            t_wo_tok = ctx.dma("gpsimd", wor[0:88, 42 * 1024:43 * 1024],
                               WO[e][5376:5464, :], waits=[])
            for hci in range(NHC):
                sl = hci % 2
                wv = [t_w01c_use.get((sl,))] if t_w01c_use.get((sl,)) else []
                t_l0 = ctx.dma("gpsimd",
                               w0b[:].rearrange("p (s c h) -> p s c h", s=2, h=512)[:, sl, :, 0:hcw[hci]],
                               W0[e].rearrange("(c p) h -> p c h", p=128)[:, :, hci * 512: hci * 512 + hcw[hci]],
                               waits=wv)
                t_l1 = ctx.dma("gpsimd",
                               w1b[:].rearrange("p (s c h) -> p s c h", s=2, h=512)[:, sl, :, 0:hcw[hci]],
                               W1[e].rearrange("(c p) h -> p c h", p=128)[:, :, hci * 512: hci * 512 + hcw[hci]],
                               waits=wv)
                for hb in range(hbn[hci]):
                    hbw = 128 if (hci < 10 or hb < 2) else 88
                    for cc in range(2):
                        zp = (hb * 2 + cc) % 2
                        pz0 = psZ0a if zp == 0 else psZ0b
                        pz1 = psZ1a if zp == 0 else psZ1b
                        w3 = [t_l0, t_l1, t_xe_tr]
                        if t_z_ev[zp]:
                            w3.append(t_z_ev[zp])
                        def zmm(E, sl=sl, hb=hb, cc=cc, hbw=hbw, pz0=pz0, pz1=pz1):
                            for c in range(NC):
                                E.matmul(pz0[0:hbw, 0:512],
                                         w0b[:, (sl * 8 + c) * 512 + hb * 128:
                                             (sl * 8 + c) * 512 + hb * 128 + hbw],
                                         xeT[:, c * 1024 + cc * 512: c * 1024 + (cc + 1) * 512],
                                         start=(c == 0), stop=(c == NC - 1))
                            last = None
                            for c in range(NC):
                                last = E.matmul(pz1[0:hbw, 0:512],
                                                w1b[:, (sl * 8 + c) * 512 + hb * 128:
                                                    (sl * 8 + c) * 512 + hb * 128 + hbw],
                                                xeT[:, c * 1024 + cc * 512: c * 1024 + (cc + 1) * 512],
                                                start=(c == 0), stop=(c == NC - 1))
                            return last
                        t_z = ctx.add("tensor", zmm, waits=w3, inc="te")
                        gsl = (hb * 2 + cc) % 2
                        w4 = [t_z]
                        if t_gel_use[gsl] if isinstance(t_gel_use, list) else t_gel_use:
                            w4.append(t_gel_use[gsl] if isinstance(t_gel_use, list) else t_gel_use)
                        t_g = ctx.add("scalar", lambda E, pz0=pz0, hbw=hbw, gsl=gsl: E.activation(
                            gel[0:hbw, gsl * 512:(gsl + 1) * 512], pz0[0:hbw, 0:512], AF.Gelu_apprx_tanh),
                            waits=w4, inc="se")
                        hsl = (hb * 2 + cc) % 4
                        w5 = [t_g]
                        if t_hid_use[hsl]:
                            w5.append(t_hid_use[hsl])
                        t_h = ctx.add("vector", lambda E, pz1=pz1, hbw=hbw, hsl=hsl, gsl=gsl: E.tensor_tensor(
                            hid[0:hbw, hsl * 512: hsl * 512 + 512], gel[0:hbw, gsl * 512:(gsl + 1) * 512],
                            pz1[0:hbw, 0:512], ALU.mult), waits=w5, inc="ve")
                        if not isinstance(t_gel_use, list):
                            t_gel_use = [None, None]
                        t_gel_use[gsl] = t_h
                        t_z_ev[zp] = t_h
                        hbg = hboff[hci] + hb
                        w6 = [t_h]
                        if t_hid_dma:
                            w6.append(t_hid_dma)
                        t_hid_dma = ctx.dma("sync",
                                            HIDD[hbg * 128: hbg * 128 + hbw, cc * 512:(cc + 1) * 512],
                                            hid[0:hbw, hsl * 512: hsl * 512 + 512],
                                            waits=w6)
                        t_hid_use[hsl] = t_hid_dma
                t_w01c_use[(sl,)] = t_z
            for cb in range(8):
                csl = cb % 2
                w = [t_hc_use[csl]] if t_hc_use[csl] else []
                t_lhc = ctx.dma("sync",
                                hc[:].rearrange("p (s hb cc) -> p s hb cc", s=2, cc=128)[:, csl],
                                HIDD.rearrange("(hb p) c -> p hb c", p=128)[:, :, cb * 128:(cb + 1) * 128],
                                waits=w + [t_hid_dma])
                for mc in range(2):
                    dp = (cb * 2 + mc) % 2
                    pd = psDa if dp == 0 else psDb
                    w2 = [t_lhc, t_wo_tok]
                    if t_d_ev[dp]:
                        w2.append(t_d_ev[dp])
                    def dmm(E, csl=csl, mc=mc, pd=pd):
                        last = None
                        for hb in range(43):
                            hbw = 128 if hb < 42 else 88
                            last = E.matmul(pd[:],
                                            hc[0:hbw, (csl * 43 + hb) * 128:(csl * 43 + hb) * 128 + 128],
                                            wor[0:hbw, hb * 1024 + mc * 512: hb * 1024 + (mc + 1) * 512],
                                            start=(hb == 0), stop=(hb == 42))
                        return last
                    t_d = ctx.add("tensor", dmm, waits=w2, inc="te")
                    t_hc_use[csl] = t_d
                    ysl = (cb * 2 + mc) % 2
                    w3 = [t_d]
                    if t_yb_use[ysl]:
                        w3.append(t_yb_use[ysl])
                    t_ye = ctx.add("vector", lambda E, ysl=ysl, pd=pd: E.tensor_copy(
                        yb[:, ysl * 512:(ysl + 1) * 512], pd[:]), waits=w3, inc="ve")
                    t_d_ev[dp] = t_ye
                    t_yd = ctx.dma("sync",
                                   YD[e * 1024 + cb * 128: e * 1024 + (cb + 1) * 128,
                                      mc * 512:(mc + 1) * 512],
                                   yb[:, ysl * 512:(ysl + 1) * 512], waits=[t_ye])
                    t_yb_use[ysl] = t_yd
            t_xeT_use = t_z
            t_wo_use = t_d
        t_experts_done = ctx.barrier_tok()

    with nc.psum_tensor("psCb", [128, 128], F32) as _unused:
        t_AB_use = None
        prev_ln = None
        t_out_dma = [None, None]
        t_rln_prev = None
        t_x3_use2 = [None, None]
        t_memA = ctx.add("gpsimd", lambda E: E.memset(Ab[:], 0.0), inc="ge")
        t_memB = ctx.add("gpsimd", lambda E: E.memset(Bb[:], 0.0), inc="ge")
        for t in range(NT):
            sl = t % 2
            w = [t_x3_use2[sl]] if t_x3_use2[sl] else []
            t_ldr = ctx.dma("sync", x3t[:, sl * 1024:(sl + 1) * 1024], X3R[:, t, :], waits=w)
            w2 = list(t_experts_done) if t == 0 else []
            w2 += [t_memA, t_memB]
            if t_AB_use:
                w2.append(t_AB_use)
            ctx.add("gpsimd", lambda E, t=t: E.indirect_dma_start(
                Ab[:], None, YD, IndirectOffsetOnAxis(sl1[:, t:t + 1], 0),
                bounds_check=BCREG[0], oob_is_err=False), waits=w2, inc="dge", n=16)
            t_ga = ctx.add("gpsimd", lambda E, t=t: E.indirect_dma_start(
                Bb[:], None, YD, IndirectOffsetOnAxis(sl2[:, t:t + 1], 0),
                bounds_check=BCREG[0], oob_is_err=False), waits=[], inc="dge", n=16)
            def comb(E, t=t, sl=sl):
                E.tensor_scalar(mo[:], Ab[:], g1[:, t:t + 1], None, ALU.mult)
                E.drain()
                E.scalar_tensor_tensor(mo[:], Bb[:], g2[:, t:t + 1], mo[:], ALU.mult, ALU.add)
                E.drain()
                return E.tensor_tensor(rln[:], mo[:], x3t[:, sl * 1024:(sl + 1) * 1024], ALU.add)
            w3 = [t_ga, t_ldr]
            if t_rln_prev:
                w3.append(t_rln_prev)
            t_res = ctx.add("vector", comb, waits=w3, inc="ve")
            t_AB_use = t_res
            t_x3_use2[sl] = t_res
            def lnf(E):
                E.bn_stats(bnst[:, 0:6], rln[:, 0:512])
                E.bn_stats(bnst[:, 6:12], rln[:, 512:1024])
                E.drain()
                E.bn_aggr(mv[:], bnst[:].rearrange("p (g k) -> p g k", k=6))
                E.drain()
                return E.nop()
            wbn = [t_res]
            if prev_ln:
                wbn.append(prev_ln)
            t_bn = ctx.add("vector", lnf, waits=wbn, inc="ve")
            def rstdf(E):
                E.activation(lnt[:, 0:1], mv[:, 1:2], AF.Ln, bias=1e-5)
                return E.activation(lnt[:, 1:2], lnt[:, 0:1], AF.Exp, scale=-0.5)
            t_rs = ctx.add("scalar", rstdf, waits=[t_bn], inc="se")
            osl = t % 2
            w4 = [t_rs]
            if t_out_dma[osl]:
                w4.append(t_out_dma[osl])
            t_ap = ctx.add("vector", lambda E, osl=osl: E.tensor_scalar(
                outb[:, osl * 1024:(osl + 1) * 1024], rln[:], mv[:, 0:1], lnt[:, 1:2],
                ALU.subtract, ALU.mult), waits=w4, inc="ve")
            prev_ln = t_ap
            t_rln_prev = t_ap
            t_out_dma[osl] = ctx.dma("sync", OUT.rearrange("(t p) dd -> p t dd", p=128)[:, t, :],
                                     outb[:, osl * 1024:(osl + 1) * 1024], waits=[t_ap])
    return ctx.barrier_tok()


def build_passD_program():
    import contextlib as _cl
    nc = bass.Bass("TRN2", target_bir_lowering=False)
    d = {}
    d["x3d_in"] = nc.dram_tensor("x3d_in", [T, DM], BF16, kind="ExternalInput").ap()
    d["logitsd_in"] = nc.dram_tensor("logitsd_in", [T, 8], F32, kind="ExternalInput").ap()
    d["l1_w0"] = nc.dram_tensor("l1_w0", [E1, DM, HFF], F32, kind="ExternalInput").ap()
    d["l1_w1"] = nc.dram_tensor("l1_w1", [E1, DM, HFF], F32, kind="ExternalInput").ap()
    d["l1_wo"] = nc.dram_tensor("l1_wo", [E1, HFF, DM], F32, kind="ExternalInput").ap()
    for k, v in CONSTS.items():
        dt = BF16 if v.dtype == BF else (F32 if v.dtype == np.float32 else U32)
        d[k] = nc.dram_tensor(k, list(v.shape), dt, kind="ExternalInput").ap()
    d["Xd"] = nc.dram_tensor("Xd", [E1 * CAP, DM], BF16,
                             kind="ExternalOutput" if DEBUG_D else "Internal").ap()
    d["Yd"] = nc.dram_tensor("Yd", [E1 * CAP, DM], BF16,
                             kind="ExternalOutput" if DEBUG_D else "Internal").ap()
    d["Hidd"] = nc.dram_tensor("Hidd", [5504, DM], BF16, kind="Internal").ap()
    d["out"] = nc.dram_tensor("out", [T, DM], F32, kind="ExternalOutput").ap()
    if DEBUG_D:
        for nm, dt in (("D_dbg_sl1", U32), ("D_dbg_sl2", U32), ("D_dbg_g1", F32),
                       ("D_dbg_g2", F32), ("D_dbg_i1", F32), ("D_dbg_p1", F32)):
            d[nm] = nc.dram_tensor(nm, [128, 32], dt, kind="ExternalOutput").ap()
        d["D_dbg_cnt"] = nc.dram_tensor("D_dbg_cnt", [128, 8], F32, kind="ExternalOutput").ap()
        d["D_dbg_rows"] = nc.dram_tensor("D_dbg_rows", [1, 512], F32, kind="ExternalOutput").ap()
        d["D_dbg_px"] = nc.dram_tensor("D_dbg_px", [128, 256], F32, kind="ExternalOutput").ap()
        d["D_dbg_u8s"] = nc.dram_tensor("D_dbg_u8s", [128, 128], F32, kind="ExternalOutput").ap()

    ctx = Ctx(nc)
    with _cl.ExitStack() as stack:
        ctx.make_sems(stack)
        sb, ps = {}, {}
        alloc_common(nc, stack, sb)
        sb["c_idf"] = stack.enter_context(nc.sbuf_tensor("sb_c_idf", [128, 128], F32))
        for nm, shape, dt in (("c_u128b", [128, 128], BF16), ("c_u8sb", [128, 128], BF16),
                              ("c_a8sb", [128, 128], BF16), ("c_iota8t", [128, 256], F32),
                              ("c_i8revt", [128, 256], F32)):
            sb[nm] = stack.enter_context(nc.sbuf_tensor("sb_" + nm, shape, dt))
        alloc_passD(nc, stack, sb)
        blk = stack.enter_context(nc.Block())
        load_common(ctx, d, sb)
        ctx.dma("sync", sb["c_idf"][:], d["c_id128f"])
        toks = []
        for nm in ("c_u128b", "c_u8sb", "c_a8sb", "c_iota8t", "c_i8revt"):
            toks.append(ctx.dma("sync", sb[nm][:], d[nm]))
        d["tok_gconsts"] = toks[-1]
        t_z0 = ctx.add("gpsimd", lambda E: E.memset(sb["D_xe"][:], 0.0), inc="ge")
        t_xz = None
        for r in range(64):
            t_xz = ctx.dma("sync", d["Xd"][r * 128:(r + 1) * 128, :], sb["D_xe"][:],
                           waits=[t_z0])
        d["tok_xzero"] = t_xz
        emit_passD(ctx, d, sb, stack)
        fin = ctx.barrier_tok()
        ctx.add("sync", lambda E: None, waits=fin)
        for eng in Ctx.ENGS:
            opsl = ctx.q[eng]
            getattr(blk, eng)(lambda E, opsl=opsl: [f(E) for f in opsl])
    return nc



def build_fused_program():
    import contextlib as _cl
    nc = bass.Bass("TRN2", target_bir_lowering=False)
    d = {}
    d["x"] = nc.dram_tensor("x", [T, DM], F32, kind="ExternalInput").ap()
    d["l0_wqkv"] = nc.dram_tensor("l0_wqkv", [DM, 3 * DM], F32, kind="ExternalInput").ap()
    d["l0_wout"] = nc.dram_tensor("l0_wout", [DM, DM], F32, kind="ExternalInput").ap()
    d["l0_wg"] = nc.dram_tensor("l0_wg", [DM, DM], F32, kind="ExternalInput").ap()
    d["l0_bg"] = nc.dram_tensor("l0_bg", [DM], F32, kind="ExternalInput").ap()
    d["l0_we"] = nc.dram_tensor("l0_we", [E0, DM, DM], F32, kind="ExternalInput").ap()
    d["l0_be"] = nc.dram_tensor("l0_be", [E0, DM], F32, kind="ExternalInput").ap()
    d["l1_wqkv"] = nc.dram_tensor("l1_wqkv", [3 * DM, DM], F32, kind="ExternalInput").ap()
    d["l1_wout"] = nc.dram_tensor("l1_wout", [DM, DM], F32, kind="ExternalInput").ap()
    d["l1_bout"] = nc.dram_tensor("l1_bout", [DM], F32, kind="ExternalInput").ap()
    d["l1_ng"] = nc.dram_tensor("l1_ng", [1, 16, 1, 1], F32, kind="ExternalInput").ap()
    d["l1_nb"] = nc.dram_tensor("l1_nb", [1, 16, 1, 1], F32, kind="ExternalInput").ap()
    d["l1_rw"] = nc.dram_tensor("l1_rw", [DM, 8], F32, kind="ExternalInput").ap()
    d["l1_w0"] = nc.dram_tensor("l1_w0", [E1, DM, HFF], F32, kind="ExternalInput").ap()
    d["l1_w1"] = nc.dram_tensor("l1_w1", [E1, DM, HFF], F32, kind="ExternalInput").ap()
    d["l1_wo"] = nc.dram_tensor("l1_wo", [E1, HFF, DM], F32, kind="ExternalInput").ap()
    for k, v in CONSTS.items():
        dt = BF16 if v.dtype == BF else (F32 if v.dtype == np.float32 else U32)
        d[k] = nc.dram_tensor(k, list(v.shape), dt, kind="ExternalInput").ap()
    d["x1d"] = nc.dram_tensor("x1d", [T, DM], BF16, kind="Internal").ap()
    d["x1td"] = nc.dram_tensor("x1td", [DM, T], BF16, kind="Internal").ap()
    d["x2d"] = nc.dram_tensor("x2d", [T, DM], BF16, kind="Internal").ap()
    d["x2td"] = nc.dram_tensor("x2td", [DM, T], BF16, kind="Internal").ap()
    d["x3d"] = nc.dram_tensor("x3d", [T, DM], BF16, kind="Internal").ap()
    d["logitsd"] = nc.dram_tensor("logitsd", [T, 8], F32, kind="Internal").ap()
    d["Xd"] = nc.dram_tensor("Xd", [E1 * CAP, DM], BF16, kind="Internal").ap()
    d["Yd"] = nc.dram_tensor("Yd", [E1 * CAP, DM], BF16, kind="Internal").ap()
    d["Hidd"] = nc.dram_tensor("Hidd", [5504, DM], BF16, kind="Internal").ap()
    d["out"] = nc.dram_tensor("out", [T, DM], F32, kind="ExternalOutput").ap()
    d["x1d_in"] = d["x1d"]; d["x1td_in"] = d["x1td"]
    d["x2d_in"] = d["x2d"]; d["x2td_in"] = d["x2td"]
    d["x3d_in"] = d["x3d"]; d["logitsd_in"] = d["logitsd"]

    ctx = Ctx(nc)
    with _cl.ExitStack() as stack:
        ctx.make_sems(stack)
        sb = {}
        alloc_common(nc, stack, sb)
        sb["c_idf"] = stack.enter_context(nc.sbuf_tensor("sb_c_idf", [128, 128], F32))
        sb["c_bc2f"] = stack.enter_context(nc.sbuf_tensor("sb_c_bc2f", [2, 128], F32))
        for nm, shape, dt in (("c_u128b", [128, 128], BF16), ("c_u8sb", [128, 128], BF16),
                              ("c_a8sb", [128, 128], BF16), ("c_iota8t", [128, 256], F32),
                              ("c_i8revt", [128, 256], F32)):
            sb[nm] = stack.enter_context(nc.sbuf_tensor("sb_" + nm, shape, dt))
        blk = stack.enter_context(nc.Block())
        load_common(ctx, d, sb)
        ctx.dma("sync", sb["c_idf"][:], d["c_id128f"])
        d["tok_bc2f"] = ctx.dma("sync", sb["c_bc2f"][:], d["c_bc2f"])
        toks = []
        for nm in ("c_u128b", "c_u8sb", "c_a8sb", "c_iota8t", "c_i8revt"):
            toks.append(ctx.dma("sync", sb[nm][:], d[nm]))
        d["tok_gconsts"] = toks[-1]

        def fence():
            bar = ctx.barrier_tok()
            ftoks = []
            for eng, ch in (("tensor", "te"), ("vector", "ve"), ("scalar", "se"),
                            ("gpsimd", "ge"), ("sync", None)):
                if ch is None:
                    ftoks.append(ctx.add(eng, lambda E: None, waits=bar))
                else:
                    ftoks.append(ctx.add(eng, lambda E: E.nop(), waits=bar, inc=ch))
            return ctx.barrier_tok()

        with _cl.ExitStack() as stA:
            sbA, psA_ = dict(sb), {}
            alloc_passA(nc, stA, sbA, psA_)
            alloc_psum_A(nc, stA, psA_)
            emit_passA(ctx, d, sbA, psA_)
        fence()
        with _cl.ExitStack() as stB:
            sbB, psB_ = dict(sb), {}
            alloc_passB(nc, stB, sbB, psB_)
            alloc_psum_B(nc, stB, psB_)
            emit_passB(ctx, d, sbB, psB_)
        fence()
        with _cl.ExitStack() as stC:
            sbC = dict(sb)
            alloc_passC(nc, stC, sbC)
            emit_passC(ctx, d, sbC, stC)
        fence()
        with _cl.ExitStack() as stD:
            sbD = dict(sb)
            alloc_passD(nc, stD, sbD)
            t_z0 = ctx.add("gpsimd", lambda E: E.memset(sbD["D_xe"][:], 0.0), inc="ge")
            t_xz = None
            for r in range(64):
                t_xz = ctx.dma("sync", d["Xd"][r * 128:(r + 1) * 128, :], sbD["D_xe"][:],
                               waits=[t_z0])
            d["tok_xzero"] = t_xz
            emit_passD(ctx, d, sbD, stD)
        fin = ctx.barrier_tok()
        ctx.add("sync", lambda E: None, waits=fin)
        for eng in Ctx.ENGS:
            opsl = ctx.q[eng]
            getattr(blk, eng)(lambda E, opsl=opsl: [f(E) for f in opsl])
    return nc


def kernel(**inputs):
    consts = consts_inmap()
    f32 = lambda k: np.ascontiguousarray(np.asarray(inputs[k], np.float32))
    X = f32("L0_attn_x")
    ncF = _get_prog("F")
    maps = [dict(x=np.ascontiguousarray(X[b]),
                 l0_wqkv=f32("L0_attn_Wqkv"), l0_wout=f32("L0_attn_Wout"),
                 l0_wg=f32("L0_ffn_Wg"), l0_bg=f32("L0_ffn_bg"),
                 l0_we=f32("L0_ffn_We"), l0_be=f32("L0_ffn_be"),
                 l1_wqkv=f32("L1_attn_w_qkv"), l1_wout=f32("L1_attn_w_out"),
                 l1_bout=f32("L1_attn_b_out"), l1_ng=f32("L1_attn_norm_g"),
                 l1_nb=f32("L1_attn_norm_b"), l1_rw=f32("L1_ffn_router_w"),
                 l1_w0=f32("L1_ffn_w0_kernel"), l1_w1=f32("L1_ffn_w1_kernel"),
                 l1_wo=f32("L1_ffn_wo_kernel"), **consts) for b in range(8)]
    res = run_bass_kernel_spmd(ncF, maps, core_ids=list(range(8))).results
    return np.stack([np.asarray(res[b]["out"], np.float32) for b in range(8)], axis=0)
